# version 05
# baseline (speedup 1.0000x reference)
import numpy as np
import ml_dtypes
import concourse.bass as bass
import concourse.mybir as mybir
from concourse.bass_utils import run_bass_kernel_spmd

F32 = mybir.dt.float32
BF16 = mybir.dt.bfloat16
AF = mybir.ActivationFunctionType
ALU = mybir.AluOpType
AX = mybir.AxisListType
BF = ml_dtypes.bfloat16

D = 2048
T = 2048
NC_ = 16
NH = 16
DH = 128
LN2_4 = 1.3862943611198906
M_THR = np.float32(513.0 / 1024.0)


_SBUF_SEQ = [0]


def _uname(name):
    _SBUF_SEQ[0] += 1
    return f"{name}_u{_SBUF_SEQ[0]}"


class Prog:
    ENGINES = ("tensor", "vector", "scalar", "sync", "gpsimd")

    def __init__(self, nc):
        self.nc = nc
        self.ops = []
        self.last_writer = {}
        self.readers = {}
        self.base_dep = None

    def op(self, eng, emit, reads=(), writes=(), dma=False, bar=False):
        idx = len(self.ops)
        deps = set()
        for k in reads:
            w = self.last_writer.get(k)
            if w is not None:
                deps.add(w)
        for k in writes:
            w = self.last_writer.get(k)
            if w is not None:
                deps.add(w)
            deps |= self.readers.get(k, set())
        if self.base_dep is not None and not bar:
            deps.add(self.base_dep)
        deps.discard(idx)
        self.ops.append(dict(idx=idx, eng=eng, emit=emit, deps=deps, dma=dma,
                             bar=bar))
        for k in reads:
            self.readers.setdefault(k, set()).add(idx)
        for k in writes:
            self.last_writer[k] = idx
            self.readers[k] = set()
        return idx

    def barrier(self, reads=(), writes=()):
        return self.op("vector", None, reads=reads, writes=writes, bar=True)

    def full_barrier(self):
        keys = set(self.last_writer) | set(self.readers)
        idx = self.barrier(writes=sorted(keys))
        self.base_dep = idx
        return idx

    NDMA = 4

    def finalize(self, block, sems):
        ops = self.ops
        n = len(ops)
        for o in ops:
            if any(ops[d].get("bar") for d in o["deps"]):
                nd = set()
                stk = list(o["deps"])
                while stk:
                    d = stk.pop()
                    if ops[d].get("bar"):
                        stk.extend(ops[d]["deps"])
                    else:
                        nd.add(d)
                o["deps"] = nd

        def edge_needed(dop, o):
            if dop["dma"]:
                return True
            if dop["eng"] == o["eng"] == "tensor" and not o["dma"]:
                return False
            return True

        need_inc = [False] * n
        for o in ops:
            if o.get("bar"):
                continue
            if o["dma"]:
                need_inc[o["idx"]] = True
            for d in o["deps"]:
                dop = ops[d]
                if edge_needed(dop, o):
                    need_inc[d] = True
        last_of = {}
        for o in ops:
            if not o["dma"] and not o.get("bar"):
                last_of[o["eng"]] = o["idx"]
        for i in last_of.values():
            need_inc[i] = True
        semkey = [None] * n
        val = [0] * n
        cum = {}
        dma_count = {}
        for o in ops:
            i = o["idx"]
            if o["dma"]:
                q = o["eng"]
                c = dma_count.get(q, 0)
                dma_count[q] = c + 1
                sk = f"{q}_d{c % self.NDMA}"
                cum[sk] = cum.get(sk, 0) + 16
            else:
                sk = o["eng"]
                if need_inc[i]:
                    cum[sk] = cum.get(sk, 0) + 1
            semkey[i] = sk
            val[i] = cum.get(sk, 0)
        finals = dict(cum)

        def body_for(eng_name):
            def body(eng):
                waited = {}
                for o in ops:
                    if o["eng"] != eng_name or o.get("bar"):
                        continue
                    i = o["idx"]
                    want = {}
                    for d in o["deps"]:
                        dop = ops[d]
                        if not edge_needed(dop, o):
                            continue
                        sk, v = semkey[d], val[d]
                        if v > waited.get(sk, 0) and v > want.get(sk, 0):
                            want[sk] = v
                    if o["dma"]:
                        sk, v = semkey[i], val[i] - 16
                        if v > waited.get(sk, 0) and v > want.get(sk, 0):
                            want[sk] = v
                    for sk, v in want.items():
                        eng.wait_ge(sems[sk], v)
                        waited[sk] = v
                    ins = o["emit"](eng)
                    if need_inc[i]:
                        ins.then_inc(sems[semkey[i]], 16 if o["dma"] else 1)
                if eng_name == "sync":
                    for sk, v in finals.items():
                        if v:
                            eng.wait_ge(sems[sk], v)
            return body

        block.sync(body_for("sync"))
        block.tensor(body_for("tensor"))
        block.vector(body_for("vector"))
        block.scalar(body_for("scalar"))
        block.gpsimd(body_for("gpsimd"))
        return finals

    @staticmethod
    def make_sems(nc, stack):
        sems = {}
        for e in Prog.ENGINES:
            sems[e] = stack.enter_context(nc.semaphore(f"s_{e}"))
        for q in ("sync", "gpsimd", "scalar"):
            for i in range(Prog.NDMA):
                sems[f"{q}_d{i}"] = stack.enter_context(
                    nc.semaphore(f"s_{q}_d{i}"))
        return sems


def k_pb(i):
    return f"pb{i}"


def emit_row_bcast_hilo(p, E, row_key, row_ap, width, banks, out_sb, out_key,
                        tmp_rows):
    hi, hi_k, lo, lo_k = tmp_rows
    p.op("vector", lambda e: e.tensor_copy(hi[0:1, 0:width], row_ap),
         reads=[row_key], writes=[hi_k])
    p.op("vector", lambda e: e.tensor_tensor(lo[0:1, 0:width], row_ap,
                                             hi[0:1, 0:width], ALU.subtract),
         reads=[row_key, hi_k], writes=[lo_k])
    nsl = (width + 511) // 512
    for i in range(nsl):
        c0, c1 = i * 512, min(width, (i + 1) * 512)
        pb, pbk = banks[i % len(banks)]
        p.op("tensor", lambda e, c0=c0, c1=c1, pb=pb: e.matmul(
            pb[:, 0:c1 - c0], E["ones_row_b"][:, :], hi[0:1, c0:c1],
            start=True, stop=False), reads=[hi_k], writes=[pbk])
        p.op("tensor", lambda e, c0=c0, c1=c1, pb=pb: e.matmul(
            pb[:, 0:c1 - c0], E["ones_row_b"][:, :], lo[0:1, c0:c1],
            start=False, stop=True), reads=[lo_k], writes=[pbk])
        p.op("scalar", lambda e, c0=c0, c1=c1, pb=pb: e.copy(
            out_sb[:, c0:c1], pb[:, 0:c1 - c0]),
            reads=[pbk], writes=[out_key])


EPS_COL = {1e-5: 0, 1e-12: 1, 1.1920929e-07: 2}


def emit_rstd_row(p, out_ap, in_ap, rw_key, eps, scale=1.0, mult4=False, E=None):
    ec = E["epsc"]
    j = EPS_COL[eps]
    p.op("scalar", lambda e: e.activation(out_ap, in_ap, AF.Ln,
                                          bias=ec[0:1, j:j + 1],
                                          scale=float(scale)),
         reads=[rw_key, "c_eps_sb"], writes=[rw_key])
    if mult4:
        p.op("scalar", lambda e: e.activation(out_ap, out_ap, AF.Exp,
                                              bias=ec[0:1, 3:4], scale=-0.5),
             reads=[rw_key, "c_eps_sb"], writes=[rw_key])
    else:
        p.op("scalar", lambda e: e.activation(out_ap, out_ap, AF.Exp,
                                              bias=0.0, scale=-0.5),
             reads=[rw_key], writes=[rw_key])


class StatRows:

    def __init__(self, p, E, name, width, banks, step=1024):
        self.p, self.E, self.width = p, E, width
        self.step = step
        self.banks = banks
        self.sum_sb = E[f"{name}_sum"]
        self.sum_k = f"{name}_sum"
        self.ssq_sb = E[f"{name}_ssq"]
        self.ssq_k = f"{name}_ssq"
        self.first = True
        self.bi = 0

    def add_chunk(self, src_ap, src_key, sq_tmp, sq_key, col0=0):
        p, E = self.p, self.E
        w = self.widthof(src_ap)
        for b0 in range(0, w, self.step):
            b1 = min(w, b0 + self.step)
            p.op("vector", lambda e, b0=b0, b1=b1: e.tensor_tensor(
                sq_tmp[:, 0:b1 - b0], src_ap[:, b0:b1], src_ap[:, b0:b1],
                ALU.mult), reads=[src_key], writes=[sq_key])
            self._mm_part(src_ap, src_key, sq_tmp, sq_key, col0, b0, b1)
        return

    def _mm_part(self, src_ap, src_key, sq_tmp, sq_key, col0, b0, b1):
        p, E = self.p, self.E
        for i in range((b1 - b0 + 511) // 512):
            a0 = b0 + i * 512
            a1 = min(b1, a0 + 512)
            pb, pbk = self.banks[self.bi % len(self.banks)]
            self.bi += 1
            pb2, pbk2 = self.banks[self.bi % len(self.banks)]
            self.bi += 1
            p.op("tensor", lambda e, a0=a0, a1=a1, pb=pb: e.matmul(
                pb[0:1, 0:a1 - a0], E["ones_col_b"][:, :],
                src_ap[:, a0:a1], start=True, stop=True),
                reads=[src_key], writes=[pbk])
            p.op("tensor", lambda e, a0=a0, a1=a1, pb2=pb2, b0=b0: e.matmul(
                pb2[0:1, 0:a1 - a0], E["ones_col_b"][:, :],
                sq_tmp[:, a0 - b0:a1 - b0], start=True, stop=True),
                reads=[sq_key], writes=[pbk2])
            op = ALU.bypass if self.first else ALU.add
            c0, c1 = col0 + a0, col0 + a1
            if self.first:
                p.op("vector", lambda e, c0=c0, c1=c1, pb=pb: e.tensor_copy(
                    self.sum_sb[0:1, c0:c1], pb[0:1, 0:c1 - c0]),
                    reads=[pbk], writes=[self.sum_k])
                p.op("vector", lambda e, c0=c0, c1=c1, pb2=pb2: e.tensor_copy(
                    self.ssq_sb[0:1, c0:c1], pb2[0:1, 0:c1 - c0]),
                    reads=[pbk2], writes=[self.ssq_k])
            else:
                p.op("vector", lambda e, c0=c0, c1=c1, pb=pb: e.tensor_tensor(
                    self.sum_sb[0:1, c0:c1], self.sum_sb[0:1, c0:c1],
                    pb[0:1, 0:c1 - c0], ALU.add),
                    reads=[pbk, self.sum_k], writes=[self.sum_k])
                p.op("vector", lambda e, c0=c0, c1=c1, pb2=pb2: e.tensor_tensor(
                    self.ssq_sb[0:1, c0:c1], self.ssq_sb[0:1, c0:c1],
                    pb2[0:1, 0:c1 - c0], ALU.add),
                    reads=[pbk2, self.ssq_k], writes=[self.ssq_k])

    @staticmethod
    def widthof(ap):
        return ap.ap[-1][1]

    def chunk_done(self):
        self.first = False

    def finalize(self, denom, eps):
        p, E, w = self.p, self.E, self.width
        m_row, m_key = self.sum_sb, self.sum_k
        r_row, r_key = self.ssq_sb, self.ssq_k
        p.op("vector", lambda e: e.tensor_scalar(
            m_row[0:1, 0:w], m_row[0:1, 0:w], 1.0 / denom, None, ALU.mult),
            reads=[m_key], writes=[m_key])
        p.op("vector", lambda e: e.tensor_scalar(
            r_row[0:1, 0:w], r_row[0:1, 0:w], 1.0 / denom, None,
            ALU.mult), reads=[r_key], writes=[r_key])
        p.op("vector", lambda e: e.tensor_tensor(
            E["row_a"][0:1, 0:w], m_row[0:1, 0:w],
            m_row[0:1, 0:w], ALU.mult), reads=[m_key], writes=["row_a"])
        p.op("vector", lambda e: e.tensor_tensor(
            r_row[0:1, 0:w], r_row[0:1, 0:w],
            E["row_a"][0:1, 0:w], ALU.subtract),
            reads=[r_key, "row_a"], writes=[r_key])
        emit_rstd_row(p, r_row[0:1, 0:w], r_row[0:1, 0:w], r_key, eps, E=E)


def make_env(nc, stack, need):
    ent = stack.enter_context
    E = {}

    def sb(name, shape, dt):
        E[name] = ent(nc.sbuf_tensor(_uname(name), shape, dt))
        return E[name]

    E["pb"] = []
    for i in range(8):
        t = ent(nc.psum_tensor(f"pb{i}", [128, 512], F32))
        E["pb"].append(t)

    sb("ones_col_b", [128, 1], BF16)
    sb("ones_row_b", [1, 128], BF16)
    sb("ones_row_f", [1, 128], F32)
    sb("one_f", [1, 1], F32)
    if need.get("sel"):
        sb("sel", [128, 2], BF16)
        sb("selT", [2, 128], BF16)
    if need.get("mask"):
        sb("mask", [128, 128], F32)

    if need.get("X", True):
        sb("X", [128, NC_ * T], BF16)
    for r in ("row_a", "st_sum", "st_ssq"):
        sb(r, [1, T], F32)

    if need.get("row_b", True):
        sb("row_b", [1, 512], F32)
    rw = need.get("rowb_w", T)
    sb("rowb_hi", [1, rw], BF16)
    sb("rowb_lo", [1, rw], BF16)
    sb("colA", [128, 16], F32)
    sb("colB", [128, 16], F32)
    sb("epsc", [128, 8], F32)
    return E


def load_consts(p, E, dr, need):
    keys = []

    def ld(dst, name):
        p.op("sync", lambda e: e.dma_start(out=dst[:], in_=dr[name].ap()),
             writes=[name + "_sb"], dma=True)
        keys.append(name + "_sb")

    ld(E["ones_col_b"], "c_ones_col")
    ld(E["ones_row_b"], "c_ones_row_b")
    ld(E["ones_row_f"], "c_ones_row_f")
    ld(E["one_f"], "c_one")
    ld(E["epsc"], "c_eps")
    if need.get("sel"):
        ld(E["sel"], "c_sel")
        ld(E["selT"], "c_selT")
    if need.get("mask"):
        ld(E["mask"], "c_mask")
    for eng in ("tensor", "vector", "scalar", "gpsimd"):
        p.op(eng, lambda e: e.nop(), reads=list(keys))


def declare_const_drams(nc, need):
    dr = {}
    dr["c_ones_col"] = nc.dram_tensor("c_ones_col", [128, 1], BF16, kind="ExternalInput")
    dr["c_ones_row_b"] = nc.dram_tensor("c_ones_row_b", [1, 128], BF16, kind="ExternalInput")
    dr["c_ones_row_f"] = nc.dram_tensor("c_ones_row_f", [1, 128], F32, kind="ExternalInput")
    dr["c_one"] = nc.dram_tensor("c_one", [1, 1], F32, kind="ExternalInput")
    dr["c_eps"] = nc.dram_tensor("c_eps", [128, 8], F32, kind="ExternalInput")
    if need.get("sel"):
        dr["c_sel"] = nc.dram_tensor("c_sel", [128, 2], BF16, kind="ExternalInput")
        dr["c_selT"] = nc.dram_tensor("c_selT", [2, 128], BF16, kind="ExternalInput")
    if need.get("mask"):
        dr["c_mask"] = nc.dram_tensor("c_mask", [128, 128], F32, kind="ExternalInput")
    return dr


def const_arrays(need):
    c = {
        "c_ones_col": np.ones((128, 1), BF),
        "c_ones_row_b": np.ones((1, 128), BF),
        "c_ones_row_f": np.ones((1, 128), np.float32),
        "c_one": np.ones((1, 1), np.float32),
        "c_eps": np.tile(np.array([[1e-5, 1e-12, 1.1920929e-07, 1.3862943611198906,
                                    0.0, -2.4260151319598084, 0.0, 0.0]], np.float32), (128, 1)),
    }
    if need.get("sel"):
        sel = np.zeros((128, 2), BF)
        sel[0:64, 0] = 1
        sel[64:128, 1] = 1
        c["c_sel"] = sel
        c["c_selT"] = np.ascontiguousarray(sel.T)
    if need.get("mask"):
        q = np.arange(128)
        c["c_mask"] = np.where(q[None, :] >= q[:, None], 0.0, -1e9).astype(np.float32)
    return c


def emit_load_stream(p, E, x_dram):
    for dc in range(NC_):
        p.op("gpsimd", lambda e, dc=dc: e.dma_start(
            out=E["X"][:, dc * T:(dc + 1) * T],
            in_=x_dram.ap()[dc * 128:(dc + 1) * 128, :]),
            writes=[f"X{dc}"], dma=True)


def emit_store_stream_fp32(p, E, y_dram, src_name, src, scr, scr_k):
    pass



def emit_sumsq_row(p, E, src_sb, src_key, row_sb, row_key, pb, pbk, step=1024):
    for b0 in range(0, T, step):
        p.op("vector", lambda e, b0=b0, step=step: e.tensor_tensor(
            E["sq2"][:, 0:step], src_sb[:, b0:b0 + step],
            src_sb[:, b0:b0 + step], ALU.mult),
            reads=[src_key], writes=["sq2"])
        for i in range(step // 512):
            a0 = b0 + i * 512
            p.op("tensor", lambda e, a0=a0, b0=b0: e.matmul(
                pb[0:1, :], E["ones_col_b"][:, :],
                E["sq2"][:, a0 - b0:a0 - b0 + 512], start=True, stop=True),
                reads=["sq2"], writes=[pbk])
            p.op("vector", lambda e, a0=a0: e.tensor_copy(
                row_sb[0:1, a0:a0 + 512], pb[0:1, :]),
                reads=[pbk], writes=[row_key])

def build_block_a(nc, p, E, dr, out_mode, qe=None):
    pb = E["pb"]
    PB = [(pb[i], k_pb(i)) for i in range(8)]

    Xk = [f"X{dc}" for dc in range(NC_)]

    st = StatRows(p, E, "st", T, [PB[0], PB[1], PB[2], PB[3]])
    for dc in range(NC_):
        st.add_chunk(E["X"][:, dc * T:(dc + 1) * T], Xk[dc], E["sq2"][:, 0:1024], "sq2")
        st.chunk_done()
    st.finalize(D, 1e-5)
    p.op("vector", lambda e: e.tensor_copy(E["row_mb"][0:1, :], E["st_sum"][0:1, :]),
         reads=["st_sum"], writes=["row_mb"])
    for j in range(NC_):
        p.op("tensor", lambda e, j=j: e.matmul(
            pb[4][:, j:j + 1], E["st_ssq"][0:1, j * 128:(j + 1) * 128],
            E["one_f"][0:1, 0:1], start=True, stop=True),
            reads=["st_ssq"], writes=[k_pb(4)])
    p.op("scalar", lambda e: e.copy(E["colA"][:, 0:16], pb[4][:, 0:16]),
         reads=[k_pb(4)], writes=["colA"])
    if qe:
        qe.pump(16)

    p.op("gpsimd", lambda e: e.dma_start(out=E["wkv"][:], in_=dr["wkv_p"].ap()),
         writes=["wkv"], dma=True)
    for kv in range(2):
        p.op("vector", lambda e, kv=kv: e.tensor_tensor(
            E["wkv"][:, kv * 2048:(kv + 1) * 2048].rearrange("p (c n) -> p c n", n=128),
            E["wkv"][:, kv * 2048:(kv + 1) * 2048].rearrange("p (c n) -> p c n", n=128),
            E["gnT"][:, :, None].broadcast_to([128, 16, 128]), ALU.mult),
            reads=["wkv", "gnT_sb"], writes=["wkv"])
    for kv in range(2):
        for dc in range(NC_):
            p.op("tensor", lambda e, kv=kv, dc=dc: e.matmul(
                pb[5][0:1, 0:128], E["ones_col_b"][:, :],
                E["wkv"][:, kv * 2048 + dc * 128: kv * 2048 + (dc + 1) * 128],
                start=(dc == 0), stop=(dc == NC_ - 1)),
                reads=["wkv"], writes=[k_pb(5)])
        p.op("vector", lambda e, kv=kv: e.tensor_scalar(
            E["nbias"][0:1, kv * 128:(kv + 1) * 128], pb[5][0:1, 0:128],
            -1.0, None, ALU.mult), reads=[k_pb(5)], writes=["nbias"])

    for tc in range(4):
        for dc in range(NC_):
            p.op("tensor", lambda e, tc=tc, dc=dc: e.matmul(
                pb[tc][:, :], E["wkv"][:, dc * 128:(dc + 1) * 128],
                E["X"][:, dc * T + tc * 512: dc * T + tc * 512 + 512],
                start=(dc == 0), stop=False),
                reads=["wkv", Xk[dc]], writes=[k_pb(tc)])
        p.op("tensor", lambda e, tc=tc: e.matmul(
            pb[tc][:, :], E["nbias"][0:1, 0:128],
            E["row_mb"][0:1, tc * 512:tc * 512 + 512],
            start=False, stop=True),
            reads=["nbias", "row_mb"], writes=[k_pb(tc)])
        p.op("scalar", lambda e, tc=tc: e.copy(
            E["k_sb"][:, tc * 512:tc * 512 + 512], pb[tc][:, :]),
            reads=[k_pb(tc)], writes=["k_sb"])
        if qe:
            qe.pump(6)
    emit_sumsq_row(p, E, E["k_sb"], "k_sb", E["row_a"], "row_a", pb[4], k_pb(4))
    emit_rstd_row(p, E["row_a"][0:1, :], E["row_a"][0:1, :], "row_a", 1e-12,
                  mult4=True, E=E)
    for j in range(NC_):
        p.op("tensor", lambda e, j=j: e.matmul(
            pb[4][:, j:j + 1], E["row_a"][0:1, j * 128:(j + 1) * 128],
            E["one_f"][0:1, 0:1], start=True, stop=True),
            reads=["row_a"], writes=[k_pb(4)])
    p.op("scalar", lambda e: e.copy(E["colB"][:, 0:16], pb[4][:, 0:16]),
         reads=[k_pb(4)], writes=["colB"])

    for tt in range(NC_):
        if qe:
            qe.pump(2)
        for dc in range(NC_):
            p.op("tensor", lambda e, tt=tt, dc=dc: e.matmul(
                pb[4][:, 0:128], E["X"][:, dc * T + tt * 128: dc * T + (tt + 1) * 128],
                E["wkv"][:, 2048 + dc * 128: 2048 + (dc + 1) * 128],
                start=(dc == 0), stop=False),
                reads=["wkv", Xk[dc]], writes=[k_pb(4)])
        p.op("tensor", lambda e, tt=tt: e.matmul(
            pb[4][:, 0:128], E["row_mb"][0:1, tt * 128:(tt + 1) * 128],
            E["nbias"][0:1, 128:256], start=False, stop=True),
            reads=["row_mb", "nbias"], writes=[k_pb(4)])
        p.op("vector", lambda e, tt=tt: e.tensor_scalar(
            E["v_sb"][:, tt * 128:(tt + 1) * 128], pb[4][:, 0:128],
            E["colA"][:, tt:tt + 1], None, ALU.mult),
            reads=[k_pb(4), "colA"], writes=["v_sb"])

    wqbuf = [E["wkv"][:, 0:2048], E["wkv"][:, 2048:4096]]
    for h in range(NH):
        if qe and h == 12:
            qe.finish()
            p.barrier(reads=[t[2] for t in qe.todo] +
                      [f"qw{s}" for s in range(NSQ)],
                      writes=[f"ctx{hh}" for hh in range(12, 16)])
        wbuf = wqbuf[h % 2]
        wk = f"wq{h % 2}"
        wr = [wk, "wkv"] if h < 2 else [wk]
        p.op("gpsimd", lambda e, h=h, wbuf=wbuf: e.dma_start(
            out=wbuf, in_=dr["wq_p"].ap()[h]), writes=wr, dma=True)
        p.op("vector", lambda e, wbuf=wbuf: e.tensor_tensor(
            wbuf.rearrange("p (c n) -> p c n", n=128),
            wbuf.rearrange("p (c n) -> p c n", n=128),
            E["gnT"][:, :, None].broadcast_to([128, 16, 128]), ALU.mult),
            reads=[wk, "gnT_sb"], writes=[wk])
        for dc in range(NC_):
            p.op("tensor", lambda e, dc=dc, wbuf=wbuf: e.matmul(
                pb[5][0:1, 0:128], E["ones_col_b"][:, :],
                wbuf[:, dc * 128:(dc + 1) * 128],
                start=(dc == 0), stop=(dc == NC_ - 1)),
                reads=[wk], writes=[k_pb(5)])
        p.op("vector", lambda e: e.tensor_scalar(
            E["nbias"][0:1, 256:384], pb[5][0:1, 0:128], -1.0, None, ALU.mult),
            reads=[k_pb(5)], writes=["nbias_q"])
        for tc in range(4):
            for dc in range(NC_):
                p.op("tensor", lambda e, tc=tc, dc=dc, wbuf=wbuf: e.matmul(
                    pb[tc][:, :], wbuf[:, dc * 128:(dc + 1) * 128],
                    E["X"][:, dc * T + tc * 512: dc * T + tc * 512 + 512],
                    start=(dc == 0), stop=False),
                    reads=[wk, Xk[dc]], writes=[k_pb(tc)])
            p.op("tensor", lambda e, tc=tc: e.matmul(
                pb[tc][:, :], E["nbias"][0:1, 256:384],
                E["row_mb"][0:1, tc * 512:tc * 512 + 512],
                start=False, stop=True),
                reads=["nbias_q", "row_mb"], writes=[k_pb(tc)])
            p.op("scalar", lambda e, tc=tc: e.copy(
                E["q_sb"][:, tc * 512:tc * 512 + 512], pb[tc][:, :]),
                reads=[k_pb(tc)], writes=["q_sb"])
            if qe:
                qe.pump(2)
        emit_sumsq_row(p, E, E["q_sb"], "q_sb", E["row_a"], "row_a", pb[4], k_pb(4))
        emit_rstd_row(p, E["row_a"][0:1, :], E["row_a"][0:1, :], "row_a", 1e-12,
                      mult4=True, E=E)
        for b0 in range(0, T, 1024):
            emit_row_bcast_hilo(p, E, "row_a", E["row_a"][0:1, b0:b0 + 1024],
                                1024, [PB[0], PB[1]], E["f32a"], "f32a",
                                (E["rowb_hi"], "rowb_hi", E["rowb_lo"], "rowb_lo"))
            p.op("vector", lambda e, b0=b0: e.tensor_tensor(
                E["q_sb"][:, b0:b0 + 1024], E["q_sb"][:, b0:b0 + 1024],
                E["f32a"][:, 0:1024], ALU.mult),
                reads=["q_sb", "f32a"], writes=["q_sb"])

        for qs in range(4):
            if qe:
                qe.pump(12)
            q0 = qs * 512
            ctx_pb, ctx_k = PB[4]
            den_pb, den_k = PB[5]

            def emit_S(kt, q0=q0):
                s_pb, s_k = PB[2 + (kt % 2)]
                p.op("tensor", lambda e, kt=kt, q0=q0, s_pb=s_pb: e.matmul(
                    s_pb[:, :], E["k_sb"][:, kt * 128:(kt + 1) * 128],
                    E["q_sb"][:, q0:q0 + 512], start=True, stop=True),
                    reads=["k_sb", "q_sb"], writes=[s_k])

            emit_S(0)
            for kt in range(NC_):
                s_pb, s_k = PB[2 + (kt % 2)]
                eb = E[f"e_sb{kt % 2}"]
                ek = f"e_sb{kt % 2}"
                p.op("scalar", lambda e, kt=kt, s_pb=s_pb, eb=eb: e.activation(
                    eb[:, 0:512], s_pb[:, :], AF.Exp,
                    bias=0.0, scale=E["colB"][:, kt:kt + 1]),
                    reads=[s_k, "colB"], writes=[ek])
                if kt + 1 < NC_:
                    emit_S(kt + 1)
                p.op("tensor", lambda e, kt=kt, ctx_pb=ctx_pb, eb=eb: e.matmul(
                    ctx_pb[:, :], E["v_sb"][:, kt * 128:(kt + 1) * 128],
                    eb[:, 0:512], start=(kt == 0), stop=(kt == NC_ - 1)),
                    reads=["v_sb", ek], writes=[ctx_k])
                p.op("tensor", lambda e, kt=kt, den_pb=den_pb, eb=eb: e.matmul(
                    den_pb[0:1, :], E["ones_col_b"][:, :],
                    eb[:, 0:512], start=(kt == 0), stop=(kt == NC_ - 1)),
                    reads=[ek], writes=[den_k])
            p.op("vector", lambda e: e.reciprocal(E["row_b"][0:1, 0:512],
                                                  den_pb[0:1, :]),
                 reads=[den_k], writes=["row_b"])
            emit_row_bcast_hilo(p, E, "row_b", E["row_b"][0:1, 0:512], 512,
                                [PB[2]], E["f32b"][:, 0:512], "f32b",
                                (E["rowb_hi"], "rowb_hi", E["rowb_lo"], "rowb_lo"))
            p.op("vector", lambda e, h=h, q0=q0, ctx_pb=ctx_pb: e.tensor_tensor(
                E["ctx"][:, h * T + q0: h * T + q0 + 512], ctx_pb[:, :],
                E["f32b"][:, 0:512], ALU.mult),
                reads=[ctx_k, "f32b"], writes=[f"ctx{h}"])

    st1 = StatRows(p, E, "st", T, [PB[4], PB[5]])
    for do in range(NC_):
        if qe:
            qe.pump(2)
        wbuf = wqbuf[do % 2]
        wk = f"wq{do % 2}"
        p.op("gpsimd", lambda e, do=do, wbuf=wbuf: e.dma_start(
            out=wbuf, in_=dr["wout_p"].ap()[do]), writes=[wk], dma=True)
        for tc in range(4):
            for hc in range(NC_):
                p.op("tensor", lambda e, tc=tc, hc=hc, wbuf=wbuf: e.matmul(
                    pb[tc % 4][:, :], wbuf[:, hc * 128:(hc + 1) * 128],
                    E["ctx"][:, hc * T + tc * 512: hc * T + tc * 512 + 512],
                    start=(hc == 0), stop=(hc == NC_ - 1)),
                    reads=[wk, f"ctx{hc}"], writes=[k_pb(tc % 4)])
            p.op("scalar", lambda e, tc=tc: e.copy(
                E["q_sb"][:, tc * 512:tc * 512 + 512], pb[tc % 4][:, :]),
                reads=[k_pb(tc % 4)] + (["q_sb"] if do == 0 and tc == 0 else []),
                writes=["o_buf"])
        p.op("sync", lambda e, do=do: e.dma_start(
            out=dr["hb_out"].ap()[do], in_=E["q_sb"][:, :]),
            reads=["o_buf"], writes=[f"hb_out{do}"], dma=True)
        st1.add_chunk(E["q_sb"][:, 0:T], "o_buf", E["sq2"][:, 0:1024], "sq2")
        st1.chunk_done()
    st1.finalize(D, 1e-5)

    st2 = StatRows(p, E, "st", T, [PB[4], PB[5]])
    for b0 in (0, 1024):
        st2.first = True
        emit_row_bcast_hilo(p, E, "st_sum", E["st_sum"][0:1, b0:b0 + 1024], 1024,
                            [PB[0], PB[1]], E["f32a"], "f32a",
                            (E["rowb_hi"], "rowb_hi", E["rowb_lo"], "rowb_lo"))
        emit_row_bcast_hilo(p, E, "st_ssq", E["st_ssq"][0:1, b0:b0 + 1024], 1024,
                            [PB[2], PB[3]], E["f32b"], "f32b",
                            (E["rowb_hi"], "rowb_hi", E["rowb_lo"], "rowb_lo"))
        for do in range(NC_):
            ob = wqbuf[do % 2]
            obk = f"wq{do % 2}"
            p.op("sync", lambda e, do=do, ob=ob, b0=b0: e.dma_start(
                out=ob[:, 0:1024], in_=dr["hb_out"].ap()[do][:, b0:b0 + 1024]),
                reads=[f"hb_out{do}"], writes=[obk], dma=True)
            p.op("vector", lambda e, ob=ob: e.tensor_tensor(
                E["scr"][:, 0:1024], ob[:, 0:1024], E["f32a"][:, 0:1024],
                ALU.subtract), reads=[obk, "f32a"], writes=["scr"])
            p.op("vector", lambda e, do=do: e.tensor_scalar(
                E["scr"][:, 0:1024], E["scr"][:, 0:1024],
                E["goT"][:, do:do + 1], None, ALU.mult),
                reads=["scr", "goT_sb"], writes=["scr"])
            p.op("vector", lambda e: e.tensor_tensor(
                E["scr"][:, 0:1024], E["scr"][:, 0:1024], E["f32b"][:, 0:1024],
                ALU.mult), reads=["scr", "f32b"], writes=["scr"])
            p.op("vector", lambda e, do=do, b0=b0: e.tensor_tensor(
                E["ctx"][:, do * T + b0: do * T + b0 + 1024], E["scr"][:, 0:1024],
                E["X"][:, do * T + b0: do * T + b0 + 1024], ALU.add),
                reads=["scr", f"X{do}"], writes=[f"ctx{do}"])
            st2.add_chunk(E["ctx"][:, do * T + b0: do * T + b0 + 1024],
                          f"ctx{do}", E["sq2"][:, 0:1024], "sq2", col0=b0)
            if do == 0:
                st2.first = False
    st2.finalize(D, 1e-5)

    for b0 in (0, 1024):
        emit_row_bcast_hilo(p, E, "st_sum", E["st_sum"][0:1, b0:b0 + 1024], 1024,
                            [PB[0], PB[1]], E["f32a"], "f32a",
                            (E["rowb_hi"], "rowb_hi", E["rowb_lo"], "rowb_lo"))
        emit_row_bcast_hilo(p, E, "st_ssq", E["st_ssq"][0:1, b0:b0 + 1024], 1024,
                            [PB[2], PB[3]], E["f32b"], "f32b",
                            (E["rowb_hi"], "rowb_hi", E["rowb_lo"], "rowb_lo"))
        for dc in range(NC_):
            p.op("vector", lambda e, dc=dc, b0=b0: e.tensor_tensor(
                E["scr"][:, 0:1024], E["ctx"][:, dc * T + b0: dc * T + b0 + 1024],
                E["f32a"][:, 0:1024], ALU.subtract),
                reads=[f"ctx{dc}", "f32a"], writes=["scr"])
            if out_mode == "hbm_fp32":
                p.op("vector", lambda e: e.tensor_tensor(
                    E["scr"][:, 0:1024], E["scr"][:, 0:1024],
                    E["f32b"][:, 0:1024], ALU.mult),
                    reads=["scr", "f32b"], writes=["scr"])
                p.op("sync", lambda e, dc=dc, b0=b0: e.dma_start(
                    out=dr["y"].ap()[dc * 128:(dc + 1) * 128, b0:b0 + 1024],
                    in_=E["scr"][:, 0:1024]),
                    reads=["scr"], writes=["y"], dma=True)
            else:
                p.op("vector", lambda e, dc=dc, b0=b0: e.tensor_tensor(
                    E["X"][:, dc * T + b0: dc * T + b0 + 1024], E["scr"][:, 0:1024],
                    E["f32b"][:, 0:1024], ALU.mult),
                    reads=["scr", "f32b"], writes=[f"X{dc}"])
    if qe:
        qe.finish()


def env_block_a(nc, stack, E, quant=False):
    ent = stack.enter_context

    def sb(name, shape, dt):
        E[name] = ent(nc.sbuf_tensor(_uname(name), shape, dt))

    sb("ctx", [128, NH * T], BF16)
    if quant:
        names = [f"q{t}{s}" for s in range(NSQ) for t in ("w", "h", "b")]
        base = 12 * T
        for i, nm in enumerate(names[:16]):
            E[nm] = E["ctx"][:, base + i * 512: base + (i + 1) * 512]
        for nm in names[16:]:
            sb(nm, [128, 512], BF16)
        for s in range(4):
            sb(f"qsc{s}", [2, 512], BF16)
    sb("k_sb", [128, T], BF16)
    sb("v_sb", [128, T], BF16)
    sb("q_sb", [128, T], BF16)
    sb("sq2", [128, 1024], BF16)
    sb("row_mb", [1, T], BF16)
    sb("wkv", [128, 2 * 2048], BF16)
    sb("e_sb0", [128, 512], BF16)
    sb("e_sb1", [128, 512], BF16)
    sb("f32a", [128, 1024], F32)
    sb("f32b", [128, 1024], F32)
    sb("scr", [128, 1024], F32)
    sb("nbias", [1, 512], BF16)
    sb("gnT_s", [128, 16], F32)
    sb("goT_s", [128, 16], F32)
    E["gnT"] = E["gnT_s"]
    E["goT"] = E["goT_s"]


def build_launch_a():
    from contextlib import ExitStack
    nc = bass.Bass("TRN2", target_bir_lowering=False)
    need = {}
    dr = declare_const_drams(nc, need)
    dr["x"] = nc.dram_tensor("x", [D, T], F32, kind="ExternalInput")
    dr["wq_p"] = nc.dram_tensor("wq_p", [NH, 128, 2048], F32, kind="ExternalInput")
    dr["wkv_p"] = nc.dram_tensor("wkv_p", [128, 2 * 16 * 128], F32, kind="ExternalInput")
    dr["wout_p"] = nc.dram_tensor("wout_p", [NC_, 128, 2048], F32, kind="ExternalInput")
    dr["gnT"] = nc.dram_tensor("gnT", [128, 16], F32, kind="ExternalInput")
    dr["goT"] = nc.dram_tensor("goT", [128, 16], F32, kind="ExternalInput")
    dr["hb_out"] = nc.dram_tensor("hb_out", [NC_, 128, T], BF16, kind="Internal")
    dr["y"] = nc.dram_tensor("y", [D, T], F32, kind="ExternalOutput")

    with ExitStack() as stack:
        E = make_env(nc, stack, need)
        env_block_a(nc, stack, E)
        sems = Prog.make_sems(nc, stack)
        block = stack.enter_context(nc.Block())
        p = Prog(nc)
        load_consts(p, E, dr, need)
        p.op("sync", lambda e: e.dma_start(out=E["gnT"][:], in_=dr["gnT"].ap()),
             writes=["gnT_sb"], dma=True)
        p.op("sync", lambda e: e.dma_start(out=E["goT"][:], in_=dr["goT"].ap()),
             writes=["goT_sb"], dma=True)
        emit_load_stream(p, E, dr["x"])
        build_block_a(nc, p, E, dr, out_mode="hbm_fp32")
        p.finalize(block, sems)
    return nc


def pack_a(inputs):
    Wq = np.asarray(inputs["L0_attn_Wq"])
    Wkv = np.asarray(inputs["L0_attn_Wkv"])
    Wout = np.asarray(inputs["L0_attn_Wout"])
    g = np.asarray(inputs["L0_attn_g_norm"])
    go = np.asarray(inputs["L0_attn_g_out"])
    wq_p = np.ascontiguousarray(
        Wq.reshape(16, 128, 16, 128).transpose(2, 1, 0, 3)).reshape(NH, 128, 2048)
    wkv_p = np.ascontiguousarray(
        Wkv.reshape(16, 128, 2, 128).transpose(1, 2, 0, 3)).reshape(128, 2 * 16 * 128)
    wout_p = np.ascontiguousarray(
        Wout.reshape(16, 128, 16, 128).transpose(2, 1, 0, 3)).reshape(NC_, 128, 2048)
    gnT = np.ascontiguousarray(g.reshape(16, 128).T).astype(np.float32)
    goT = np.ascontiguousarray(go.reshape(16, 128).T).astype(np.float32)
    return dict(wq_p=wq_p, wkv_p=wkv_p, wout_p=wout_p, gnT=gnT, goT=goT)





INV_M = float(1.0 / np.float64(M_THR))


NSQ = 6


def env_block_b(nc, stack, E, fused=False):
    ent = stack.enter_context

    def sb(name, shape, dt):
        E[name] = ent(nc.sbuf_tensor(_uname(name), shape, dt))

    sb("h_sb", [128, 64 * 512], BF16)
    if not fused:
        sb("xq0", [128, 16 * 512], BF16)
        sb("xq1", [128, 16 * 512], BF16)
    sb("whg0", [128, 2048], BF16)
    sb("whg1", [128, 2048], BF16)
    sb("whp0", [128, 2048], BF16)
    sb("whp1", [128, 2048], BF16)
    sb("s_f32", [128, 512], F32)
    sb("f32a", [128, 512], F32)
    sb("f32b", [128, 512], F32)
    sb("f32c", [128, 512], F32)
    sb("scr", [128, 512], F32)
    sb("sqd", [128, 512], BF16)
    if not fused:
        h = E["h_sb"]
        for s in range(NSQ):
            E[f"qw{s}"] = h[:, (s * 3 + 0) * 512:(s * 3 + 1) * 512]
            E[f"qh{s}"] = h[:, (s * 3 + 1) * 512:(s * 3 + 2) * 512]
            E[f"qb{s}"] = h[:, (s * 3 + 2) * 512:(s * 3 + 3) * 512]
        for s in range(4):
            E[f"qsc{s}"] = h[0:2, (NSQ * 3 + s) * 512:(NSQ * 3 + s + 1) * 512]


class QuantEmitter:

    def __init__(self, p, E, dr, banks2=False):
        self.p, self.E = p, E
        self.banks2 = banks2
        groups = []
        gu = dr["wgu_p"].ap()
        g2 = dr["sc_gu2"].ap()
        order = []
        for j in range(64):
            order += [j, 64 + j]
        for j2 in order:
            grp = [(gu[j2][:, h * 512:(h + 1) * 512],
                    g2[j2][:, h * 512:(h + 1) * 512]) for h in range(4)]
            groups.append((grp, (f"scg{j2}",)))
        wp = dr["wp_p"].ap()
        scp = dr["sc_p"].ap()
        for do in range(16):
            fw = wp[do].rearrange("p c o -> p (c o)")
            fs = scp[do].rearrange("p c o -> p (c o)")
            for g in range(4):
                c0 = g * 2048
                grp = [(fw[:, c0 + h * 512:c0 + (h + 1) * 512],
                        fs[:, c0 + h * 512:c0 + (h + 1) * 512])
                       for h in range(4)]
                groups.append((grp, (f"scp{do}_{g}",)))
        self.todo = []
        uid = 0
        for grp, gkeys in groups:
            ukeys = []
            for i, (src, dst) in enumerate(grp):
                ukey = f"uq{uid}"
                uid += 1
                ukeys.append(ukey)
                bar = (gkeys, list(ukeys)) if i == len(grp) - 1 else None
                self.todo.append((src, dst, ukey, bar))
        self.w = 0

    def _bufs(self, u):
        s = u % NSQ
        s2 = u % 4
        E = self.E
        return (E[f"qw{s}"], E[f"qh{s}"], E[f"qb{s}"], E[f"qsc{s2}"],
                f"qw{s}", f"qh{s}", f"qb{s}", f"qsc{s2}")

    def _wave(self, w):
        p, E = self.p, self.E
        pb = E["pb"]
        n = len(self.todo)
        if 0 <= w - 5 < n:
            u = w - 5
            src, dst, ukey, bar = self.todo[u]
            qw, qh, qb, qsc, kw, kh, kb, ksc = self._bufs(u)
            p.op("sync", lambda e, dst=dst, qh=qh: e.dma_start(
                out=dst, in_=qh[:, :]), reads=[kh], writes=[ukey],
                dma=True)
            if bar is not None:
                gkeys, ukeys = bar
                p.barrier(reads=ukeys, writes=list(gkeys))
        if 0 <= w - 4 < n:
            u = w - 4
            qw, qh, qb, qsc, kw, kh, kb, ksc = self._bufs(u)
            p.op("vector", lambda e, qw=qw, qb=qb, qh=qh:
                 e.scalar_tensor_tensor(
                     qh[:, :], qw[:, :], INV_M, qb[:, :],
                     ALU.mult, ALU.is_gt),
                 reads=[kw, kb], writes=[kh])
            p.op("vector", lambda e, qw=qw, qb=qb: e.scalar_tensor_tensor(
                qw[:, :], qw[:, :], -INV_M, qb[:, :], ALU.mult, ALU.is_gt),
                reads=[kw, kb], writes=[kw])
            p.op("gpsimd", lambda e, qh=qh, qw=qw: e.tensor_tensor(
                qh[:, :], qh[:, :], qw[:, :], ALU.subtract),
                reads=[kh, kw], writes=[kh])
            p.op("gpsimd", lambda e, qh=qh, qb=qb: e.tensor_tensor(
                qh[:, :], qh[:, :], qb[:, :], ALU.mult),
                reads=[kh, kb], writes=[kh])
        if 0 <= w - 3 < n:
            u = w - 3
            qw, qh, qb, qsc, kw, kh, kb, ksc = self._bufs(u)
            b = self.bank(u)
            p.op("scalar", lambda e, b=b, qb=qb: e.copy(
                qb[:, :], pb[b][:, :]), reads=[k_pb(b)], writes=[kb])
        if 0 <= w - 2 < n:
            u = w - 2
            qw, qh, qb, qsc, kw, kh, kb, ksc = self._bufs(u)
            b = self.bank(u)
            bk = k_pb(b)
            p.op("tensor", lambda e, b=b, qh=qh: e.matmul(
                pb[b][0:2, :], E["sel"][:, :], qh[:, :],
                start=True, stop=True), reads=[kh], writes=[bk])
            p.op("vector", lambda e, b=b, qsc=qsc: e.tensor_scalar(
                qsc[0:2, :], pb[b][0:2, :], 1.0 / 64.0, 1e-8,
                ALU.mult, ALU.max), reads=[bk], writes=[ksc])
            p.op("tensor", lambda e, b=b, qsc=qsc: e.matmul(
                pb[b][:, :], E["selT"][:, :], qsc[0:2, :],
                start=True, stop=True), reads=[ksc], writes=[bk])
        if 0 <= w - 1 < n:
            u = w - 1
            qw, qh, qb, qsc, kw, kh, kb, ksc = self._bufs(u)
            p.op("scalar", lambda e, qw=qw, qh=qh: e.activation(
                qh[:, :], qw[:, :], AF.Abs), reads=[kw], writes=[kh])
        if w < len(self.todo):
            u = w
            src = self.todo[u][0]
            qw, qh, qb, qsc, kw, kh, kb, ksc = self._bufs(u)
            p.op("gpsimd", lambda e, src=src, qw=qw: e.dma_start(
                out=qw[:, :], in_=src), writes=[kw], dma=True)

    def bank(self, u):
        return 6 + u % 2 if self.banks2 else u % 8

    def pump(self, waves):
        n = len(self.todo) + 5
        for _ in range(waves):
            if self.w >= n:
                return
            self._wave(self.w)
            self.w += 1

    def finish(self):
        self.pump(len(self.todo) + 5)

    def emit_all(self):
        self.finish()


def build_block_b(nc, p, E, dr, out_mode, in_mode, quant=True):
    pb = E["pb"]
    PB = [(pb[i], k_pb(i)) for i in range(8)]

    if quant:
        qe = QuantEmitter(p, E, dr)
        qe.emit_all()
        p.barrier(reads=[t[2] for t in qe.todo] +
                  [f"qw{s}" for s in range(NSQ)],
                  writes=[f"h{j}" for j in range(64)])

    if in_mode == "hbm":
        xin = dr["x"].ap().rearrange("(c p) t -> p c t", p=128)
    st = StatRows(p, E, "st", 512, [PB[4], PB[5]], step=512)
    for q in range(4):
        if in_mode == "hbm":
            xq = E[f"xq{q % 2}"]
            xqk = f"xq{q % 2}"
            p.op("gpsimd", lambda e, q=q, xq=xq: e.dma_start(
                out=xq[:, :].rearrange("p (c t) -> p c t", t=512),
                in_=xin[:, :, q * 512:(q + 1) * 512]),
                writes=[xqk], dma=True)

            def xs(dc, q=q, xq=xq, xqk=xqk):
                return xq[:, dc * 512:(dc + 1) * 512], xqk
        else:
            def xs(dc, q=q):
                return (E["X"][:, dc * T + q * 512: dc * T + (q + 1) * 512],
                        f"X{dc}")
        for j in range(64):
            gb, gbk = pb[(j % 2) * 2], k_pb((j % 2) * 2)
            ub, ubk = pb[1 + (j % 2) * 2], k_pb(1 + (j % 2) * 2)
            assert f"scg{j}" in p.last_writer
            assert f"scg{64 + j}" in p.last_writer
            p.op("scalar", lambda e, j=j: e.dma_start(
                out=E["whg0"][:, :], in_=dr["sc_gu2"].ap()[j]),
                writes=["whg0"], dma=True, reads=[f"scg{j}"])
            p.op("scalar", lambda e, j=j: e.dma_start(
                out=E["whg1"][:, :], in_=dr["sc_gu2"].ap()[64 + j]),
                writes=["whg1"], dma=True, reads=[f"scg{64 + j}"])
            for dc in range(16):
                xa, xk = xs(dc)
                p.op("tensor", lambda e, dc=dc, gb=gb, xa=xa: e.matmul(
                    gb[:, :], E["whg0"][:, dc * 128:(dc + 1) * 128], xa,
                    start=(dc == 0), stop=(dc == 15)),
                    reads=["whg0", xk], writes=[gbk])
            for dc in range(16):
                xa, xk = xs(dc)
                p.op("tensor", lambda e, dc=dc, ub=ub, xa=xa: e.matmul(
                    ub[:, :], E["whg1"][:, dc * 128:(dc + 1) * 128], xa,
                    start=(dc == 0), stop=(dc == 15)),
                    reads=["whg1", xk], writes=[ubk])
            p.op("scalar", lambda e, gb=gb: e.activation(
                E["s_f32"][:, :], gb[:, :], AF.Silu), reads=[gbk], writes=["s_f32"])
            p.op("vector", lambda e, j=j, ub=ub: e.tensor_tensor(
                E["h_sb"][:, j * 512:(j + 1) * 512], E["s_f32"][:, :], ub[:, :],
                ALU.mult), reads=["s_f32", ubk], writes=[f"h{j}"])
            p.op("vector", lambda e, j=j: e.tensor_tensor(
                E["sqd"][:, 0:512], E["h_sb"][:, j * 512:(j + 1) * 512],
                E["h_sb"][:, j * 512:(j + 1) * 512], ALU.mult),
                reads=[f"h{j}"], writes=["sqd"])
            p.op("tensor", lambda e, j=j: e.matmul(
                pb[4][0:1, :], E["ones_col_b"][:, :], E["sqd"][:, 0:512],
                start=(j == 0), stop=(j == 63)), reads=["sqd"], writes=[k_pb(4)])
        p.op("vector", lambda e: e.tensor_copy(E["row_b"][0:1, :], pb[4][0:1, :]),
             reads=[k_pb(4)], writes=["row_b"])
        emit_rstd_row(p, E["row_b"][0:1, :], E["row_b"][0:1, :], "row_b",
                      1.1920929e-07, scale=1.0 / 8192.0, E=E)
        emit_row_bcast_hilo(p, E, "row_b", E["row_b"][0:1, :], 512,
                            [PB[5]], E["f32c"][:, :], "f32c",
                            (E["rowb_hi"], "rowb_hi", E["rowb_lo"], "rowb_lo"))
        st.first = True
        for do in range(16):
            db, dbk = pb[(do % 2) * 2], k_pb((do % 2) * 2)
            for grp in range(4):
                whp = E[f"whp{grp % 2}"]
                whpk = f"whp{grp % 2}"
                if q == 0:
                    assert f"scp{do}_{grp}" in p.last_writer
                p.op("gpsimd", lambda e, do=do, grp=grp, whp=whp: e.dma_start(
                    out=whp[:, :],
                    in_=dr["sc_p"].ap()[do].rearrange(
                        "p c o -> p (c o)")[:, grp * 2048:(grp + 1) * 2048]),
                    writes=[whpk], dma=True, reads=[f"scp{do}_{grp}"])
                for hc2 in range(16):
                    hc = grp * 16 + hc2
                    p.op("tensor", lambda e, hc=hc, hc2=hc2, db=db, whp=whp: e.matmul(
                        db[:, :], whp[:, hc2 * 128:(hc2 + 1) * 128],
                        E["h_sb"][:, hc * 512:(hc + 1) * 512],
                        start=(hc == 0), stop=(hc == 63)),
                        reads=[whpk, f"h{hc}"], writes=[dbk])
            xa, xk = xs(do)
            p.op("vector", lambda e, db=db: e.tensor_tensor(
                E["scr"][:, :], db[:, :], E["f32c"][:, :], ALU.mult),
                reads=[dbk, "f32c"], writes=["scr"])
            p.op("vector", lambda e, xa=xa: e.tensor_tensor(
                xa, E["scr"][:, :], xa, ALU.add),
                reads=["scr", xk], writes=[xk])
            st.add_chunk(xa, xk, E["sqd"][:, 0:512], "sqd")
            if do == 0:
                st.first = False
        st.finalize(D, 1e-5)
        emit_row_bcast_hilo(p, E, "st_sum", E["st_sum"][0:1, 0:512], 512,
                            [PB[4]], E["f32a"][:, :], "f32a",
                            (E["rowb_hi"], "rowb_hi", E["rowb_lo"], "rowb_lo"))
        emit_row_bcast_hilo(p, E, "st_ssq", E["st_ssq"][0:1, 0:512], 512,
                            [PB[5]], E["f32b"][:, :], "f32b",
                            (E["rowb_hi"], "rowb_hi", E["rowb_lo"], "rowb_lo"))
        for do in range(16):
            xa, xk = xs(do)
            p.op("vector", lambda e, xa=xa: e.tensor_tensor(
                E["scr"][:, :], xa, E["f32a"][:, :],
                ALU.subtract), reads=[xk, "f32a"], writes=["scr"])
            if out_mode == "hbm_fp32":
                p.op("vector", lambda e: e.tensor_tensor(
                    E["scr"][:, :], E["scr"][:, :], E["f32b"][:, :], ALU.mult),
                    reads=["scr", "f32b"], writes=["scr"])
                p.op("sync", lambda e, do=do, q=q: e.dma_start(
                    out=dr["y"].ap()[do * 128:(do + 1) * 128,
                                     q * 512:(q + 1) * 512],
                    in_=E["scr"][:, :]), reads=["scr"], writes=[f"y{do}_{q}"],
                    dma=True)
            else:
                p.op("vector", lambda e, do=do, q=q: e.tensor_tensor(
                    E["X"][:, do * T + q * 512: do * T + (q + 1) * 512],
                    E["scr"][:, :], E["f32b"][:, :], ALU.mult),
                    reads=["scr", "f32b"], writes=[f"X{do}"])


def build_launch_b():
    from contextlib import ExitStack
    nc = bass.Bass("TRN2", target_bir_lowering=False)
    need = {"sel": True, "X": False, "aug": False}
    dr = declare_const_drams(nc, need)
    dr["x"] = nc.dram_tensor("x", [D, T], F32, kind="ExternalInput")
    dr["wgu_p"] = nc.dram_tensor("wgu_p", [128, 128, 2048], F32, kind="ExternalInput")
    dr["wp_p"] = nc.dram_tensor("wp_p", [16, 128, 64, 128], F32, kind="ExternalInput")
    dr["sc_gu2"] = nc.dram_tensor("sc_gu2", [128, 128, 2048], BF16, kind="Internal")
    dr["sc_p"] = nc.dram_tensor("sc_p", [16, 128, 64, 128], BF16, kind="Internal")
    dr["y"] = nc.dram_tensor("y", [D, T], F32, kind="ExternalOutput")
    with ExitStack() as stack:
        E = make_env(nc, stack, need)
        env_block_b(nc, stack, E)
        sems = Prog.make_sems(nc, stack)
        block = stack.enter_context(nc.Block())
        p = Prog(nc)
        load_consts(p, E, dr, need)
        build_block_b(nc, p, E, dr, out_mode="hbm_fp32", in_mode="hbm")
        p.finalize(block, sems)
    return nc


def pack_b(inputs):
    wgu = np.asarray(inputs["L0_ffn_w_gate_up"])
    wpr = np.asarray(inputs["L0_ffn_w_proj"])
    wgu_p = np.ascontiguousarray(
        wgu.reshape(2, 64, 128, 16, 128).transpose(0, 1, 4, 3, 2)
    ).reshape(128, 128, 2048)
    wp_p = np.ascontiguousarray(
        wpr.reshape(16, 128, 64, 128).transpose(0, 3, 2, 1))
    return dict(wgu_p=wgu_p, wp_p=wp_p)


SC128 = 0.08838834764831845


def env_block_c(nc, stack, E):
    ent = stack.enter_context

    def sb(name, shape, dt):
        E[name] = ent(nc.sbuf_tensor(_uname(name), shape, dt))

    sb("ctx", [128, NH * T], BF16)
    sb("qk0", [128, 4096], BF16)
    sb("qk1", [128, 4096], BF16)
    sb("v4", [128, 4096], BF16)
    sb("q_sb", [128, T], BF16)
    sb("k_sb", [128, T], BF16)
    sb("e_sb0", [128, 512], BF16)
    sb("e_sb1", [128, 512], BF16)
    sb("bT_s", [128, 48], F32)
    E["bT"] = E["bT_s"]
    sb("bvr", [1, 512], BF16)
    sb("scs", [1, 8], F32)
    sb("f32a", [128, 512], F32)
    sb("f32b", [128, 512], F32)
    sb("f32c", [128, 512], F32)
    sb("sq2", [128, 512], BF16)
    E["negb"] = E["colA"]
    E["scr"] = E["f32c"]


def build_block_c(nc, p, E, dr, out_mode, in_mode):
    pb = E["pb"]
    PB = [(pb[i], k_pb(i)) for i in range(8)]
    Xk = [f"X{dc}" for dc in range(NC_)]
    if in_mode == "hbm":
        emit_load_stream(p, E, dr["x"])
    p.op("sync", lambda e: e.dma_start(out=E["bT"][:], in_=dr["bT"].ap()),
         writes=["bT"], dma=True)

    for hg in range(8):
        wv = E[f"qk{hg % 2}"][:, :]
        wvk = f"qk{hg % 2}"
        p.op("gpsimd", lambda e, hg=hg, wv=wv: e.dma_start(
            out=wv, in_=dr["wv_p"].ap()[hg]),
            writes=[wvk], dma=True)
        p.op("gpsimd", lambda e, hg=hg: e.dma_start(
            out=E["bvr"][0:1, 0:256], in_=dr["bv_p"].ap()[hg:hg + 1, :]),
            writes=["bvr"], dma=True)
        for tt in range(NC_):
            vb, vbk = PB[tt % 2]
            for dc in range(NC_):
                p.op("tensor", lambda e, tt=tt, dc=dc, vb=vb, wv=wv: e.matmul(
                    vb[:, 0:256], E["X"][:, dc * T + tt * 128: dc * T + (tt + 1) * 128],
                    wv[:, dc * 256:(dc + 1) * 256], start=(dc == 0), stop=False),
                    reads=[wvk, Xk[dc]], writes=[vbk])
            p.op("tensor", lambda e, vb=vb: e.matmul(
                vb[:, 0:256], E["ones_row_b"][:, :], E["bvr"][0:1, 0:256],
                start=False, stop=True), reads=["bvr"], writes=[vbk])
            p.op("scalar", lambda e, tt=tt, vb=vb: e.copy(
                E["v4"][:, tt * 256:(tt + 1) * 256], vb[:, 0:256]),
                reads=[vbk], writes=["v4"])

        for hh in range(2):
            h = hg * 2 + hh
            qk = E[f"qk{h % 2}"]
            qkk = f"qk{h % 2}"
            p.op("gpsimd", lambda e, h=h, qk=qk: e.dma_start(
                out=qk[:, :], in_=dr["wqk_p"].ap()[h].transpose([1, 0, 2])),
                writes=[qkk], dma=True)
            for kv in range(2):
                dst = E["q_sb"] if kv == 0 else E["k_sb"]
                dstk = "q_sb" if kv == 0 else "k_sb"
                for tc in range(4):
                    for dc in range(NC_):
                        p.op("tensor", lambda e, kv=kv, tc=tc, dc=dc, qk=qk: e.matmul(
                            pb[tc][:, :],
                            qk[:, kv * 2048 + dc * 128: kv * 2048 + (dc + 1) * 128],
                            E["X"][:, dc * T + tc * 512: dc * T + tc * 512 + 512],
                            start=(dc == 0), stop=(dc == NC_ - 1)),
                            reads=[qkk, Xk[dc]], writes=[k_pb(tc)])
                    p.op("vector", lambda e, kv=kv, tc=tc, h=h, dst=dst: e.tensor_scalar(
                        dst[:, tc * 512:tc * 512 + 512], pb[tc][:, :],
                        E["bT"][:, kv * 16 + h: kv * 16 + h + 1], None, ALU.add),
                        reads=[k_pb(tc), "bT"], writes=[dstk])
            emit_sumsq_row(p, E, E["k_sb"], "k_sb", E["row_a"], "row_a",
                           pb[4], k_pb(4), step=512)
            p.op("vector", lambda e: e.reduce_max(
                E["scs"][0:1, 0:1], E["row_a"][0:1, 0:T], axis=AX.X),
                reads=["row_a"], writes=["scs"])
            emit_sumsq_row(p, E, E["q_sb"], "q_sb", E["row_a"], "row_a",
                           pb[4], k_pb(4), step=512)
            p.op("vector", lambda e: e.reduce_max(
                E["scs"][0:1, 1:2], E["row_a"][0:1, 0:T], axis=AX.X),
                reads=["row_a"], writes=["scs"])
            p.op("vector", lambda e: e.tensor_tensor(
                E["scs"][0:1, 2:3], E["scs"][0:1, 0:1], E["scs"][0:1, 1:2],
                ALU.mult), reads=["scs"], writes=["scs"])
            p.op("scalar", lambda e: e.activation(
                E["scs"][0:1, 3:4], E["scs"][0:1, 2:3], AF.Ln, bias=0.0),
                reads=["scs"], writes=["scs"])
            p.op("scalar", lambda e: e.activation(
                E["scs"][0:1, 4:5], E["scs"][0:1, 3:4], AF.Exp,
                bias=E["epsc"][0:1, 5:6], scale=0.5),
                reads=["scs", "c_eps_sb"], writes=["scs"])
            p.op("vector", lambda e: e.tensor_scalar(
                E["scs"][0:1, 5:6], E["scs"][0:1, 4:5], -1.0, None, ALU.mult),
                reads=["scs"], writes=["scs"])
            p.op("tensor", lambda e: e.matmul(
                pb[7][:, 0:1], E["ones_row_f"][:, :], E["scs"][0:1, 5:6],
                start=True, stop=True), reads=["scs"], writes=[k_pb(7)])
            p.op("scalar", lambda e: e.copy(E["negb"][:, 0:1], pb[7][:, 0:1]),
                 reads=[k_pb(7)], writes=["negb"])

            for qs in range(4):
                nkt = 4 * qs + 4
                ctxb, ctxk = PB[2]
                denb, denk = PB[6]

                def emit_S(kt, qs=qs):
                    s_pb, s_k = PB[4 + (kt % 2)]
                    d = kt - 4 * qs
                    c0 = max(0, d * 128)
                    p.op("tensor", lambda e, kt=kt, qs=qs, c0=c0, s_pb=s_pb: e.matmul(
                        s_pb[:, c0:512], E["k_sb"][:, kt * 128:(kt + 1) * 128],
                        E["q_sb"][:, qs * 512 + c0: qs * 512 + 512],
                        start=True, stop=True),
                        reads=["k_sb", "q_sb"], writes=[s_k])
                    if d >= 0:
                        p.op("vector", lambda e, c0=c0, s_pb=s_pb: e.tensor_tensor(
                            s_pb[:, c0:c0 + 128], s_pb[:, c0:c0 + 128],
                            E["mask"][:, :], ALU.add),
                            reads=[s_k, "c_mask_sb"], writes=[s_k])

                kts = list(range(nkt))
                emit_S(kts[0])
                for idx, kt in enumerate(kts):
                    s_pb, s_k = PB[4 + (kt % 2)]
                    d = kt - 4 * qs
                    c0 = max(0, d * 128)
                    eb = E[f"e_sb{kt % 2}"]
                    ek = f"e_sb{kt % 2}"
                    first = idx == 0
                    last = idx == nkt - 1
                    p.op("scalar", lambda e, c0=c0, s_pb=s_pb, eb=eb: e.activation(
                        eb[:, c0:512], s_pb[:, c0:512], AF.Exp,
                        bias=E["negb"][:, 0:1], scale=SC128),
                        reads=[s_k, "negb"], writes=[ek])
                    if c0 > 0:
                        p.op("vector", lambda e, c0=c0, eb=eb: e.memset(
                            eb[:, 0:c0], 0.0), reads=[ek], writes=[ek])
                    if idx + 1 < nkt:
                        emit_S(kts[idx + 1])
                    p.op("tensor", lambda e, kt=kt, hh=hh, ctxb=ctxb,
                         eb=eb, first=first, last=last: e.matmul(
                        ctxb[:, 0:512],
                        E["v4"][:, kt * 256 + hh * 128: kt * 256 + (hh + 1) * 128],
                        eb[:, 0:512], start=first, stop=last),
                        reads=["v4", ek], writes=[ctxk])
                    p.op("tensor", lambda e, kt=kt, denb=denb, eb=eb,
                         first=first, last=last: e.matmul(
                        denb[0:1, 0:512], E["ones_col_b"][:, :],
                        eb[:, 0:512], start=first, stop=last),
                        reads=[ek], writes=[denk])
                p.op("vector", lambda e, denb=denb: e.reciprocal(
                    E["row_a"][0:1, 0:512], denb[0:1, :]),
                    reads=[denk], writes=["row_a"])
                emit_row_bcast_hilo(p, E, "row_a", E["row_a"][0:1, 0:512], 512,
                                    [PB[7]], E["f32c"][:, :], "f32c",
                                    (E["rowb_hi"], "rowb_hi", E["rowb_lo"], "rowb_lo"))
                p.op("vector", lambda e, h=h, qs=qs, ctxb=ctxb: e.tensor_tensor(
                    E["ctx"][:, h * T + qs * 512: h * T + (qs + 1) * 512],
                    ctxb[:, :], E["f32c"][:, :], ALU.mult),
                    reads=[ctxk, "f32c"], writes=[f"ctx{h}"])

    p.barrier(reads=["f32c"], writes=["scr"])
    st = StatRows(p, E, "st", T, [PB[6], PB[7]], step=512)
    for do in range(NC_):
        wo = E[f"qk{do % 2}"][:, 0:2048]
        wok = f"qk{do % 2}"
        p.op("gpsimd", lambda e, do=do, wo=wo: e.dma_start(
            out=wo, in_=dr["wo_p"].ap()[do]), writes=[wok], dma=True)
        for tc in range(4):
            for hc in range(NC_):
                p.op("tensor", lambda e, tc=tc, hc=hc, wo=wo: e.matmul(
                    pb[tc][:, :], wo[:, hc * 128:(hc + 1) * 128],
                    E["ctx"][:, hc * T + tc * 512: hc * T + tc * 512 + 512],
                    start=(hc == 0), stop=(hc == NC_ - 1)),
                    reads=[wok, f"ctx{hc}"], writes=[k_pb(tc)])
            p.op("vector", lambda e, do=do, tc=tc: e.tensor_tensor(
                E["X"][:, do * T + tc * 512: do * T + tc * 512 + 512],
                pb[tc][:, :],
                E["X"][:, do * T + tc * 512: do * T + tc * 512 + 512],
                ALU.add), reads=[k_pb(tc), Xk[do]], writes=[Xk[do]])
        st.add_chunk(E["X"][:, do * T:(do + 1) * T], Xk[do],
                     E["sq2"][:, 0:512], "sq2")
        st.chunk_done()
    st.finalize(D, 1e-5)
    for qtr in range(4):
        emit_row_bcast_hilo(p, E, "st_sum", E["st_sum"][0:1, qtr * 512:(qtr + 1) * 512],
                            512, [PB[6]], E["f32a"][:, :], "f32a",
                            (E["rowb_hi"], "rowb_hi", E["rowb_lo"], "rowb_lo"))
        emit_row_bcast_hilo(p, E, "st_ssq", E["st_ssq"][0:1, qtr * 512:(qtr + 1) * 512],
                            512, [PB[7]], E["f32b"][:, :], "f32b",
                            (E["rowb_hi"], "rowb_hi", E["rowb_lo"], "rowb_lo"))
        for dc in range(NC_):
            p.op("vector", lambda e, dc=dc, qtr=qtr: e.tensor_tensor(
                E["scr"][:, :], E["X"][:, dc * T + qtr * 512: dc * T + (qtr + 1) * 512],
                E["f32a"][:, :], ALU.subtract),
                reads=[Xk[dc], "f32a"], writes=["scr"])
            if out_mode == "hbm_fp32":
                p.op("vector", lambda e: e.tensor_tensor(
                    E["scr"][:, :], E["scr"][:, :], E["f32b"][:, :], ALU.mult),
                    reads=["scr", "f32b"], writes=["scr"])
                p.op("sync", lambda e, dc=dc, qtr=qtr: e.dma_start(
                    out=dr["y"].ap()[dc * 128:(dc + 1) * 128,
                                     qtr * 512:(qtr + 1) * 512],
                    in_=E["scr"][:, :]), reads=["scr"], writes=["y"], dma=True)
            else:
                p.op("vector", lambda e, dc=dc, qtr=qtr: e.tensor_tensor(
                    E["X"][:, dc * T + qtr * 512: dc * T + (qtr + 1) * 512],
                    E["scr"][:, :], E["f32b"][:, :], ALU.mult),
                    reads=["scr", "f32b"], writes=[Xk[dc]])


def build_launch_c():
    from contextlib import ExitStack
    nc = bass.Bass("TRN2", target_bir_lowering=False)
    need = {"mask": True, "aug": False, "rowb_w": 1024, "row_b": False}
    dr = declare_const_drams(nc, need)
    dr["x"] = nc.dram_tensor("x", [D, T], F32, kind="ExternalInput")
    dr["wqk_p"] = nc.dram_tensor("wqk_p", [NH, 2, 128, 2048], F32, kind="ExternalInput")
    dr["wv_p"] = nc.dram_tensor("wv_p", [8, 128, 4096], F32, kind="ExternalInput")
    dr["wo_p"] = nc.dram_tensor("wo_p", [NC_, 128, 2048], F32, kind="ExternalInput")
    dr["bT"] = nc.dram_tensor("bT", [128, 48], F32, kind="ExternalInput")
    dr["bv_p"] = nc.dram_tensor("bv_p", [8, 256], F32, kind="ExternalInput")
    dr["y"] = nc.dram_tensor("y", [D, T], F32, kind="ExternalOutput")
    with ExitStack() as stack:
        E = make_env(nc, stack, need)
        env_block_c(nc, stack, E)
        sems = Prog.make_sems(nc, stack)
        block = stack.enter_context(nc.Block())
        p = Prog(nc)
        load_consts(p, E, dr, need)
        build_block_c(nc, p, E, dr, out_mode="hbm_fp32", in_mode="hbm")
        p.finalize(block, sems)
    return nc


def pack_c(inputs):
    wqkv = np.asarray(inputs["L1_attn_w_qkv"])
    b = np.asarray(inputs["L1_attn_b_qkv"])
    wo = np.asarray(inputs["L1_attn_w_o"])
    wq = wqkv[:, 0:2048].reshape(16, 128, 16, 128)
    wk = wqkv[:, 2048:4096].reshape(16, 128, 16, 128)
    wqk_p = np.ascontiguousarray(np.stack(
        [wq.transpose(2, 1, 0, 3).reshape(16, 128, 2048),
         wk.transpose(2, 1, 0, 3).reshape(16, 128, 2048)], axis=1))
    wv_p = np.ascontiguousarray(
        wqkv[:, 4096:6144].reshape(16, 128, 8, 256).transpose(2, 1, 0, 3)
    ).reshape(8, 128, 4096)
    wo_p = np.ascontiguousarray(
        wo.reshape(16, 128, 16, 128).transpose(2, 1, 0, 3)).reshape(NC_, 128, 2048)
    bT = np.ascontiguousarray(
        b.reshape(3, 16, 128).transpose(2, 0, 1).reshape(128, 48)).astype(np.float32)
    bv_p = np.ascontiguousarray(b[4096:].reshape(8, 256)).astype(np.float32)
    return dict(wqk_p=wqk_p, wv_p=wv_p, wo_p=wo_p, bT=bT, bv_p=bv_p)


NI = 44


def env_block_d(nc, stack, E):
    ent = stack.enter_context

    def sb(name, shape, dt):
        E[name] = ent(nc.sbuf_tensor(_uname(name), shape, dt))

    sb("hd", [128, NI * 512], BF16)
    sb("wgu0", [128, 4096], BF16)
    sb("wgu1", [128, 4096], BF16)
    sb("wd0", [128, NI * 128], BF16)
    sb("wd1", [128, NI * 128], BF16)
    sb("s_f32", [128, 512], F32)
    sb("gsc_s", [128, NI], F32)
    sb("usc_s", [128, NI], F32)
    sb("dsc_s", [128, 16], F32)
    E["gsc"] = E["gsc_s"]
    E["usc"] = E["usc_s"]
    E["dsc"] = E["dsc_s"]
    sb("f32a", [128, 512], F32)
    sb("f32b", [128, 512], F32)
    sb("scr", [128, 512], F32)
    sb("sq2", [128, 512], BF16)


def build_block_d(nc, p, E, dr, out_mode, in_mode):
    pb = E["pb"]
    PB = [(pb[i], k_pb(i)) for i in range(8)]
    for nm in ("gsc", "usc", "dsc"):
        p.op("sync", lambda e, nm=nm: e.dma_start(
            out=E[nm][:, :], in_=dr[nm].ap()), writes=[nm], dma=True)

    if in_mode == "hbm":
        emit_load_stream(p, E, dr["x"])
    Xk = [f"X{dc}" for dc in range(NC_)]
    for q in range(4):
        b0 = q * 512

        def xs(dc, b0=b0):
            return (E["X"][:, dc * T + b0: dc * T + b0 + 512], Xk[dc])

        for ic in range(NI):
            wgu = E[f"wgu{ic % 2}"]
            wguk = f"wgu{ic % 2}"
            p.op("gpsimd", lambda e, ic=ic, wgu=wgu: e.dma_start(
                out=wgu[:, :], in_=dr["wgu_p"].ap()[ic]), writes=[wguk], dma=True)
            gb0 = (ic % 2) * 4
            g_b, g_k = pb[gb0], k_pb(gb0)
            u_b, u_k = pb[gb0 + 1], k_pb(gb0 + 1)
            for dc in range(NC_):
                xa, xk = xs(dc)
                p.op("tensor", lambda e, dc=dc, g_b=g_b, wgu=wgu, xa=xa: e.matmul(
                    g_b[:, :], wgu[:, dc * 128:(dc + 1) * 128], xa,
                    start=(dc == 0), stop=(dc == NC_ - 1)),
                    reads=[wguk, xk], writes=[g_k])
            for dc in range(NC_):
                xa, xk = xs(dc)
                p.op("tensor", lambda e, dc=dc, u_b=u_b, wgu=wgu, xa=xa: e.matmul(
                    u_b[:, :], wgu[:, 2048 + dc * 128: 2048 + (dc + 1) * 128], xa,
                    start=(dc == 0), stop=(dc == NC_ - 1)),
                    reads=[wguk, xk], writes=[u_k])
            p.op("scalar", lambda e, ic=ic, g_b=g_b: e.activation(
                E["s_f32"][:, :], g_b[:, :], AF.Silu,
                scale=E["gsc"][:, ic:ic + 1]),
                reads=[g_k, "gsc"], writes=["s_f32"])
            p.op("vector", lambda e, ic=ic, u_b=u_b: e.scalar_tensor_tensor(
                E["hd"][:, ic * 512:(ic + 1) * 512], u_b[:, :],
                E["usc"][:, ic:ic + 1], E["s_f32"][:, :],
                ALU.mult, ALU.mult),
                reads=[u_k, "usc", "s_f32"], writes=[f"hd{ic}"])
        st = StatRows(p, E, "st", 512, [PB[6], PB[7]], step=512)
        st.first = True
        for do in range(NC_):
            wd = E[f"wd{do % 2}"]
            wdk = f"wd{do % 2}"
            p.op("gpsimd", lambda e, do=do, wd=wd: e.dma_start(
                out=wd[:, :], in_=dr["wd_p"].ap()[do]), writes=[wdk], dma=True)
            db, dk = PB[(do % 2) * 3]
            for ic in range(NI):
                p.op("tensor", lambda e, ic=ic, db=db, wd=wd: e.matmul(
                    db[:, :], wd[:, ic * 128:(ic + 1) * 128],
                    E["hd"][:, ic * 512:(ic + 1) * 512],
                    start=(ic == 0), stop=(ic == NI - 1)),
                    reads=[wdk, f"hd{ic}"], writes=[dk])
            xa, xk = xs(do)
            p.op("vector", lambda e, do=do, db=db, xa=xa: e.scalar_tensor_tensor(
                xa, db[:, :], E["dsc"][:, do:do + 1], xa,
                ALU.mult, ALU.add),
                reads=[dk, "dsc", xk], writes=[xk])
            st.add_chunk(xa, xk, E["sq2"][:, 0:512], "sq2")
            if do == 0:
                st.first = False
        st.finalize(D, 1e-5)
        emit_row_bcast_hilo(p, E, "st_sum", E["st_sum"][0:1, 0:512], 512,
                            [PB[6]], E["f32a"][:, :], "f32a",
                            (E["rowb_hi"], "rowb_hi", E["rowb_lo"], "rowb_lo"))
        emit_row_bcast_hilo(p, E, "st_ssq", E["st_ssq"][0:1, 0:512], 512,
                            [PB[7]], E["f32b"][:, :], "f32b",
                            (E["rowb_hi"], "rowb_hi", E["rowb_lo"], "rowb_lo"))
        for dc in range(NC_):
            xa, xk = xs(dc)
            ob, obk = ((E["scr"], "scr") if dc % 2 == 0 else
                       (E["s_f32"], "s_f32"))
            p.op("vector", lambda e, xa=xa, ob=ob: e.tensor_tensor(
                ob[:, :], xa, E["f32a"][:, :], ALU.subtract),
                reads=[xk, "f32a"], writes=[obk])
            p.op("vector", lambda e, ob=ob: e.tensor_tensor(
                ob[:, :], ob[:, :], E["f32b"][:, :], ALU.mult),
                reads=[obk, "f32b"], writes=[obk])
            p.op("sync", lambda e, dc=dc, b0=b0, ob=ob: e.dma_start(
                out=dr["y"].ap()[dc * 128:(dc + 1) * 128, b0:b0 + 512],
                in_=ob[:, :]), reads=[obk], writes=[f"y{dc}_{b0}"],
                dma=True)


def build_launch_d():
    from contextlib import ExitStack
    nc = bass.Bass("TRN2", target_bir_lowering=False)
    need = {"X": True, "aug": False, "rowb_w": 1024}
    dr = declare_const_drams(nc, need)
    dr["x"] = nc.dram_tensor("x", [D, T], F32, kind="ExternalInput")
    dr["wgu_p"] = nc.dram_tensor("wgu_p", [NI, 128, 4096], F32, kind="ExternalInput")
    dr["wd_p"] = nc.dram_tensor("wd_p", [NC_, 128, NI * 128], F32, kind="ExternalInput")
    dr["gsc"] = nc.dram_tensor("gsc", [128, NI], F32, kind="ExternalInput")
    dr["usc"] = nc.dram_tensor("usc", [128, NI], F32, kind="ExternalInput")
    dr["dsc"] = nc.dram_tensor("dsc", [128, 16], F32, kind="ExternalInput")
    dr["y"] = nc.dram_tensor("y", [D, T], F32, kind="ExternalOutput")
    with ExitStack() as stack:
        E = make_env(nc, stack, need)
        env_block_d(nc, stack, E)
        sems = Prog.make_sems(nc, stack)
        block = stack.enter_context(nc.Block())
        p = Prog(nc)
        load_consts(p, E, dr, need)
        build_block_d(nc, p, E, dr, out_mode="hbm_fp32", in_mode="hbm")
        p.finalize(block, sems)
    return nc


def pack_d(inputs):
    gw = np.asarray(inputs["L1_ffn_gate_w"])
    uw = np.asarray(inputs["L1_ffn_up_w"])
    dw = np.asarray(inputs["L1_ffn_down_w"])
    def t_iT(w):
        return w.T.reshape(16, 128, 44, 128).transpose(2, 1, 0, 3)
    wg = t_iT(gw).reshape(NI, 128, 2048)
    wu = t_iT(uw).reshape(NI, 128, 2048)
    wgu_p = np.ascontiguousarray(
        np.stack([wg, wu], axis=1).transpose(0, 2, 1, 3)).reshape(NI, 128, 4096)
    wd_p = np.ascontiguousarray(
        dw.T.reshape(44, 128, 16, 128).transpose(2, 1, 0, 3)).reshape(NC_, 128, NI * 128)
    gsc = np.ascontiguousarray(
        np.asarray(inputs["L1_ffn_gate_s"]).reshape(44, 128).T).astype(np.float32)
    usc = np.ascontiguousarray(
        np.asarray(inputs["L1_ffn_up_s"]).reshape(44, 128).T).astype(np.float32)
    dsc = np.ascontiguousarray(
        np.asarray(inputs["L1_ffn_down_s"]).reshape(16, 128).T).astype(np.float32)
    return dict(wgu_p=wgu_p, wd_p=wd_p, gsc=gsc, usc=usc, dsc=dsc)



_NC_CACHE = {}


def _get_nc(name):
    if name not in _NC_CACHE:
        _NC_CACHE[name] = {"a": build_launch_a, "b": build_launch_b,
                           "c": build_launch_c, "d": build_launch_d}[name]()
    return _NC_CACHE[name]


def kernel_unfused(**inputs):
    x = np.asarray(inputs["L0_attn_x"])
    nb = x.shape[0]
    cores = list(range(nb))

    pa, pb_, pc, pd = (pack_a(inputs), pack_b(inputs), pack_c(inputs),
                       pack_d(inputs))
    ca = const_arrays({})
    cb = const_arrays({"sel": True})
    cc = const_arrays({"mask": True})
    cd = const_arrays({})

    cur = [np.ascontiguousarray(x[b].T) for b in range(nb)]

    res = run_bass_kernel_spmd(
        _get_nc("a"), [dict(ca, **pa, x=cur[b]) for b in range(nb)],
        core_ids=cores)
    cur = [res.results[b]["y"] for b in range(nb)]

    res = run_bass_kernel_spmd(
        _get_nc("b"), [dict(cb, **pb_, x=cur[b]) for b in range(nb)],
        core_ids=cores)
    cur = [res.results[b]["y"] for b in range(nb)]

    res = run_bass_kernel_spmd(
        _get_nc("c"), [dict(cc, **pc, x=cur[b]) for b in range(nb)],
        core_ids=cores)
    cur = [res.results[b]["y"] for b in range(nb)]

    res = run_bass_kernel_spmd(
        _get_nc("d"), [dict(cd, **pd, x=cur[b]) for b in range(nb)],
        core_ids=cores)
    out = np.stack([np.ascontiguousarray(res.results[b]["y"].T)
                    for b in range(nb)])
    return out.astype(np.float32)


def build_fused():
    from contextlib import ExitStack
    nc = bass.Bass("TRN2", target_bir_lowering=False)
    need = {"sel": True, "mask": True, "X": True, "aug": True,
            "rowb_w": 1024, "row_b": True}
    dr = declare_const_drams(nc, need)
    dr["x"] = nc.dram_tensor("x", [D, T], F32, kind="ExternalInput")
    dr["y"] = nc.dram_tensor("y", [D, T], F32, kind="ExternalOutput")
    dr["wq_p"] = nc.dram_tensor("wq_p", [NH, 128, 2048], F32, kind="ExternalInput")
    dr["wkv_p"] = nc.dram_tensor("wkv_p", [128, 2 * 16 * 128], F32, kind="ExternalInput")
    dr["wout_p"] = nc.dram_tensor("wout_p", [NC_, 128, 2048], F32, kind="ExternalInput")
    dr["gnT"] = nc.dram_tensor("gnT", [128, 16], F32, kind="ExternalInput")
    dr["goT"] = nc.dram_tensor("goT", [128, 16], F32, kind="ExternalInput")
    dr["hb_out"] = nc.dram_tensor("hb_out", [NC_, 128, T], BF16, kind="Internal")
    dr["wgu_p"] = nc.dram_tensor("wgu_p", [128, 128, 2048], F32, kind="ExternalInput")
    dr["wp_p"] = nc.dram_tensor("wp_p", [16, 128, 64, 128], F32, kind="ExternalInput")
    dr["sc_gu2"] = nc.dram_tensor("sc_gu2", [128, 128, 2048], BF16, kind="Internal")
    dr["sc_p"] = nc.dram_tensor("sc_p", [16, 128, 64, 128], BF16, kind="Internal")
    dr["wqk_p"] = nc.dram_tensor("wqk_p", [NH, 2, 128, 2048], F32, kind="ExternalInput")
    dr["wv_p"] = nc.dram_tensor("wv_p", [8, 128, 4096], F32, kind="ExternalInput")
    dr["wo_p"] = nc.dram_tensor("wo_p", [NC_, 128, 2048], F32, kind="ExternalInput")
    dr["bT"] = nc.dram_tensor("bT", [128, 48], F32, kind="ExternalInput")
    dr["bv_p"] = nc.dram_tensor("bv_p", [8, 256], F32, kind="ExternalInput")
    dr["wgu_d"] = nc.dram_tensor("wgu_d", [NI, 128, 4096], F32, kind="ExternalInput")
    dr["wd_p"] = nc.dram_tensor("wd_p", [NC_, 128, NI * 128], F32, kind="ExternalInput")
    dr["gsc"] = nc.dram_tensor("gsc", [128, NI], F32, kind="ExternalInput")
    dr["usc"] = nc.dram_tensor("usc", [128, NI], F32, kind="ExternalInput")
    dr["dsc"] = nc.dram_tensor("dsc", [128, 16], F32, kind="ExternalInput")

    with ExitStack() as stack:
        E = make_env(nc, stack, need)
        sems = Prog.make_sems(nc, stack)
        block = stack.enter_context(nc.Block())
        p = Prog(nc)
        load_consts(p, E, dr, need)
        emit_load_stream(p, E, dr["x"])
        with ExitStack() as sa:
            env_block_a(nc, sa, E, quant=True)
            Ea = dict(E)
            qe = QuantEmitter(p, Ea, dr, banks2=True)
            p.op("sync", lambda e: e.dma_start(out=Ea["gnT"][:], in_=dr["gnT"].ap()),
                 writes=["gnT_sb"], dma=True)
            p.op("sync", lambda e: e.dma_start(out=Ea["goT"][:], in_=dr["goT"].ap()),
                 writes=["goT_sb"], dma=True)
            build_block_a(nc, p, Ea, dr, out_mode="sbuf", qe=qe)
        p.full_barrier()
        with ExitStack() as s2:
            env_block_b(nc, s2, E, fused=True)
            build_block_b(nc, p, dict(E), dr, out_mode="sbuf", in_mode="sbuf",
                          quant=False)
        p.full_barrier()
        with ExitStack() as s3:
            env_block_c(nc, s3, E)
            build_block_c(nc, p, dict(E), dr, out_mode="sbuf", in_mode="sbuf")
        p.full_barrier()
        with ExitStack() as s4:
            env_block_d(nc, s4, E)
            dr2 = dict(dr)
            dr2["wgu_p"] = dr["wgu_d"]
            build_block_d(nc, p, dict(E), dr2, out_mode="hbm_fp32", in_mode="sbuf")
        p.finalize(block, sems)
    return nc


def kernel(**inputs):
    import os
    if os.environ.get("MK_UNFUSED"):
        return kernel_unfused(**inputs)
    x = np.asarray(inputs["L0_attn_x"])
    nb = x.shape[0]
    cores = list(range(nb))
    packs = {}
    packs.update(pack_a(inputs))
    packs.update(pack_b(inputs))
    packs.update(pack_c(inputs))
    pd = pack_d(inputs)
    packs["wgu_d"] = pd.pop("wgu_p")
    packs.update(pd)
    consts = const_arrays({"sel": True, "mask": True})
    if "fused" not in _NC_CACHE:
        _NC_CACHE["fused"] = build_fused()
    nc = _NC_CACHE["fused"]
    in_maps = [dict(consts, **packs, x=np.ascontiguousarray(x[b].T))
               for b in range(nb)]
    res = run_bass_kernel_spmd(nc, in_maps, core_ids=cores)
    out = np.stack([np.ascontiguousarray(res.results[b]["y"].T)
                    for b in range(nb)])
    return out.astype(np.float32)


# version 07
# speedup vs baseline: 1.0741x; 1.0741x over previous
import numpy as np
import ml_dtypes
import concourse.bass as bass
import concourse.mybir as mybir
from concourse.bass_utils import run_bass_kernel_spmd

F32 = mybir.dt.float32
BF16 = mybir.dt.bfloat16
AF = mybir.ActivationFunctionType
ALU = mybir.AluOpType
AX = mybir.AxisListType
BF = ml_dtypes.bfloat16

D = 2048
T = 2048
NC_ = 16
NH = 16
DH = 128
LN2_4 = 1.3862943611198906
M_THR = np.float32(513.0 / 1024.0)


_SBUF_SEQ = [0]


def _uname(name):
    _SBUF_SEQ[0] += 1
    return f"{name}_u{_SBUF_SEQ[0]}"


class Prog:
    ENGINES = ("tensor", "vector", "scalar", "sync", "gpsimd")

    def __init__(self, nc):
        self.nc = nc
        self.ops = []
        self.last_writer = {}
        self.readers = {}
        self.base_dep = None

    def op(self, eng, emit, reads=(), writes=(), dma=False, bar=False):
        idx = len(self.ops)
        deps = set()
        for k in reads:
            w = self.last_writer.get(k)
            if w is not None:
                deps.add(w)
        for k in writes:
            w = self.last_writer.get(k)
            if w is not None:
                deps.add(w)
            deps |= self.readers.get(k, set())
        if self.base_dep is not None and not bar:
            deps.add(self.base_dep)
        deps.discard(idx)
        self.ops.append(dict(idx=idx, eng=eng, emit=emit, deps=deps, dma=dma,
                             bar=bar))
        for k in reads:
            self.readers.setdefault(k, set()).add(idx)
        for k in writes:
            self.last_writer[k] = idx
            self.readers[k] = set()
        return idx

    def barrier(self, reads=(), writes=()):
        return self.op("vector", None, reads=reads, writes=writes, bar=True)

    def full_barrier(self):
        keys = set(self.last_writer) | set(self.readers)
        idx = self.barrier(writes=sorted(keys))
        self.base_dep = idx
        return idx

    NDMA = 4

    def finalize(self, block, sems):
        ops = self.ops
        n = len(ops)
        for o in ops:
            if any(ops[d].get("bar") for d in o["deps"]):
                nd = set()
                stk = list(o["deps"])
                while stk:
                    d = stk.pop()
                    if ops[d].get("bar"):
                        stk.extend(ops[d]["deps"])
                    else:
                        nd.add(d)
                o["deps"] = nd

        def edge_needed(dop, o):
            if dop["dma"]:
                return True
            if dop["eng"] == o["eng"] == "tensor" and not o["dma"]:
                return False
            return True

        need_inc = [False] * n
        for o in ops:
            if o.get("bar"):
                continue
            if o["dma"]:
                need_inc[o["idx"]] = True
            for d in o["deps"]:
                dop = ops[d]
                if edge_needed(dop, o):
                    need_inc[d] = True
        last_of = {}
        for o in ops:
            if not o["dma"] and not o.get("bar"):
                last_of[o["eng"]] = o["idx"]
        for i in last_of.values():
            need_inc[i] = True
        semkey = [None] * n
        val = [0] * n
        cum = {}
        dma_count = {}
        for o in ops:
            i = o["idx"]
            if o["dma"]:
                q = o["eng"]
                c = dma_count.get(q, 0)
                dma_count[q] = c + 1
                sk = f"{q}_d{c % self.NDMA}"
                cum[sk] = cum.get(sk, 0) + 16
            else:
                sk = o["eng"]
                if need_inc[i]:
                    cum[sk] = cum.get(sk, 0) + 1
            semkey[i] = sk
            val[i] = cum.get(sk, 0)
        finals = dict(cum)

        def body_for(eng_name):
            def body(eng):
                waited = {}
                for o in ops:
                    if o["eng"] != eng_name or o.get("bar"):
                        continue
                    i = o["idx"]
                    want = {}
                    for d in o["deps"]:
                        dop = ops[d]
                        if not edge_needed(dop, o):
                            continue
                        sk, v = semkey[d], val[d]
                        if v > waited.get(sk, 0) and v > want.get(sk, 0):
                            want[sk] = v
                    if o["dma"]:
                        sk, v = semkey[i], val[i] - 16
                        if v > waited.get(sk, 0) and v > want.get(sk, 0):
                            want[sk] = v
                    for sk, v in want.items():
                        eng.wait_ge(sems[sk], v)
                        waited[sk] = v
                    ins = o["emit"](eng)
                    if need_inc[i]:
                        ins.then_inc(sems[semkey[i]], 16 if o["dma"] else 1)
                if eng_name == "sync":
                    for sk, v in finals.items():
                        if v:
                            eng.wait_ge(sems[sk], v)
            return body

        block.sync(body_for("sync"))
        block.tensor(body_for("tensor"))
        block.vector(body_for("vector"))
        block.scalar(body_for("scalar"))
        block.gpsimd(body_for("gpsimd"))
        return finals

    @staticmethod
    def make_sems(nc, stack):
        sems = {}
        for e in Prog.ENGINES:
            sems[e] = stack.enter_context(nc.semaphore(f"s_{e}"))
        for q in ("sync", "gpsimd", "scalar"):
            for i in range(Prog.NDMA):
                sems[f"{q}_d{i}"] = stack.enter_context(
                    nc.semaphore(f"s_{q}_d{i}"))
        return sems


def k_pb(i):
    return f"pb{i}"


def emit_row_bcast_hilo(p, E, row_key, row_ap, width, banks, out_sb, out_key,
                        tmp_rows):
    hi, hi_k, lo, lo_k = tmp_rows
    p.op("vector", lambda e: e.tensor_copy(hi[0:1, 0:width], row_ap),
         reads=[row_key], writes=[hi_k])
    p.op("vector", lambda e: e.tensor_tensor(lo[0:1, 0:width], row_ap,
                                             hi[0:1, 0:width], ALU.subtract),
         reads=[row_key, hi_k], writes=[lo_k])
    nsl = (width + 511) // 512
    for i in range(nsl):
        c0, c1 = i * 512, min(width, (i + 1) * 512)
        pb, pbk = banks[i % len(banks)]
        p.op("tensor", lambda e, c0=c0, c1=c1, pb=pb: e.matmul(
            pb[:, 0:c1 - c0], E["ones_row_b"][:, :], hi[0:1, c0:c1],
            start=True, stop=False), reads=[hi_k], writes=[pbk])
        p.op("tensor", lambda e, c0=c0, c1=c1, pb=pb: e.matmul(
            pb[:, 0:c1 - c0], E["ones_row_b"][:, :], lo[0:1, c0:c1],
            start=False, stop=True), reads=[lo_k], writes=[pbk])
        p.op("scalar", lambda e, c0=c0, c1=c1, pb=pb: e.copy(
            out_sb[:, c0:c1], pb[:, 0:c1 - c0]),
            reads=[pbk], writes=[out_key])


EPS_COL = {1e-5: 0, 1e-12: 1, 1.1920929e-07: 2}


def emit_rstd_row(p, out_ap, in_ap, rw_key, eps, scale=1.0, mult4=False, E=None):
    ec = E["epsc"]
    j = EPS_COL[eps]
    p.op("scalar", lambda e: e.activation(out_ap, in_ap, AF.Ln,
                                          bias=ec[0:1, j:j + 1],
                                          scale=float(scale)),
         reads=[rw_key, "c_eps_sb"], writes=[rw_key])
    if mult4:
        p.op("scalar", lambda e: e.activation(out_ap, out_ap, AF.Exp,
                                              bias=ec[0:1, 3:4], scale=-0.5),
             reads=[rw_key, "c_eps_sb"], writes=[rw_key])
    else:
        p.op("scalar", lambda e: e.activation(out_ap, out_ap, AF.Exp,
                                              bias=0.0, scale=-0.5),
             reads=[rw_key], writes=[rw_key])


class StatRows:

    def __init__(self, p, E, name, width, banks, step=1024):
        self.p, self.E, self.width = p, E, width
        self.step = step
        self.banks = banks
        self.sum_sb = E[f"{name}_sum"]
        self.sum_k = f"{name}_sum"
        self.ssq_sb = E[f"{name}_ssq"]
        self.ssq_k = f"{name}_ssq"
        self.first = True
        self.bi = 0

    def add_chunk(self, src_ap, src_key, sq_tmp, sq_key, col0=0):
        p, E = self.p, self.E
        w = self.widthof(src_ap)
        for b0 in range(0, w, self.step):
            b1 = min(w, b0 + self.step)
            p.op("vector", lambda e, b0=b0, b1=b1: e.tensor_tensor(
                sq_tmp[:, 0:b1 - b0], src_ap[:, b0:b1], src_ap[:, b0:b1],
                ALU.mult), reads=[src_key], writes=[sq_key])
            self._mm_part(src_ap, src_key, sq_tmp, sq_key, col0, b0, b1)
        return

    def _mm_part(self, src_ap, src_key, sq_tmp, sq_key, col0, b0, b1):
        p, E = self.p, self.E
        for i in range((b1 - b0 + 511) // 512):
            a0 = b0 + i * 512
            a1 = min(b1, a0 + 512)
            pb, pbk = self.banks[self.bi % len(self.banks)]
            self.bi += 1
            pb2, pbk2 = self.banks[self.bi % len(self.banks)]
            self.bi += 1
            p.op("tensor", lambda e, a0=a0, a1=a1, pb=pb: e.matmul(
                pb[0:1, 0:a1 - a0], E["ones_col_b"][:, :],
                src_ap[:, a0:a1], start=True, stop=True),
                reads=[src_key], writes=[pbk])
            p.op("tensor", lambda e, a0=a0, a1=a1, pb2=pb2, b0=b0: e.matmul(
                pb2[0:1, 0:a1 - a0], E["ones_col_b"][:, :],
                sq_tmp[:, a0 - b0:a1 - b0], start=True, stop=True),
                reads=[sq_key], writes=[pbk2])
            op = ALU.bypass if self.first else ALU.add
            c0, c1 = col0 + a0, col0 + a1
            if self.first:
                p.op("vector", lambda e, c0=c0, c1=c1, pb=pb: e.tensor_copy(
                    self.sum_sb[0:1, c0:c1], pb[0:1, 0:c1 - c0]),
                    reads=[pbk], writes=[self.sum_k])
                p.op("vector", lambda e, c0=c0, c1=c1, pb2=pb2: e.tensor_copy(
                    self.ssq_sb[0:1, c0:c1], pb2[0:1, 0:c1 - c0]),
                    reads=[pbk2], writes=[self.ssq_k])
            else:
                p.op("vector", lambda e, c0=c0, c1=c1, pb=pb: e.tensor_tensor(
                    self.sum_sb[0:1, c0:c1], self.sum_sb[0:1, c0:c1],
                    pb[0:1, 0:c1 - c0], ALU.add),
                    reads=[pbk, self.sum_k], writes=[self.sum_k])
                p.op("vector", lambda e, c0=c0, c1=c1, pb2=pb2: e.tensor_tensor(
                    self.ssq_sb[0:1, c0:c1], self.ssq_sb[0:1, c0:c1],
                    pb2[0:1, 0:c1 - c0], ALU.add),
                    reads=[pbk2, self.ssq_k], writes=[self.ssq_k])

    @staticmethod
    def widthof(ap):
        return ap.ap[-1][1]

    def chunk_done(self):
        self.first = False

    def finalize(self, denom, eps):
        p, E, w = self.p, self.E, self.width
        m_row, m_key = self.sum_sb, self.sum_k
        r_row, r_key = self.ssq_sb, self.ssq_k
        p.op("vector", lambda e: e.tensor_scalar(
            m_row[0:1, 0:w], m_row[0:1, 0:w], 1.0 / denom, None, ALU.mult),
            reads=[m_key], writes=[m_key])
        p.op("vector", lambda e: e.tensor_scalar(
            r_row[0:1, 0:w], r_row[0:1, 0:w], 1.0 / denom, None,
            ALU.mult), reads=[r_key], writes=[r_key])
        p.op("vector", lambda e: e.tensor_tensor(
            E["row_a"][0:1, 0:w], m_row[0:1, 0:w],
            m_row[0:1, 0:w], ALU.mult), reads=[m_key], writes=["row_a"])
        p.op("vector", lambda e: e.tensor_tensor(
            r_row[0:1, 0:w], r_row[0:1, 0:w],
            E["row_a"][0:1, 0:w], ALU.subtract),
            reads=[r_key, "row_a"], writes=[r_key])
        emit_rstd_row(p, r_row[0:1, 0:w], r_row[0:1, 0:w], r_key, eps, E=E)


def make_env(nc, stack, need):
    ent = stack.enter_context
    E = {}

    def sb(name, shape, dt):
        E[name] = ent(nc.sbuf_tensor(_uname(name), shape, dt))
        return E[name]

    E["pb"] = []
    for i in range(8):
        t = ent(nc.psum_tensor(f"pb{i}", [128, 512], F32))
        E["pb"].append(t)

    sb("ones_col_b", [128, 1], BF16)
    sb("ones_row_b", [1, 128], BF16)
    sb("ones_row_f", [1, 128], F32)
    sb("one_f", [1, 1], F32)
    if need.get("sel"):
        sb("sel", [128, 2], BF16)
        sb("selT", [2, 128], BF16)
    if need.get("mask"):
        sb("mask", [128, 128], F32)

    if need.get("X", True):
        sb("X", [128, NC_ * T], BF16)
    for r in ("row_a", "st_sum", "st_ssq"):
        sb(r, [1, T], F32)

    if need.get("row_b", True):
        sb("row_b", [1, 512], F32)
    rw = need.get("rowb_w", T)
    sb("rowb_hi", [1, rw], BF16)
    sb("rowb_lo", [1, rw], BF16)
    sb("colA", [128, 16], F32)
    sb("colB", [128, 16], F32)
    sb("epsc", [128, 8], F32)
    return E


def load_consts(p, E, dr, need):
    keys = []

    def ld(dst, name):
        p.op("sync", lambda e: e.dma_start(out=dst[:], in_=dr[name].ap()),
             writes=[name + "_sb"], dma=True)
        keys.append(name + "_sb")

    ld(E["ones_col_b"], "c_ones_col")
    ld(E["ones_row_b"], "c_ones_row_b")
    ld(E["ones_row_f"], "c_ones_row_f")
    ld(E["one_f"], "c_one")
    ld(E["epsc"], "c_eps")
    if need.get("sel"):
        ld(E["sel"], "c_sel")
        ld(E["selT"], "c_selT")
    if need.get("mask"):
        ld(E["mask"], "c_mask")
    for eng in ("tensor", "vector", "scalar", "gpsimd"):
        p.op(eng, lambda e: e.nop(), reads=list(keys))


def declare_const_drams(nc, need):
    dr = {}
    dr["c_ones_col"] = nc.dram_tensor("c_ones_col", [128, 1], BF16, kind="ExternalInput")
    dr["c_ones_row_b"] = nc.dram_tensor("c_ones_row_b", [1, 128], BF16, kind="ExternalInput")
    dr["c_ones_row_f"] = nc.dram_tensor("c_ones_row_f", [1, 128], F32, kind="ExternalInput")
    dr["c_one"] = nc.dram_tensor("c_one", [1, 1], F32, kind="ExternalInput")
    dr["c_eps"] = nc.dram_tensor("c_eps", [128, 8], F32, kind="ExternalInput")
    if need.get("sel"):
        dr["c_sel"] = nc.dram_tensor("c_sel", [128, 2], BF16, kind="ExternalInput")
        dr["c_selT"] = nc.dram_tensor("c_selT", [2, 128], BF16, kind="ExternalInput")
    if need.get("mask"):
        dr["c_mask"] = nc.dram_tensor("c_mask", [128, 128], F32, kind="ExternalInput")
    return dr


def const_arrays(need):
    c = {
        "c_ones_col": np.ones((128, 1), BF),
        "c_ones_row_b": np.ones((1, 128), BF),
        "c_ones_row_f": np.ones((1, 128), np.float32),
        "c_one": np.ones((1, 1), np.float32),
        "c_eps": np.tile(np.array([[1e-5, 1e-12, 1.1920929e-07, 1.3862943611198906,
                                    0.0, -2.4260151319598084, 0.0, 0.0]], np.float32), (128, 1)),
    }
    if need.get("sel"):
        sel = np.zeros((128, 2), BF)
        sel[0:64, 0] = 1
        sel[64:128, 1] = 1
        c["c_sel"] = sel
        c["c_selT"] = np.ascontiguousarray(sel.T)
    if need.get("mask"):
        q = np.arange(128)
        c["c_mask"] = np.where(q[None, :] >= q[:, None], 0.0, -1e9).astype(np.float32)
    return c


def emit_load_stream(p, E, x_dram):
    for dc in range(NC_):
        p.op("gpsimd", lambda e, dc=dc: e.dma_start(
            out=E["X"][:, dc * T:(dc + 1) * T],
            in_=x_dram.ap()[dc * 128:(dc + 1) * 128, :]),
            writes=[f"X{dc}"], dma=True)


def emit_store_stream_fp32(p, E, y_dram, src_name, src, scr, scr_k):
    pass



def emit_sumsq_row(p, E, src_sb, src_key, row_sb, row_key, pb, pbk, step=1024):
    for b0 in range(0, T, step):
        p.op("vector", lambda e, b0=b0, step=step: e.tensor_tensor(
            E["sq2"][:, 0:step], src_sb[:, b0:b0 + step],
            src_sb[:, b0:b0 + step], ALU.mult),
            reads=[src_key], writes=["sq2"])
        for i in range(step // 512):
            a0 = b0 + i * 512
            p.op("tensor", lambda e, a0=a0, b0=b0: e.matmul(
                pb[0:1, :], E["ones_col_b"][:, :],
                E["sq2"][:, a0 - b0:a0 - b0 + 512], start=True, stop=True),
                reads=["sq2"], writes=[pbk])
            p.op("vector", lambda e, a0=a0: e.tensor_copy(
                row_sb[0:1, a0:a0 + 512], pb[0:1, :]),
                reads=[pbk], writes=[row_key])

def build_block_a(nc, p, E, dr, out_mode, qe=None):
    pb = E["pb"]
    PB = [(pb[i], k_pb(i)) for i in range(8)]

    Xk = [f"X{dc}" for dc in range(NC_)]

    st = StatRows(p, E, "st", T, [PB[0], PB[1], PB[2], PB[3]])
    for dc in range(NC_):
        st.add_chunk(E["X"][:, dc * T:(dc + 1) * T], Xk[dc], E["sq2"][:, 0:1024], "sq2")
        st.chunk_done()
    st.finalize(D, 1e-5)
    p.op("vector", lambda e: e.tensor_copy(E["row_mb"][0:1, :], E["st_sum"][0:1, :]),
         reads=["st_sum"], writes=["row_mb"])
    for j in range(NC_):
        p.op("tensor", lambda e, j=j: e.matmul(
            pb[4][:, j:j + 1], E["st_ssq"][0:1, j * 128:(j + 1) * 128],
            E["one_f"][0:1, 0:1], start=True, stop=True),
            reads=["st_ssq"], writes=[k_pb(4)])
    p.op("scalar", lambda e: e.copy(E["colA"][:, 0:16], pb[4][:, 0:16]),
         reads=[k_pb(4)], writes=["colA"])
    if qe:
        qe.pump(16)

    p.op("gpsimd", lambda e: e.dma_start(out=E["wkv"][:], in_=dr["wkv_p"].ap()),
         writes=["wkv"], dma=True)
    for kv in range(2):
        p.op("vector", lambda e, kv=kv: e.tensor_tensor(
            E["wkv"][:, kv * 2048:(kv + 1) * 2048].rearrange("p (c n) -> p c n", n=128),
            E["wkv"][:, kv * 2048:(kv + 1) * 2048].rearrange("p (c n) -> p c n", n=128),
            E["gnT"][:, :, None].broadcast_to([128, 16, 128]), ALU.mult),
            reads=["wkv", "gnT_sb"], writes=["wkv"])
    for kv in range(2):
        for dc in range(NC_):
            p.op("tensor", lambda e, kv=kv, dc=dc: e.matmul(
                pb[5][0:1, 0:128], E["ones_col_b"][:, :],
                E["wkv"][:, kv * 2048 + dc * 128: kv * 2048 + (dc + 1) * 128],
                start=(dc == 0), stop=(dc == NC_ - 1)),
                reads=["wkv"], writes=[k_pb(5)])
        p.op("vector", lambda e, kv=kv: e.tensor_scalar(
            E["nbias"][0:1, kv * 128:(kv + 1) * 128], pb[5][0:1, 0:128],
            -1.0, None, ALU.mult), reads=[k_pb(5)], writes=["nbias"])

    for tc in range(4):
        for dc in range(NC_):
            p.op("tensor", lambda e, tc=tc, dc=dc: e.matmul(
                pb[tc][:, :], E["wkv"][:, dc * 128:(dc + 1) * 128],
                E["X"][:, dc * T + tc * 512: dc * T + tc * 512 + 512],
                start=(dc == 0), stop=False),
                reads=["wkv", Xk[dc]], writes=[k_pb(tc)])
        p.op("tensor", lambda e, tc=tc: e.matmul(
            pb[tc][:, :], E["nbias"][0:1, 0:128],
            E["row_mb"][0:1, tc * 512:tc * 512 + 512],
            start=False, stop=True),
            reads=["nbias", "row_mb"], writes=[k_pb(tc)])
        p.op("scalar", lambda e, tc=tc: e.copy(
            E["k_sb"][:, tc * 512:tc * 512 + 512], pb[tc][:, :]),
            reads=[k_pb(tc)], writes=["k_sb"])
        if qe:
            qe.pump(6)
    emit_sumsq_row(p, E, E["k_sb"], "k_sb", E["row_a"], "row_a", pb[4], k_pb(4))
    emit_rstd_row(p, E["row_a"][0:1, :], E["row_a"][0:1, :], "row_a", 1e-12,
                  mult4=True, E=E)
    for j in range(NC_):
        p.op("tensor", lambda e, j=j: e.matmul(
            pb[4][:, j:j + 1], E["row_a"][0:1, j * 128:(j + 1) * 128],
            E["one_f"][0:1, 0:1], start=True, stop=True),
            reads=["row_a"], writes=[k_pb(4)])
    p.op("scalar", lambda e: e.copy(E["colB"][:, 0:16], pb[4][:, 0:16]),
         reads=[k_pb(4)], writes=["colB"])

    for tt in range(NC_):
        if qe:
            qe.pump(2)
        for dc in range(NC_):
            p.op("tensor", lambda e, tt=tt, dc=dc: e.matmul(
                pb[4][:, 0:128], E["X"][:, dc * T + tt * 128: dc * T + (tt + 1) * 128],
                E["wkv"][:, 2048 + dc * 128: 2048 + (dc + 1) * 128],
                start=(dc == 0), stop=False),
                reads=["wkv", Xk[dc]], writes=[k_pb(4)])
        p.op("tensor", lambda e, tt=tt: e.matmul(
            pb[4][:, 0:128], E["row_mb"][0:1, tt * 128:(tt + 1) * 128],
            E["nbias"][0:1, 128:256], start=False, stop=True),
            reads=["row_mb", "nbias"], writes=[k_pb(4)])
        p.op("vector", lambda e, tt=tt: e.tensor_scalar(
            E["v_sb"][:, tt * 128:(tt + 1) * 128], pb[4][:, 0:128],
            E["colA"][:, tt:tt + 1], None, ALU.mult),
            reads=[k_pb(4), "colA"], writes=["v_sb"])

    wqbuf = [E["wkv"][:, 0:2048], E["wkv"][:, 2048:4096]]
    for h in range(NH):
        if qe and h == 12:
            qe.finish()
            p.barrier(reads=[t[2] for t in qe.todo] +
                      [f"qw{s}" for s in range(NSQ)],
                      writes=[f"ctx{hh}" for hh in range(12, 16)])
        wbuf = wqbuf[h % 2]
        wk = f"wq{h % 2}"
        wr = [wk, "wkv"] if h < 2 else [wk]
        p.op("gpsimd", lambda e, h=h, wbuf=wbuf: e.dma_start(
            out=wbuf, in_=dr["wq_p"].ap()[h]), writes=wr, dma=True)
        p.op("vector", lambda e, wbuf=wbuf: e.tensor_tensor(
            wbuf.rearrange("p (c n) -> p c n", n=128),
            wbuf.rearrange("p (c n) -> p c n", n=128),
            E["gnT"][:, :, None].broadcast_to([128, 16, 128]), ALU.mult),
            reads=[wk, "gnT_sb"], writes=[wk])
        for dc in range(NC_):
            p.op("tensor", lambda e, dc=dc, wbuf=wbuf: e.matmul(
                pb[5][0:1, 0:128], E["ones_col_b"][:, :],
                wbuf[:, dc * 128:(dc + 1) * 128],
                start=(dc == 0), stop=(dc == NC_ - 1)),
                reads=[wk], writes=[k_pb(5)])
        p.op("vector", lambda e: e.tensor_scalar(
            E["nbias"][0:1, 256:384], pb[5][0:1, 0:128], -1.0, None, ALU.mult),
            reads=[k_pb(5)], writes=["nbias_q"])
        for tc in range(4):
            for dc in range(NC_):
                p.op("tensor", lambda e, tc=tc, dc=dc, wbuf=wbuf: e.matmul(
                    pb[tc][:, :], wbuf[:, dc * 128:(dc + 1) * 128],
                    E["X"][:, dc * T + tc * 512: dc * T + tc * 512 + 512],
                    start=(dc == 0), stop=False),
                    reads=[wk, Xk[dc]], writes=[k_pb(tc)])
            p.op("tensor", lambda e, tc=tc: e.matmul(
                pb[tc][:, :], E["nbias"][0:1, 256:384],
                E["row_mb"][0:1, tc * 512:tc * 512 + 512],
                start=False, stop=True),
                reads=["nbias_q", "row_mb"], writes=[k_pb(tc)])
            p.op("scalar", lambda e, tc=tc: e.copy(
                E["q_sb"][:, tc * 512:tc * 512 + 512], pb[tc][:, :]),
                reads=[k_pb(tc)], writes=["q_sb"])
            if qe:
                qe.pump(2)
        emit_sumsq_row(p, E, E["q_sb"], "q_sb", E["row_a"], "row_a", pb[4], k_pb(4))
        emit_rstd_row(p, E["row_a"][0:1, :], E["row_a"][0:1, :], "row_a", 1e-12,
                      mult4=True, E=E)
        for b0 in range(0, T, 1024):
            emit_row_bcast_hilo(p, E, "row_a", E["row_a"][0:1, b0:b0 + 1024],
                                1024, [PB[0], PB[1]], E["f32a"], "f32a",
                                (E["rowb_hi"], "rowb_hi", E["rowb_lo"], "rowb_lo"))
            p.op("vector", lambda e, b0=b0: e.tensor_tensor(
                E["q_sb"][:, b0:b0 + 1024], E["q_sb"][:, b0:b0 + 1024],
                E["f32a"][:, 0:1024], ALU.mult),
                reads=["q_sb", "f32a"], writes=["q_sb"])

        for qs in range(4):
            q0 = qs * 512
            ctx_pb, ctx_k = PB[4]
            den_pb, den_k = PB[5]

            def emit_S(kt, q0=q0):
                s_pb, s_k = PB[2 + (kt % 2)]
                p.op("tensor", lambda e, kt=kt, q0=q0, s_pb=s_pb: e.matmul(
                    s_pb[:, :], E["k_sb"][:, kt * 128:(kt + 1) * 128],
                    E["q_sb"][:, q0:q0 + 512], start=True, stop=True),
                    reads=["k_sb", "q_sb"], writes=[s_k])

            emit_S(0)
            for kt in range(NC_):
                s_pb, s_k = PB[2 + (kt % 2)]
                eb = E[f"e_sb{kt % 2}"]
                ek = f"e_sb{kt % 2}"
                p.op("scalar", lambda e, kt=kt, s_pb=s_pb, eb=eb: e.activation(
                    eb[:, 0:512], s_pb[:, :], AF.Exp,
                    bias=0.0, scale=E["colB"][:, kt:kt + 1]),
                    reads=[s_k, "colB"], writes=[ek])
                if kt + 1 < NC_:
                    emit_S(kt + 1)
                if qe:
                    qe.pump(1)
                p.op("tensor", lambda e, kt=kt, ctx_pb=ctx_pb, eb=eb: e.matmul(
                    ctx_pb[:, :], E["v_sb"][:, kt * 128:(kt + 1) * 128],
                    eb[:, 0:512], start=(kt == 0), stop=(kt == NC_ - 1)),
                    reads=["v_sb", ek], writes=[ctx_k])
                p.op("tensor", lambda e, kt=kt, den_pb=den_pb, eb=eb: e.matmul(
                    den_pb[0:1, :], E["ones_col_b"][:, :],
                    eb[:, 0:512], start=(kt == 0), stop=(kt == NC_ - 1)),
                    reads=[ek], writes=[den_k])
            p.op("vector", lambda e: e.reciprocal(E["row_b"][0:1, 0:512],
                                                  den_pb[0:1, :]),
                 reads=[den_k], writes=["row_b"])
            emit_row_bcast_hilo(p, E, "row_b", E["row_b"][0:1, 0:512], 512,
                                [PB[2]], E["f32b"][:, 0:512], "f32b",
                                (E["rowb_hi"], "rowb_hi", E["rowb_lo"], "rowb_lo"))
            p.op("vector", lambda e, h=h, q0=q0, ctx_pb=ctx_pb: e.tensor_tensor(
                E["ctx"][:, h * T + q0: h * T + q0 + 512], ctx_pb[:, :],
                E["f32b"][:, 0:512], ALU.mult),
                reads=[ctx_k, "f32b"], writes=[f"ctx{h}"])

    st1 = StatRows(p, E, "st", T, [PB[4], PB[5]])
    for do in range(NC_):
        if qe:
            qe.pump(2)
        wbuf = wqbuf[do % 2]
        wk = f"wq{do % 2}"
        p.op("gpsimd", lambda e, do=do, wbuf=wbuf: e.dma_start(
            out=wbuf, in_=dr["wout_p"].ap()[do]), writes=[wk], dma=True)
        for tc in range(4):
            for hc in range(NC_):
                p.op("tensor", lambda e, tc=tc, hc=hc, wbuf=wbuf: e.matmul(
                    pb[tc % 4][:, :], wbuf[:, hc * 128:(hc + 1) * 128],
                    E["ctx"][:, hc * T + tc * 512: hc * T + tc * 512 + 512],
                    start=(hc == 0), stop=(hc == NC_ - 1)),
                    reads=[wk, f"ctx{hc}"], writes=[k_pb(tc % 4)])
            p.op("scalar", lambda e, tc=tc: e.copy(
                E["q_sb"][:, tc * 512:tc * 512 + 512], pb[tc % 4][:, :]),
                reads=[k_pb(tc % 4)] + (["q_sb"] if do == 0 and tc == 0 else []),
                writes=["o_buf"])
        p.op("sync", lambda e, do=do: e.dma_start(
            out=dr["hb_out"].ap()[do], in_=E["q_sb"][:, :]),
            reads=["o_buf"], writes=[f"hb_out{do}"], dma=True)
        st1.add_chunk(E["q_sb"][:, 0:T], "o_buf", E["sq2"][:, 0:1024], "sq2")
        st1.chunk_done()
    st1.finalize(D, 1e-5)

    st2 = StatRows(p, E, "st", T, [PB[4], PB[5]])
    for b0 in (0, 1024):
        st2.first = True
        emit_row_bcast_hilo(p, E, "st_sum", E["st_sum"][0:1, b0:b0 + 1024], 1024,
                            [PB[0], PB[1]], E["f32a"], "f32a",
                            (E["rowb_hi"], "rowb_hi", E["rowb_lo"], "rowb_lo"))
        emit_row_bcast_hilo(p, E, "st_ssq", E["st_ssq"][0:1, b0:b0 + 1024], 1024,
                            [PB[2], PB[3]], E["f32b"], "f32b",
                            (E["rowb_hi"], "rowb_hi", E["rowb_lo"], "rowb_lo"))
        for do in range(NC_):
            ob = wqbuf[do % 2]
            obk = f"wq{do % 2}"
            p.op("sync", lambda e, do=do, ob=ob, b0=b0: e.dma_start(
                out=ob[:, 0:1024], in_=dr["hb_out"].ap()[do][:, b0:b0 + 1024]),
                reads=[f"hb_out{do}"], writes=[obk], dma=True)
            p.op("vector", lambda e, ob=ob: e.tensor_tensor(
                E["scr"][:, 0:1024], ob[:, 0:1024], E["f32a"][:, 0:1024],
                ALU.subtract), reads=[obk, "f32a"], writes=["scr"])
            p.op("vector", lambda e, do=do: e.tensor_scalar(
                E["scr"][:, 0:1024], E["scr"][:, 0:1024],
                E["goT"][:, do:do + 1], None, ALU.mult),
                reads=["scr", "goT_sb"], writes=["scr"])
            p.op("vector", lambda e: e.tensor_tensor(
                E["scr"][:, 0:1024], E["scr"][:, 0:1024], E["f32b"][:, 0:1024],
                ALU.mult), reads=["scr", "f32b"], writes=["scr"])
            p.op("vector", lambda e, do=do, b0=b0: e.tensor_tensor(
                E["ctx"][:, do * T + b0: do * T + b0 + 1024], E["scr"][:, 0:1024],
                E["X"][:, do * T + b0: do * T + b0 + 1024], ALU.add),
                reads=["scr", f"X{do}"], writes=[f"ctx{do}"])
            st2.add_chunk(E["ctx"][:, do * T + b0: do * T + b0 + 1024],
                          f"ctx{do}", E["sq2"][:, 0:1024], "sq2", col0=b0)
            if do == 0:
                st2.first = False
    st2.finalize(D, 1e-5)

    for b0 in (0, 1024):
        emit_row_bcast_hilo(p, E, "st_sum", E["st_sum"][0:1, b0:b0 + 1024], 1024,
                            [PB[0], PB[1]], E["f32a"], "f32a",
                            (E["rowb_hi"], "rowb_hi", E["rowb_lo"], "rowb_lo"))
        emit_row_bcast_hilo(p, E, "st_ssq", E["st_ssq"][0:1, b0:b0 + 1024], 1024,
                            [PB[2], PB[3]], E["f32b"], "f32b",
                            (E["rowb_hi"], "rowb_hi", E["rowb_lo"], "rowb_lo"))
        for dc in range(NC_):
            p.op("vector", lambda e, dc=dc, b0=b0: e.tensor_tensor(
                E["scr"][:, 0:1024], E["ctx"][:, dc * T + b0: dc * T + b0 + 1024],
                E["f32a"][:, 0:1024], ALU.subtract),
                reads=[f"ctx{dc}", "f32a"], writes=["scr"])
            if out_mode == "hbm_fp32":
                p.op("vector", lambda e: e.tensor_tensor(
                    E["scr"][:, 0:1024], E["scr"][:, 0:1024],
                    E["f32b"][:, 0:1024], ALU.mult),
                    reads=["scr", "f32b"], writes=["scr"])
                p.op("sync", lambda e, dc=dc, b0=b0: e.dma_start(
                    out=dr["y"].ap()[dc * 128:(dc + 1) * 128, b0:b0 + 1024],
                    in_=E["scr"][:, 0:1024]),
                    reads=["scr"], writes=["y"], dma=True)
            else:
                p.op("vector", lambda e, dc=dc, b0=b0: e.tensor_tensor(
                    E["X"][:, dc * T + b0: dc * T + b0 + 1024], E["scr"][:, 0:1024],
                    E["f32b"][:, 0:1024], ALU.mult),
                    reads=["scr", "f32b"], writes=[f"X{dc}"])
    if qe:
        qe.finish()


def env_block_a(nc, stack, E, quant=False):
    ent = stack.enter_context

    def sb(name, shape, dt):
        E[name] = ent(nc.sbuf_tensor(_uname(name), shape, dt))

    sb("ctx", [128, NH * T], BF16)
    if quant:
        names = [f"q{t}{s}" for s in range(NSQ) for t in ("w", "h", "b")]
        base = 12 * T
        for i, nm in enumerate(names[:16]):
            E[nm] = E["ctx"][:, base + i * 512: base + (i + 1) * 512]
        for nm in names[16:]:
            sb(nm, [128, 512], BF16)
        for s in range(4):
            sb(f"qsc{s}", [2, 512], BF16)
    sb("k_sb", [128, T], BF16)
    sb("v_sb", [128, T], BF16)
    sb("q_sb", [128, T], BF16)
    sb("sq2", [128, 1024], BF16)
    sb("row_mb", [1, T], BF16)
    sb("wkv", [128, 2 * 2048], BF16)
    sb("e_sb0", [128, 512], BF16)
    sb("e_sb1", [128, 512], BF16)
    sb("f32a", [128, 1024], F32)
    sb("f32b", [128, 1024], F32)
    sb("scr", [128, 1024], F32)
    sb("nbias", [1, 512], BF16)
    sb("gnT_s", [128, 16], F32)
    sb("goT_s", [128, 16], F32)
    E["gnT"] = E["gnT_s"]
    E["goT"] = E["goT_s"]


def build_launch_a():
    from contextlib import ExitStack
    nc = bass.Bass("TRN2", target_bir_lowering=False)
    need = {}
    dr = declare_const_drams(nc, need)
    dr["x"] = nc.dram_tensor("x", [D, T], F32, kind="ExternalInput")
    dr["wq_p"] = nc.dram_tensor("wq_p", [NH, 128, 2048], F32, kind="ExternalInput")
    dr["wkv_p"] = nc.dram_tensor("wkv_p", [128, 2 * 16 * 128], F32, kind="ExternalInput")
    dr["wout_p"] = nc.dram_tensor("wout_p", [NC_, 128, 2048], F32, kind="ExternalInput")
    dr["gnT"] = nc.dram_tensor("gnT", [128, 16], F32, kind="ExternalInput")
    dr["goT"] = nc.dram_tensor("goT", [128, 16], F32, kind="ExternalInput")
    dr["hb_out"] = nc.dram_tensor("hb_out", [NC_, 128, T], BF16, kind="Internal")
    dr["y"] = nc.dram_tensor("y", [D, T], F32, kind="ExternalOutput")

    with ExitStack() as stack:
        E = make_env(nc, stack, need)
        env_block_a(nc, stack, E)
        sems = Prog.make_sems(nc, stack)
        block = stack.enter_context(nc.Block())
        p = Prog(nc)
        load_consts(p, E, dr, need)
        p.op("sync", lambda e: e.dma_start(out=E["gnT"][:], in_=dr["gnT"].ap()),
             writes=["gnT_sb"], dma=True)
        p.op("sync", lambda e: e.dma_start(out=E["goT"][:], in_=dr["goT"].ap()),
             writes=["goT_sb"], dma=True)
        emit_load_stream(p, E, dr["x"])
        build_block_a(nc, p, E, dr, out_mode="hbm_fp32")
        p.finalize(block, sems)
    return nc


def pack_a(inputs):
    Wq = np.asarray(inputs["L0_attn_Wq"])
    Wkv = np.asarray(inputs["L0_attn_Wkv"])
    Wout = np.asarray(inputs["L0_attn_Wout"])
    g = np.asarray(inputs["L0_attn_g_norm"])
    go = np.asarray(inputs["L0_attn_g_out"])
    wq_p = np.ascontiguousarray(
        Wq.reshape(16, 128, 16, 128).transpose(2, 1, 0, 3)).reshape(NH, 128, 2048)
    wkv_p = np.ascontiguousarray(
        Wkv.reshape(16, 128, 2, 128).transpose(1, 2, 0, 3)).reshape(128, 2 * 16 * 128)
    wout_p = np.ascontiguousarray(
        Wout.reshape(16, 128, 16, 128).transpose(2, 1, 0, 3)).reshape(NC_, 128, 2048)
    gnT = np.ascontiguousarray(g.reshape(16, 128).T).astype(np.float32)
    goT = np.ascontiguousarray(go.reshape(16, 128).T).astype(np.float32)
    return dict(wq_p=wq_p, wkv_p=wkv_p, wout_p=wout_p, gnT=gnT, goT=goT)





INV_M = float(1.0 / np.float64(M_THR))


NSQ = 6


def env_block_b(nc, stack, E, fused=False):
    ent = stack.enter_context

    def sb(name, shape, dt):
        E[name] = ent(nc.sbuf_tensor(_uname(name), shape, dt))

    sb("h_sb", [128, 64 * 512], BF16)
    if not fused:
        sb("xq0", [128, 16 * 512], BF16)
        sb("xq1", [128, 16 * 512], BF16)
    sb("whg0", [128, 2048], BF16)
    sb("whg1", [128, 2048], BF16)
    sb("whp0", [128, 2048], BF16)
    sb("whp1", [128, 2048], BF16)
    sb("s_f32", [128, 512], F32)
    sb("f32a", [128, 512], F32)
    sb("f32b", [128, 512], F32)
    sb("f32c", [128, 512], F32)
    sb("scr", [128, 512], F32)
    sb("sqd", [128, 512], BF16)
    if not fused:
        h = E["h_sb"]
        for s in range(NSQ):
            E[f"qw{s}"] = h[:, (s * 3 + 0) * 512:(s * 3 + 1) * 512]
            E[f"qh{s}"] = h[:, (s * 3 + 1) * 512:(s * 3 + 2) * 512]
            E[f"qb{s}"] = h[:, (s * 3 + 2) * 512:(s * 3 + 3) * 512]
        for s in range(4):
            E[f"qsc{s}"] = h[0:2, (NSQ * 3 + s) * 512:(NSQ * 3 + s + 1) * 512]


class QuantEmitter:

    def __init__(self, p, E, dr, banks2=False):
        self.p, self.E = p, E
        self.banks2 = banks2
        groups = []
        gu = dr["wgu_p"].ap()
        g2 = dr["sc_gu2"].ap()
        order = []
        for j in range(64):
            order += [j, 64 + j]
        for j2 in order:
            grp = [(gu[j2][:, h * 512:(h + 1) * 512],
                    g2[j2][:, h * 512:(h + 1) * 512]) for h in range(4)]
            groups.append((grp, (f"scg{j2}",)))
        wp = dr["wp_p"].ap()
        scp = dr["sc_p"].ap()
        for do in range(16):
            fw = wp[do].rearrange("p c o -> p (c o)")
            fs = scp[do].rearrange("p c o -> p (c o)")
            for g in range(4):
                c0 = g * 2048
                grp = [(fw[:, c0 + h * 512:c0 + (h + 1) * 512],
                        fs[:, c0 + h * 512:c0 + (h + 1) * 512])
                       for h in range(4)]
                groups.append((grp, (f"scp{do}_{g}",)))
        self.todo = []
        uid = 0
        for grp, gkeys in groups:
            ukeys = []
            for i, (src, dst) in enumerate(grp):
                ukey = f"uq{uid}"
                uid += 1
                ukeys.append(ukey)
                bar = (gkeys, list(ukeys)) if i == len(grp) - 1 else None
                self.todo.append((src, dst, ukey, bar))
        self.w = 0

    def _bufs(self, u):
        s = u % NSQ
        s2 = u % 4
        E = self.E
        return (E[f"qw{s}"], E[f"qh{s}"], E[f"qb{s}"], E[f"qsc{s2}"],
                f"qw{s}", f"qh{s}", f"qb{s}", f"qsc{s2}")

    def _wave(self, w):
        p, E = self.p, self.E
        pb = E["pb"]
        n = len(self.todo)
        if 0 <= w - 5 < n:
            u = w - 5
            src, dst, ukey, bar = self.todo[u]
            qw, qh, qb, qsc, kw, kh, kb, ksc = self._bufs(u)
            p.op("sync", lambda e, dst=dst, qh=qh: e.dma_start(
                out=dst, in_=qh[:, :]), reads=[kh], writes=[ukey],
                dma=True)
            if bar is not None:
                gkeys, ukeys = bar
                p.barrier(reads=ukeys, writes=list(gkeys))
        if 0 <= w - 4 < n:
            u = w - 4
            qw, qh, qb, qsc, kw, kh, kb, ksc = self._bufs(u)
            p.op("vector", lambda e, qb=qb, qh=qh:
                 e.scalar_tensor_tensor(
                     qh[:, :], qh[:, :], INV_M, qb[:, :],
                     ALU.mult, ALU.is_gt),
                 reads=[kh, kb], writes=[kh])
            p.op("gpsimd", lambda e, qw=qw, qb=qb: e.tensor_tensor(
                qw[:, :], qw[:, :], qb[:, :], ALU.mult),
                reads=[kw, kb], writes=[kw])
            p.op("vector", lambda e, qh=qh, qw=qw: e.tensor_tensor(
                qh[:, :], qh[:, :], qw[:, :], ALU.mult),
                reads=[kh, kw], writes=[kh])
        if 0 <= w - 3 < n:
            u = w - 3
            qw, qh, qb, qsc, kw, kh, kb, ksc = self._bufs(u)
            b = self.bank(u)
            p.op("scalar", lambda e, b=b, qb=qb: e.copy(
                qb[:, :], pb[b][:, :]), reads=[k_pb(b)], writes=[kb])
        if 0 <= w - 2 < n:
            u = w - 2
            qw, qh, qb, qsc, kw, kh, kb, ksc = self._bufs(u)
            b = self.bank(u)
            bk = k_pb(b)
            p.op("tensor", lambda e, b=b, qh=qh: e.matmul(
                pb[b][0:2, :], E["sel"][:, :], qh[:, :],
                start=True, stop=True), reads=[kh], writes=[bk])
            p.op("vector", lambda e, b=b, qsc=qsc: e.tensor_scalar(
                qsc[0:2, :], pb[b][0:2, :], 1.0 / 64.0, 1e-8,
                ALU.mult, ALU.max), reads=[bk], writes=[ksc])
            p.op("tensor", lambda e, b=b, qsc=qsc: e.matmul(
                pb[b][:, :], E["selT"][:, :], qsc[0:2, :],
                start=True, stop=True), reads=[ksc], writes=[bk])
        if 0 <= w - 1 < n:
            u = w - 1
            qw, qh, qb, qsc, kw, kh, kb, ksc = self._bufs(u)
            p.op("scalar", lambda e, qw=qw, qh=qh: e.activation(
                qh[:, :], qw[:, :], AF.Abs), reads=[kw], writes=[kh])
            p.op("scalar", lambda e, qw=qw: e.activation(
                qw[:, :], qw[:, :], AF.Sign), reads=[kw], writes=[kw])
        if w < len(self.todo):
            u = w
            src = self.todo[u][0]
            qw, qh, qb, qsc, kw, kh, kb, ksc = self._bufs(u)
            p.op("gpsimd", lambda e, src=src, qw=qw: e.dma_start(
                out=qw[:, :], in_=src), writes=[kw], dma=True)

    def bank(self, u):
        return 6 + u % 2 if self.banks2 else u % 8

    def pump(self, waves):
        n = len(self.todo) + 5
        for _ in range(waves):
            if self.w >= n:
                return
            self._wave(self.w)
            self.w += 1

    def finish(self):
        self.pump(len(self.todo) + 5)

    def emit_all(self):
        self.finish()


def build_block_b(nc, p, E, dr, out_mode, in_mode, quant=True):
    pb = E["pb"]
    PB = [(pb[i], k_pb(i)) for i in range(8)]

    if quant:
        qe = QuantEmitter(p, E, dr)
        qe.emit_all()
        p.barrier(reads=[t[2] for t in qe.todo] +
                  [f"qw{s}" for s in range(NSQ)],
                  writes=[f"h{j}" for j in range(64)])

    if in_mode == "hbm":
        xin = dr["x"].ap().rearrange("(c p) t -> p c t", p=128)
    st = StatRows(p, E, "st", 512, [PB[4], PB[5]], step=512)
    for q in range(4):
        if in_mode == "hbm":
            xq = E[f"xq{q % 2}"]
            xqk = f"xq{q % 2}"
            p.op("gpsimd", lambda e, q=q, xq=xq: e.dma_start(
                out=xq[:, :].rearrange("p (c t) -> p c t", t=512),
                in_=xin[:, :, q * 512:(q + 1) * 512]),
                writes=[xqk], dma=True)

            def xs(dc, q=q, xq=xq, xqk=xqk):
                return xq[:, dc * 512:(dc + 1) * 512], xqk
        else:
            def xs(dc, q=q):
                return (E["X"][:, dc * T + q * 512: dc * T + (q + 1) * 512],
                        f"X{dc}")
        for j in range(64):
            gb, gbk = pb[(j % 2) * 2], k_pb((j % 2) * 2)
            ub, ubk = pb[1 + (j % 2) * 2], k_pb(1 + (j % 2) * 2)
            assert f"scg{j}" in p.last_writer
            assert f"scg{64 + j}" in p.last_writer
            p.op("scalar", lambda e, j=j: e.dma_start(
                out=E["whg0"][:, :], in_=dr["sc_gu2"].ap()[j]),
                writes=["whg0"], dma=True, reads=[f"scg{j}"])
            p.op("scalar", lambda e, j=j: e.dma_start(
                out=E["whg1"][:, :], in_=dr["sc_gu2"].ap()[64 + j]),
                writes=["whg1"], dma=True, reads=[f"scg{64 + j}"])
            for dc in range(16):
                xa, xk = xs(dc)
                p.op("tensor", lambda e, dc=dc, gb=gb, xa=xa: e.matmul(
                    gb[:, :], E["whg0"][:, dc * 128:(dc + 1) * 128], xa,
                    start=(dc == 0), stop=(dc == 15)),
                    reads=["whg0", xk], writes=[gbk])
            for dc in range(16):
                xa, xk = xs(dc)
                p.op("tensor", lambda e, dc=dc, ub=ub, xa=xa: e.matmul(
                    ub[:, :], E["whg1"][:, dc * 128:(dc + 1) * 128], xa,
                    start=(dc == 0), stop=(dc == 15)),
                    reads=["whg1", xk], writes=[ubk])
            p.op("scalar", lambda e, gb=gb: e.activation(
                E["s_f32"][:, :], gb[:, :], AF.Silu), reads=[gbk], writes=["s_f32"])
            p.op("vector", lambda e, j=j, ub=ub: e.tensor_tensor(
                E["h_sb"][:, j * 512:(j + 1) * 512], E["s_f32"][:, :], ub[:, :],
                ALU.mult), reads=["s_f32", ubk], writes=[f"h{j}"])
            p.op("vector", lambda e, j=j: e.tensor_tensor(
                E["sqd"][:, 0:512], E["h_sb"][:, j * 512:(j + 1) * 512],
                E["h_sb"][:, j * 512:(j + 1) * 512], ALU.mult),
                reads=[f"h{j}"], writes=["sqd"])
            p.op("tensor", lambda e, j=j: e.matmul(
                pb[4][0:1, :], E["ones_col_b"][:, :], E["sqd"][:, 0:512],
                start=(j == 0), stop=(j == 63)), reads=["sqd"], writes=[k_pb(4)])
        p.op("vector", lambda e: e.tensor_copy(E["row_b"][0:1, :], pb[4][0:1, :]),
             reads=[k_pb(4)], writes=["row_b"])
        emit_rstd_row(p, E["row_b"][0:1, :], E["row_b"][0:1, :], "row_b",
                      1.1920929e-07, scale=1.0 / 8192.0, E=E)
        emit_row_bcast_hilo(p, E, "row_b", E["row_b"][0:1, :], 512,
                            [PB[5]], E["f32c"][:, :], "f32c",
                            (E["rowb_hi"], "rowb_hi", E["rowb_lo"], "rowb_lo"))
        st.first = True
        for do in range(16):
            db, dbk = pb[(do % 2) * 2], k_pb((do % 2) * 2)
            for grp in range(4):
                whp = E[f"whp{grp % 2}"]
                whpk = f"whp{grp % 2}"
                if q == 0:
                    assert f"scp{do}_{grp}" in p.last_writer
                p.op("gpsimd", lambda e, do=do, grp=grp, whp=whp: e.dma_start(
                    out=whp[:, :],
                    in_=dr["sc_p"].ap()[do].rearrange(
                        "p c o -> p (c o)")[:, grp * 2048:(grp + 1) * 2048]),
                    writes=[whpk], dma=True, reads=[f"scp{do}_{grp}"])
                for hc2 in range(16):
                    hc = grp * 16 + hc2
                    p.op("tensor", lambda e, hc=hc, hc2=hc2, db=db, whp=whp: e.matmul(
                        db[:, :], whp[:, hc2 * 128:(hc2 + 1) * 128],
                        E["h_sb"][:, hc * 512:(hc + 1) * 512],
                        start=(hc == 0), stop=(hc == 63)),
                        reads=[whpk, f"h{hc}"], writes=[dbk])
            xa, xk = xs(do)
            p.op("vector", lambda e, db=db: e.tensor_tensor(
                E["scr"][:, :], db[:, :], E["f32c"][:, :], ALU.mult),
                reads=[dbk, "f32c"], writes=["scr"])
            p.op("vector", lambda e, xa=xa: e.tensor_tensor(
                xa, E["scr"][:, :], xa, ALU.add),
                reads=["scr", xk], writes=[xk])
            st.add_chunk(xa, xk, E["sqd"][:, 0:512], "sqd")
            if do == 0:
                st.first = False
        st.finalize(D, 1e-5)
        emit_row_bcast_hilo(p, E, "st_sum", E["st_sum"][0:1, 0:512], 512,
                            [PB[4]], E["f32a"][:, :], "f32a",
                            (E["rowb_hi"], "rowb_hi", E["rowb_lo"], "rowb_lo"))
        emit_row_bcast_hilo(p, E, "st_ssq", E["st_ssq"][0:1, 0:512], 512,
                            [PB[5]], E["f32b"][:, :], "f32b",
                            (E["rowb_hi"], "rowb_hi", E["rowb_lo"], "rowb_lo"))
        for do in range(16):
            xa, xk = xs(do)
            p.op("vector", lambda e, xa=xa: e.tensor_tensor(
                E["scr"][:, :], xa, E["f32a"][:, :],
                ALU.subtract), reads=[xk, "f32a"], writes=["scr"])
            if out_mode == "hbm_fp32":
                p.op("vector", lambda e: e.tensor_tensor(
                    E["scr"][:, :], E["scr"][:, :], E["f32b"][:, :], ALU.mult),
                    reads=["scr", "f32b"], writes=["scr"])
                p.op("sync", lambda e, do=do, q=q: e.dma_start(
                    out=dr["y"].ap()[do * 128:(do + 1) * 128,
                                     q * 512:(q + 1) * 512],
                    in_=E["scr"][:, :]), reads=["scr"], writes=[f"y{do}_{q}"],
                    dma=True)
            else:
                p.op("vector", lambda e, do=do, q=q: e.tensor_tensor(
                    E["X"][:, do * T + q * 512: do * T + (q + 1) * 512],
                    E["scr"][:, :], E["f32b"][:, :], ALU.mult),
                    reads=["scr", "f32b"], writes=[f"X{do}"])


def build_launch_b():
    from contextlib import ExitStack
    nc = bass.Bass("TRN2", target_bir_lowering=False)
    need = {"sel": True, "X": False, "aug": False}
    dr = declare_const_drams(nc, need)
    dr["x"] = nc.dram_tensor("x", [D, T], F32, kind="ExternalInput")
    dr["wgu_p"] = nc.dram_tensor("wgu_p", [128, 128, 2048], F32, kind="ExternalInput")
    dr["wp_p"] = nc.dram_tensor("wp_p", [16, 128, 64, 128], F32, kind="ExternalInput")
    dr["sc_gu2"] = nc.dram_tensor("sc_gu2", [128, 128, 2048], BF16, kind="Internal")
    dr["sc_p"] = nc.dram_tensor("sc_p", [16, 128, 64, 128], BF16, kind="Internal")
    dr["y"] = nc.dram_tensor("y", [D, T], F32, kind="ExternalOutput")
    with ExitStack() as stack:
        E = make_env(nc, stack, need)
        env_block_b(nc, stack, E)
        sems = Prog.make_sems(nc, stack)
        block = stack.enter_context(nc.Block())
        p = Prog(nc)
        load_consts(p, E, dr, need)
        build_block_b(nc, p, E, dr, out_mode="hbm_fp32", in_mode="hbm")
        p.finalize(block, sems)
    return nc


def pack_b(inputs):
    wgu = np.asarray(inputs["L0_ffn_w_gate_up"])
    wpr = np.asarray(inputs["L0_ffn_w_proj"])
    wgu_p = np.ascontiguousarray(
        wgu.reshape(2, 64, 128, 16, 128).transpose(0, 1, 4, 3, 2)
    ).reshape(128, 128, 2048)
    wp_p = np.ascontiguousarray(
        wpr.reshape(16, 128, 64, 128).transpose(0, 3, 2, 1))
    return dict(wgu_p=wgu_p, wp_p=wp_p)


SC128 = 0.08838834764831845


def env_block_c(nc, stack, E):
    ent = stack.enter_context

    def sb(name, shape, dt):
        E[name] = ent(nc.sbuf_tensor(_uname(name), shape, dt))

    sb("ctx", [128, NH * T], BF16)
    sb("qk0", [128, 4096], BF16)
    sb("qk1", [128, 4096], BF16)
    sb("v4", [128, 4096], BF16)
    sb("q_sb", [128, T], BF16)
    sb("k_sb", [128, T], BF16)
    sb("e_sb0", [128, 512], BF16)
    sb("e_sb1", [128, 512], BF16)
    sb("bT_s", [128, 48], F32)
    E["bT"] = E["bT_s"]
    sb("bvr", [1, 512], BF16)
    sb("scs", [1, 8], F32)
    sb("f32a", [128, 512], F32)
    sb("f32b", [128, 512], F32)
    sb("f32c", [128, 512], F32)
    sb("sq2", [128, 512], BF16)
    E["negb"] = E["colA"]
    E["scr"] = E["f32c"]


def build_block_c(nc, p, E, dr, out_mode, in_mode):
    pb = E["pb"]
    PB = [(pb[i], k_pb(i)) for i in range(8)]
    Xk = [f"X{dc}" for dc in range(NC_)]
    if in_mode == "hbm":
        emit_load_stream(p, E, dr["x"])
    p.op("sync", lambda e: e.dma_start(out=E["bT"][:], in_=dr["bT"].ap()),
         writes=["bT"], dma=True)

    for hg in range(8):
        wv = E[f"qk{hg % 2}"][:, :]
        wvk = f"qk{hg % 2}"
        p.op("gpsimd", lambda e, hg=hg, wv=wv: e.dma_start(
            out=wv, in_=dr["wv_p"].ap()[hg]),
            writes=[wvk], dma=True)
        p.op("gpsimd", lambda e, hg=hg: e.dma_start(
            out=E["bvr"][0:1, 0:256], in_=dr["bv_p"].ap()[hg:hg + 1, :]),
            writes=["bvr"], dma=True)
        for tt in range(NC_):
            vb, vbk = PB[tt % 2]
            for dc in range(NC_):
                p.op("tensor", lambda e, tt=tt, dc=dc, vb=vb, wv=wv: e.matmul(
                    vb[:, 0:256], E["X"][:, dc * T + tt * 128: dc * T + (tt + 1) * 128],
                    wv[:, dc * 256:(dc + 1) * 256], start=(dc == 0), stop=False),
                    reads=[wvk, Xk[dc]], writes=[vbk])
            p.op("tensor", lambda e, vb=vb: e.matmul(
                vb[:, 0:256], E["ones_row_b"][:, :], E["bvr"][0:1, 0:256],
                start=False, stop=True), reads=["bvr"], writes=[vbk])
            p.op("scalar", lambda e, tt=tt, vb=vb: e.copy(
                E["v4"][:, tt * 256:(tt + 1) * 256], vb[:, 0:256]),
                reads=[vbk], writes=["v4"])

        for hh in range(2):
            h = hg * 2 + hh
            qk = E[f"qk{h % 2}"]
            qkk = f"qk{h % 2}"
            p.op("gpsimd", lambda e, h=h, qk=qk: e.dma_start(
                out=qk[:, :], in_=dr["wqk_p"].ap()[h].transpose([1, 0, 2])),
                writes=[qkk], dma=True)
            for kv in range(2):
                dst = E["q_sb"] if kv == 0 else E["k_sb"]
                dstk = "q_sb" if kv == 0 else "k_sb"
                for tc in range(4):
                    for dc in range(NC_):
                        p.op("tensor", lambda e, kv=kv, tc=tc, dc=dc, qk=qk: e.matmul(
                            pb[tc][:, :],
                            qk[:, kv * 2048 + dc * 128: kv * 2048 + (dc + 1) * 128],
                            E["X"][:, dc * T + tc * 512: dc * T + tc * 512 + 512],
                            start=(dc == 0), stop=(dc == NC_ - 1)),
                            reads=[qkk, Xk[dc]], writes=[k_pb(tc)])
                    p.op("vector", lambda e, kv=kv, tc=tc, h=h, dst=dst: e.tensor_scalar(
                        dst[:, tc * 512:tc * 512 + 512], pb[tc][:, :],
                        E["bT"][:, kv * 16 + h: kv * 16 + h + 1], None, ALU.add),
                        reads=[k_pb(tc), "bT"], writes=[dstk])
            emit_sumsq_row(p, E, E["k_sb"], "k_sb", E["row_a"], "row_a",
                           pb[4], k_pb(4), step=512)
            p.op("vector", lambda e: e.reduce_max(
                E["scs"][0:1, 0:1], E["row_a"][0:1, 0:T], axis=AX.X),
                reads=["row_a"], writes=["scs"])
            emit_sumsq_row(p, E, E["q_sb"], "q_sb", E["row_a"], "row_a",
                           pb[4], k_pb(4), step=512)
            p.op("vector", lambda e: e.reduce_max(
                E["scs"][0:1, 1:2], E["row_a"][0:1, 0:T], axis=AX.X),
                reads=["row_a"], writes=["scs"])
            p.op("vector", lambda e: e.tensor_tensor(
                E["scs"][0:1, 2:3], E["scs"][0:1, 0:1], E["scs"][0:1, 1:2],
                ALU.mult), reads=["scs"], writes=["scs"])
            p.op("scalar", lambda e: e.activation(
                E["scs"][0:1, 3:4], E["scs"][0:1, 2:3], AF.Ln, bias=0.0),
                reads=["scs"], writes=["scs"])
            p.op("scalar", lambda e: e.activation(
                E["scs"][0:1, 4:5], E["scs"][0:1, 3:4], AF.Exp,
                bias=E["epsc"][0:1, 5:6], scale=0.5),
                reads=["scs", "c_eps_sb"], writes=["scs"])
            p.op("vector", lambda e: e.tensor_scalar(
                E["scs"][0:1, 5:6], E["scs"][0:1, 4:5], -1.0, None, ALU.mult),
                reads=["scs"], writes=["scs"])
            p.op("tensor", lambda e: e.matmul(
                pb[7][:, 0:1], E["ones_row_f"][:, :], E["scs"][0:1, 5:6],
                start=True, stop=True), reads=["scs"], writes=[k_pb(7)])
            p.op("scalar", lambda e: e.copy(E["negb"][:, 0:1], pb[7][:, 0:1]),
                 reads=[k_pb(7)], writes=["negb"])

            for qs in range(4):
                nkt = 4 * qs + 4
                ctxb, ctxk = PB[2]
                denb, denk = PB[6]

                def emit_S(kt, qs=qs):
                    s_pb, s_k = PB[4 + (kt % 2)]
                    d = kt - 4 * qs
                    c0 = max(0, d * 128)
                    p.op("tensor", lambda e, kt=kt, qs=qs, c0=c0, s_pb=s_pb: e.matmul(
                        s_pb[:, c0:512], E["k_sb"][:, kt * 128:(kt + 1) * 128],
                        E["q_sb"][:, qs * 512 + c0: qs * 512 + 512],
                        start=True, stop=True),
                        reads=["k_sb", "q_sb"], writes=[s_k])
                    if d >= 0:
                        p.op("vector", lambda e, c0=c0, s_pb=s_pb: e.tensor_tensor(
                            s_pb[:, c0:c0 + 128], s_pb[:, c0:c0 + 128],
                            E["mask"][:, :], ALU.add),
                            reads=[s_k, "c_mask_sb"], writes=[s_k])

                kts = list(range(nkt))
                emit_S(kts[0])
                for idx, kt in enumerate(kts):
                    s_pb, s_k = PB[4 + (kt % 2)]
                    d = kt - 4 * qs
                    c0 = max(0, d * 128)
                    eb = E[f"e_sb{kt % 2}"]
                    ek = f"e_sb{kt % 2}"
                    first = idx == 0
                    last = idx == nkt - 1
                    p.op("scalar", lambda e, c0=c0, s_pb=s_pb, eb=eb: e.activation(
                        eb[:, c0:512], s_pb[:, c0:512], AF.Exp,
                        bias=E["negb"][:, 0:1], scale=SC128),
                        reads=[s_k, "negb"], writes=[ek])
                    if c0 > 0:
                        p.op("vector", lambda e, c0=c0, eb=eb: e.memset(
                            eb[:, 0:c0], 0.0), reads=[ek], writes=[ek])
                    if idx + 1 < nkt:
                        emit_S(kts[idx + 1])
                    p.op("tensor", lambda e, kt=kt, hh=hh, ctxb=ctxb,
                         eb=eb, first=first, last=last: e.matmul(
                        ctxb[:, 0:512],
                        E["v4"][:, kt * 256 + hh * 128: kt * 256 + (hh + 1) * 128],
                        eb[:, 0:512], start=first, stop=last),
                        reads=["v4", ek], writes=[ctxk])
                    p.op("tensor", lambda e, kt=kt, denb=denb, eb=eb,
                         first=first, last=last: e.matmul(
                        denb[0:1, 0:512], E["ones_col_b"][:, :],
                        eb[:, 0:512], start=first, stop=last),
                        reads=[ek], writes=[denk])
                p.op("vector", lambda e, denb=denb: e.reciprocal(
                    E["row_a"][0:1, 0:512], denb[0:1, :]),
                    reads=[denk], writes=["row_a"])
                emit_row_bcast_hilo(p, E, "row_a", E["row_a"][0:1, 0:512], 512,
                                    [PB[7]], E["f32c"][:, :], "f32c",
                                    (E["rowb_hi"], "rowb_hi", E["rowb_lo"], "rowb_lo"))
                p.op("vector", lambda e, h=h, qs=qs, ctxb=ctxb: e.tensor_tensor(
                    E["ctx"][:, h * T + qs * 512: h * T + (qs + 1) * 512],
                    ctxb[:, :], E["f32c"][:, :], ALU.mult),
                    reads=[ctxk, "f32c"], writes=[f"ctx{h}"])

    p.barrier(reads=["f32c"], writes=["scr"])
    st = StatRows(p, E, "st", T, [PB[6], PB[7]], step=512)
    for do in range(NC_):
        wo = E[f"qk{do % 2}"][:, 0:2048]
        wok = f"qk{do % 2}"
        p.op("gpsimd", lambda e, do=do, wo=wo: e.dma_start(
            out=wo, in_=dr["wo_p"].ap()[do]), writes=[wok], dma=True)
        for tc in range(4):
            for hc in range(NC_):
                p.op("tensor", lambda e, tc=tc, hc=hc, wo=wo: e.matmul(
                    pb[tc][:, :], wo[:, hc * 128:(hc + 1) * 128],
                    E["ctx"][:, hc * T + tc * 512: hc * T + tc * 512 + 512],
                    start=(hc == 0), stop=(hc == NC_ - 1)),
                    reads=[wok, f"ctx{hc}"], writes=[k_pb(tc)])
            p.op("vector", lambda e, do=do, tc=tc: e.tensor_tensor(
                E["X"][:, do * T + tc * 512: do * T + tc * 512 + 512],
                pb[tc][:, :],
                E["X"][:, do * T + tc * 512: do * T + tc * 512 + 512],
                ALU.add), reads=[k_pb(tc), Xk[do]], writes=[Xk[do]])
        st.add_chunk(E["X"][:, do * T:(do + 1) * T], Xk[do],
                     E["sq2"][:, 0:512], "sq2")
        st.chunk_done()
    st.finalize(D, 1e-5)
    for qtr in range(4):
        emit_row_bcast_hilo(p, E, "st_sum", E["st_sum"][0:1, qtr * 512:(qtr + 1) * 512],
                            512, [PB[6]], E["f32a"][:, :], "f32a",
                            (E["rowb_hi"], "rowb_hi", E["rowb_lo"], "rowb_lo"))
        emit_row_bcast_hilo(p, E, "st_ssq", E["st_ssq"][0:1, qtr * 512:(qtr + 1) * 512],
                            512, [PB[7]], E["f32b"][:, :], "f32b",
                            (E["rowb_hi"], "rowb_hi", E["rowb_lo"], "rowb_lo"))
        for dc in range(NC_):
            p.op("vector", lambda e, dc=dc, qtr=qtr: e.tensor_tensor(
                E["scr"][:, :], E["X"][:, dc * T + qtr * 512: dc * T + (qtr + 1) * 512],
                E["f32a"][:, :], ALU.subtract),
                reads=[Xk[dc], "f32a"], writes=["scr"])
            if out_mode == "hbm_fp32":
                p.op("vector", lambda e: e.tensor_tensor(
                    E["scr"][:, :], E["scr"][:, :], E["f32b"][:, :], ALU.mult),
                    reads=["scr", "f32b"], writes=["scr"])
                p.op("sync", lambda e, dc=dc, qtr=qtr: e.dma_start(
                    out=dr["y"].ap()[dc * 128:(dc + 1) * 128,
                                     qtr * 512:(qtr + 1) * 512],
                    in_=E["scr"][:, :]), reads=["scr"], writes=["y"], dma=True)
            else:
                p.op("vector", lambda e, dc=dc, qtr=qtr: e.tensor_tensor(
                    E["X"][:, dc * T + qtr * 512: dc * T + (qtr + 1) * 512],
                    E["scr"][:, :], E["f32b"][:, :], ALU.mult),
                    reads=["scr", "f32b"], writes=[Xk[dc]])


def build_launch_c():
    from contextlib import ExitStack
    nc = bass.Bass("TRN2", target_bir_lowering=False)
    need = {"mask": True, "aug": False, "rowb_w": 1024, "row_b": False}
    dr = declare_const_drams(nc, need)
    dr["x"] = nc.dram_tensor("x", [D, T], F32, kind="ExternalInput")
    dr["wqk_p"] = nc.dram_tensor("wqk_p", [NH, 2, 128, 2048], F32, kind="ExternalInput")
    dr["wv_p"] = nc.dram_tensor("wv_p", [8, 128, 4096], F32, kind="ExternalInput")
    dr["wo_p"] = nc.dram_tensor("wo_p", [NC_, 128, 2048], F32, kind="ExternalInput")
    dr["bT"] = nc.dram_tensor("bT", [128, 48], F32, kind="ExternalInput")
    dr["bv_p"] = nc.dram_tensor("bv_p", [8, 256], F32, kind="ExternalInput")
    dr["y"] = nc.dram_tensor("y", [D, T], F32, kind="ExternalOutput")
    with ExitStack() as stack:
        E = make_env(nc, stack, need)
        env_block_c(nc, stack, E)
        sems = Prog.make_sems(nc, stack)
        block = stack.enter_context(nc.Block())
        p = Prog(nc)
        load_consts(p, E, dr, need)
        build_block_c(nc, p, E, dr, out_mode="hbm_fp32", in_mode="hbm")
        p.finalize(block, sems)
    return nc


def pack_c(inputs):
    wqkv = np.asarray(inputs["L1_attn_w_qkv"])
    b = np.asarray(inputs["L1_attn_b_qkv"])
    wo = np.asarray(inputs["L1_attn_w_o"])
    wq = wqkv[:, 0:2048].reshape(16, 128, 16, 128)
    wk = wqkv[:, 2048:4096].reshape(16, 128, 16, 128)
    wqk_p = np.ascontiguousarray(np.stack(
        [wq.transpose(2, 1, 0, 3).reshape(16, 128, 2048),
         wk.transpose(2, 1, 0, 3).reshape(16, 128, 2048)], axis=1))
    wv_p = np.ascontiguousarray(
        wqkv[:, 4096:6144].reshape(16, 128, 8, 256).transpose(2, 1, 0, 3)
    ).reshape(8, 128, 4096)
    wo_p = np.ascontiguousarray(
        wo.reshape(16, 128, 16, 128).transpose(2, 1, 0, 3)).reshape(NC_, 128, 2048)
    bT = np.ascontiguousarray(
        b.reshape(3, 16, 128).transpose(2, 0, 1).reshape(128, 48)).astype(np.float32)
    bv_p = np.ascontiguousarray(b[4096:].reshape(8, 256)).astype(np.float32)
    return dict(wqk_p=wqk_p, wv_p=wv_p, wo_p=wo_p, bT=bT, bv_p=bv_p)


NI = 44


def env_block_d(nc, stack, E):
    ent = stack.enter_context

    def sb(name, shape, dt):
        E[name] = ent(nc.sbuf_tensor(_uname(name), shape, dt))

    sb("hd", [128, NI * 512], BF16)
    sb("wgu0", [128, 4096], BF16)
    sb("wgu1", [128, 4096], BF16)
    sb("wd0", [128, NI * 128], BF16)
    sb("wd1", [128, NI * 128], BF16)
    sb("s_f32", [128, 512], F32)
    sb("gsc_s", [128, NI], F32)
    sb("usc_s", [128, NI], F32)
    sb("dsc_s", [128, 16], F32)
    E["gsc"] = E["gsc_s"]
    E["usc"] = E["usc_s"]
    E["dsc"] = E["dsc_s"]
    sb("f32a", [128, 512], F32)
    sb("f32b", [128, 512], F32)
    sb("scr", [128, 512], F32)
    sb("sq2", [128, 512], BF16)


def build_block_d(nc, p, E, dr, out_mode, in_mode):
    pb = E["pb"]
    PB = [(pb[i], k_pb(i)) for i in range(8)]
    for nm in ("gsc", "usc", "dsc"):
        p.op("sync", lambda e, nm=nm: e.dma_start(
            out=E[nm][:, :], in_=dr[nm].ap()), writes=[nm], dma=True)

    if in_mode == "hbm":
        emit_load_stream(p, E, dr["x"])
    Xk = [f"X{dc}" for dc in range(NC_)]
    for q in range(4):
        b0 = q * 512

        def xs(dc, b0=b0):
            return (E["X"][:, dc * T + b0: dc * T + b0 + 512], Xk[dc])

        for ic in range(NI):
            wgu = E[f"wgu{ic % 2}"]
            wguk = f"wgu{ic % 2}"
            p.op("gpsimd", lambda e, ic=ic, wgu=wgu: e.dma_start(
                out=wgu[:, :], in_=dr["wgu_p"].ap()[ic]), writes=[wguk], dma=True)
            gb0 = (ic % 2) * 4
            g_b, g_k = pb[gb0], k_pb(gb0)
            u_b, u_k = pb[gb0 + 1], k_pb(gb0 + 1)
            for dc in range(NC_):
                xa, xk = xs(dc)
                p.op("tensor", lambda e, dc=dc, g_b=g_b, wgu=wgu, xa=xa: e.matmul(
                    g_b[:, :], wgu[:, dc * 128:(dc + 1) * 128], xa,
                    start=(dc == 0), stop=(dc == NC_ - 1)),
                    reads=[wguk, xk], writes=[g_k])
            for dc in range(NC_):
                xa, xk = xs(dc)
                p.op("tensor", lambda e, dc=dc, u_b=u_b, wgu=wgu, xa=xa: e.matmul(
                    u_b[:, :], wgu[:, 2048 + dc * 128: 2048 + (dc + 1) * 128], xa,
                    start=(dc == 0), stop=(dc == NC_ - 1)),
                    reads=[wguk, xk], writes=[u_k])
            p.op("scalar", lambda e, ic=ic, g_b=g_b: e.activation(
                E["s_f32"][:, :], g_b[:, :], AF.Silu,
                scale=E["gsc"][:, ic:ic + 1]),
                reads=[g_k, "gsc"], writes=["s_f32"])
            p.op("vector", lambda e, ic=ic, u_b=u_b: e.scalar_tensor_tensor(
                E["hd"][:, ic * 512:(ic + 1) * 512], u_b[:, :],
                E["usc"][:, ic:ic + 1], E["s_f32"][:, :],
                ALU.mult, ALU.mult),
                reads=[u_k, "usc", "s_f32"], writes=[f"hd{ic}"])
        st = StatRows(p, E, "st", 512, [PB[6], PB[7]], step=512)
        st.first = True
        for do in range(NC_):
            wd = E[f"wd{do % 2}"]
            wdk = f"wd{do % 2}"
            p.op("gpsimd", lambda e, do=do, wd=wd: e.dma_start(
                out=wd[:, :], in_=dr["wd_p"].ap()[do]), writes=[wdk], dma=True)
            db, dk = PB[(do % 2) * 3]
            for ic in range(NI):
                p.op("tensor", lambda e, ic=ic, db=db, wd=wd: e.matmul(
                    db[:, :], wd[:, ic * 128:(ic + 1) * 128],
                    E["hd"][:, ic * 512:(ic + 1) * 512],
                    start=(ic == 0), stop=(ic == NI - 1)),
                    reads=[wdk, f"hd{ic}"], writes=[dk])
            xa, xk = xs(do)
            p.op("vector", lambda e, do=do, db=db, xa=xa: e.scalar_tensor_tensor(
                xa, db[:, :], E["dsc"][:, do:do + 1], xa,
                ALU.mult, ALU.add),
                reads=[dk, "dsc", xk], writes=[xk])
            st.add_chunk(xa, xk, E["sq2"][:, 0:512], "sq2")
            if do == 0:
                st.first = False
        st.finalize(D, 1e-5)
        emit_row_bcast_hilo(p, E, "st_sum", E["st_sum"][0:1, 0:512], 512,
                            [PB[6]], E["f32a"][:, :], "f32a",
                            (E["rowb_hi"], "rowb_hi", E["rowb_lo"], "rowb_lo"))
        emit_row_bcast_hilo(p, E, "st_ssq", E["st_ssq"][0:1, 0:512], 512,
                            [PB[7]], E["f32b"][:, :], "f32b",
                            (E["rowb_hi"], "rowb_hi", E["rowb_lo"], "rowb_lo"))
        for dc in range(NC_):
            xa, xk = xs(dc)
            ob, obk = ((E["scr"], "scr") if dc % 2 == 0 else
                       (E["s_f32"], "s_f32"))
            p.op("vector", lambda e, xa=xa, ob=ob: e.tensor_tensor(
                ob[:, :], xa, E["f32a"][:, :], ALU.subtract),
                reads=[xk, "f32a"], writes=[obk])
            p.op("vector", lambda e, ob=ob: e.tensor_tensor(
                ob[:, :], ob[:, :], E["f32b"][:, :], ALU.mult),
                reads=[obk, "f32b"], writes=[obk])
            p.op("sync", lambda e, dc=dc, b0=b0, ob=ob: e.dma_start(
                out=dr["y"].ap()[dc * 128:(dc + 1) * 128, b0:b0 + 512],
                in_=ob[:, :]), reads=[obk], writes=[f"y{dc}_{b0}"],
                dma=True)


def build_launch_d():
    from contextlib import ExitStack
    nc = bass.Bass("TRN2", target_bir_lowering=False)
    need = {"X": True, "aug": False, "rowb_w": 1024}
    dr = declare_const_drams(nc, need)
    dr["x"] = nc.dram_tensor("x", [D, T], F32, kind="ExternalInput")
    dr["wgu_p"] = nc.dram_tensor("wgu_p", [NI, 128, 4096], F32, kind="ExternalInput")
    dr["wd_p"] = nc.dram_tensor("wd_p", [NC_, 128, NI * 128], F32, kind="ExternalInput")
    dr["gsc"] = nc.dram_tensor("gsc", [128, NI], F32, kind="ExternalInput")
    dr["usc"] = nc.dram_tensor("usc", [128, NI], F32, kind="ExternalInput")
    dr["dsc"] = nc.dram_tensor("dsc", [128, 16], F32, kind="ExternalInput")
    dr["y"] = nc.dram_tensor("y", [D, T], F32, kind="ExternalOutput")
    with ExitStack() as stack:
        E = make_env(nc, stack, need)
        env_block_d(nc, stack, E)
        sems = Prog.make_sems(nc, stack)
        block = stack.enter_context(nc.Block())
        p = Prog(nc)
        load_consts(p, E, dr, need)
        build_block_d(nc, p, E, dr, out_mode="hbm_fp32", in_mode="hbm")
        p.finalize(block, sems)
    return nc


def pack_d(inputs):
    gw = np.asarray(inputs["L1_ffn_gate_w"])
    uw = np.asarray(inputs["L1_ffn_up_w"])
    dw = np.asarray(inputs["L1_ffn_down_w"])
    def t_iT(w):
        return w.T.reshape(16, 128, 44, 128).transpose(2, 1, 0, 3)
    wg = t_iT(gw).reshape(NI, 128, 2048)
    wu = t_iT(uw).reshape(NI, 128, 2048)
    wgu_p = np.ascontiguousarray(
        np.stack([wg, wu], axis=1).transpose(0, 2, 1, 3)).reshape(NI, 128, 4096)
    wd_p = np.ascontiguousarray(
        dw.T.reshape(44, 128, 16, 128).transpose(2, 1, 0, 3)).reshape(NC_, 128, NI * 128)
    gsc = np.ascontiguousarray(
        np.asarray(inputs["L1_ffn_gate_s"]).reshape(44, 128).T).astype(np.float32)
    usc = np.ascontiguousarray(
        np.asarray(inputs["L1_ffn_up_s"]).reshape(44, 128).T).astype(np.float32)
    dsc = np.ascontiguousarray(
        np.asarray(inputs["L1_ffn_down_s"]).reshape(16, 128).T).astype(np.float32)
    return dict(wgu_p=wgu_p, wd_p=wd_p, gsc=gsc, usc=usc, dsc=dsc)



_NC_CACHE = {}


def _get_nc(name):
    if name not in _NC_CACHE:
        _NC_CACHE[name] = {"a": build_launch_a, "b": build_launch_b,
                           "c": build_launch_c, "d": build_launch_d}[name]()
    return _NC_CACHE[name]


def kernel_unfused(**inputs):
    x = np.asarray(inputs["L0_attn_x"])
    nb = x.shape[0]
    cores = list(range(nb))

    pa, pb_, pc, pd = (pack_a(inputs), pack_b(inputs), pack_c(inputs),
                       pack_d(inputs))
    ca = const_arrays({})
    cb = const_arrays({"sel": True})
    cc = const_arrays({"mask": True})
    cd = const_arrays({})

    cur = [np.ascontiguousarray(x[b].T) for b in range(nb)]

    res = run_bass_kernel_spmd(
        _get_nc("a"), [dict(ca, **pa, x=cur[b]) for b in range(nb)],
        core_ids=cores)
    cur = [res.results[b]["y"] for b in range(nb)]

    res = run_bass_kernel_spmd(
        _get_nc("b"), [dict(cb, **pb_, x=cur[b]) for b in range(nb)],
        core_ids=cores)
    cur = [res.results[b]["y"] for b in range(nb)]

    res = run_bass_kernel_spmd(
        _get_nc("c"), [dict(cc, **pc, x=cur[b]) for b in range(nb)],
        core_ids=cores)
    cur = [res.results[b]["y"] for b in range(nb)]

    res = run_bass_kernel_spmd(
        _get_nc("d"), [dict(cd, **pd, x=cur[b]) for b in range(nb)],
        core_ids=cores)
    out = np.stack([np.ascontiguousarray(res.results[b]["y"].T)
                    for b in range(nb)])
    return out.astype(np.float32)


def build_fused():
    from contextlib import ExitStack
    nc = bass.Bass("TRN2", target_bir_lowering=False)
    need = {"sel": True, "mask": True, "X": True, "aug": True,
            "rowb_w": 1024, "row_b": True}
    dr = declare_const_drams(nc, need)
    dr["x"] = nc.dram_tensor("x", [D, T], F32, kind="ExternalInput")
    dr["y"] = nc.dram_tensor("y", [D, T], F32, kind="ExternalOutput")
    dr["wq_p"] = nc.dram_tensor("wq_p", [NH, 128, 2048], F32, kind="ExternalInput")
    dr["wkv_p"] = nc.dram_tensor("wkv_p", [128, 2 * 16 * 128], F32, kind="ExternalInput")
    dr["wout_p"] = nc.dram_tensor("wout_p", [NC_, 128, 2048], F32, kind="ExternalInput")
    dr["gnT"] = nc.dram_tensor("gnT", [128, 16], F32, kind="ExternalInput")
    dr["goT"] = nc.dram_tensor("goT", [128, 16], F32, kind="ExternalInput")
    dr["hb_out"] = nc.dram_tensor("hb_out", [NC_, 128, T], BF16, kind="Internal")
    dr["wgu_p"] = nc.dram_tensor("wgu_p", [128, 128, 2048], F32, kind="ExternalInput")
    dr["wp_p"] = nc.dram_tensor("wp_p", [16, 128, 64, 128], F32, kind="ExternalInput")
    dr["sc_gu2"] = nc.dram_tensor("sc_gu2", [128, 128, 2048], BF16, kind="Internal")
    dr["sc_p"] = nc.dram_tensor("sc_p", [16, 128, 64, 128], BF16, kind="Internal")
    dr["wqk_p"] = nc.dram_tensor("wqk_p", [NH, 2, 128, 2048], F32, kind="ExternalInput")
    dr["wv_p"] = nc.dram_tensor("wv_p", [8, 128, 4096], F32, kind="ExternalInput")
    dr["wo_p"] = nc.dram_tensor("wo_p", [NC_, 128, 2048], F32, kind="ExternalInput")
    dr["bT"] = nc.dram_tensor("bT", [128, 48], F32, kind="ExternalInput")
    dr["bv_p"] = nc.dram_tensor("bv_p", [8, 256], F32, kind="ExternalInput")
    dr["wgu_d"] = nc.dram_tensor("wgu_d", [NI, 128, 4096], F32, kind="ExternalInput")
    dr["wd_p"] = nc.dram_tensor("wd_p", [NC_, 128, NI * 128], F32, kind="ExternalInput")
    dr["gsc"] = nc.dram_tensor("gsc", [128, NI], F32, kind="ExternalInput")
    dr["usc"] = nc.dram_tensor("usc", [128, NI], F32, kind="ExternalInput")
    dr["dsc"] = nc.dram_tensor("dsc", [128, 16], F32, kind="ExternalInput")

    with ExitStack() as stack:
        E = make_env(nc, stack, need)
        sems = Prog.make_sems(nc, stack)
        block = stack.enter_context(nc.Block())
        p = Prog(nc)
        load_consts(p, E, dr, need)
        emit_load_stream(p, E, dr["x"])
        with ExitStack() as sa:
            env_block_a(nc, sa, E, quant=True)
            Ea = dict(E)
            qe = QuantEmitter(p, Ea, dr, banks2=True)
            p.op("sync", lambda e: e.dma_start(out=Ea["gnT"][:], in_=dr["gnT"].ap()),
                 writes=["gnT_sb"], dma=True)
            p.op("sync", lambda e: e.dma_start(out=Ea["goT"][:], in_=dr["goT"].ap()),
                 writes=["goT_sb"], dma=True)
            build_block_a(nc, p, Ea, dr, out_mode="sbuf", qe=qe)
        p.full_barrier()
        with ExitStack() as s2:
            env_block_b(nc, s2, E, fused=True)
            build_block_b(nc, p, dict(E), dr, out_mode="sbuf", in_mode="sbuf",
                          quant=False)
        p.full_barrier()
        with ExitStack() as s3:
            env_block_c(nc, s3, E)
            build_block_c(nc, p, dict(E), dr, out_mode="sbuf", in_mode="sbuf")
        p.full_barrier()
        with ExitStack() as s4:
            env_block_d(nc, s4, E)
            dr2 = dict(dr)
            dr2["wgu_p"] = dr["wgu_d"]
            build_block_d(nc, p, dict(E), dr2, out_mode="hbm_fp32", in_mode="sbuf")
        p.finalize(block, sems)
    return nc


def kernel(**inputs):
    import os
    if os.environ.get("MK_UNFUSED"):
        return kernel_unfused(**inputs)
    x = np.asarray(inputs["L0_attn_x"])
    nb = x.shape[0]
    cores = list(range(nb))
    packs = {}
    packs.update(pack_a(inputs))
    packs.update(pack_b(inputs))
    packs.update(pack_c(inputs))
    pd = pack_d(inputs)
    packs["wgu_d"] = pd.pop("wgu_p")
    packs.update(pd)
    consts = const_arrays({"sel": True, "mask": True})
    if "fused" not in _NC_CACHE:
        _NC_CACHE["fused"] = build_fused()
    nc = _NC_CACHE["fused"]
    in_maps = [dict(consts, **packs, x=np.ascontiguousarray(x[b].T))
               for b in range(nb)]
    res = run_bass_kernel_spmd(nc, in_maps, core_ids=cores)
    out = np.stack([np.ascontiguousarray(res.results[b]["y"].T)
                    for b in range(nb)])
    return out.astype(np.float32)


# version 08
# speedup vs baseline: 1.0925x; 1.0171x over previous
import numpy as np
import ml_dtypes
import concourse.bass as bass
import concourse.mybir as mybir
from concourse.bass_utils import run_bass_kernel_spmd

F32 = mybir.dt.float32
BF16 = mybir.dt.bfloat16
AF = mybir.ActivationFunctionType
ALU = mybir.AluOpType
AX = mybir.AxisListType
BF = ml_dtypes.bfloat16

D = 2048
T = 2048
NC_ = 16
NH = 16
DH = 128
LN2_4 = 1.3862943611198906
M_THR = np.float32(513.0 / 1024.0)


_SBUF_SEQ = [0]


def _uname(name):
    _SBUF_SEQ[0] += 1
    return f"{name}_u{_SBUF_SEQ[0]}"


class Prog:
    ENGINES = ("tensor", "vector", "scalar", "sync", "gpsimd")

    def __init__(self, nc):
        self.nc = nc
        self.ops = []
        self.last_writer = {}
        self.readers = {}
        self.base_dep = None

    def op(self, eng, emit, reads=(), writes=(), dma=False, bar=False):
        idx = len(self.ops)
        deps = set()
        for k in reads:
            w = self.last_writer.get(k)
            if w is not None:
                deps.add(w)
        for k in writes:
            w = self.last_writer.get(k)
            if w is not None:
                deps.add(w)
            deps |= self.readers.get(k, set())
        if self.base_dep is not None and not bar:
            deps.add(self.base_dep)
        deps.discard(idx)
        self.ops.append(dict(idx=idx, eng=eng, emit=emit, deps=deps, dma=dma,
                             bar=bar))
        for k in reads:
            self.readers.setdefault(k, set()).add(idx)
        for k in writes:
            self.last_writer[k] = idx
            self.readers[k] = set()
        return idx

    def barrier(self, reads=(), writes=()):
        return self.op("vector", None, reads=reads, writes=writes, bar=True)

    def full_barrier(self):
        keys = set(self.last_writer) | set(self.readers)
        idx = self.barrier(writes=sorted(keys))
        self.base_dep = idx
        return idx

    NDMA = 4

    def finalize(self, block, sems):
        ops = self.ops
        n = len(ops)
        for o in ops:
            if any(ops[d].get("bar") for d in o["deps"]):
                nd = set()
                stk = list(o["deps"])
                while stk:
                    d = stk.pop()
                    if ops[d].get("bar"):
                        stk.extend(ops[d]["deps"])
                    else:
                        nd.add(d)
                o["deps"] = nd

        def edge_needed(dop, o):
            if dop["dma"]:
                return True
            if dop["eng"] == o["eng"] == "tensor" and not o["dma"]:
                return False
            return True

        need_inc = [False] * n
        for o in ops:
            if o.get("bar"):
                continue
            if o["dma"]:
                need_inc[o["idx"]] = True
            for d in o["deps"]:
                dop = ops[d]
                if edge_needed(dop, o):
                    need_inc[d] = True
        last_of = {}
        for o in ops:
            if not o["dma"] and not o.get("bar"):
                last_of[o["eng"]] = o["idx"]
        for i in last_of.values():
            need_inc[i] = True
        semkey = [None] * n
        val = [0] * n
        cum = {}
        dma_count = {}
        for o in ops:
            i = o["idx"]
            if o["dma"]:
                q = o["eng"]
                c = dma_count.get(q, 0)
                dma_count[q] = c + 1
                sk = f"{q}_d{c % self.NDMA}"
                cum[sk] = cum.get(sk, 0) + 16
            else:
                sk = o["eng"]
                if need_inc[i]:
                    cum[sk] = cum.get(sk, 0) + 1
            semkey[i] = sk
            val[i] = cum.get(sk, 0)
        finals = dict(cum)

        def body_for(eng_name):
            def body(eng):
                waited = {}
                for o in ops:
                    if o["eng"] != eng_name or o.get("bar"):
                        continue
                    i = o["idx"]
                    want = {}
                    for d in o["deps"]:
                        dop = ops[d]
                        if not edge_needed(dop, o):
                            continue
                        sk, v = semkey[d], val[d]
                        if v > waited.get(sk, 0) and v > want.get(sk, 0):
                            want[sk] = v
                    if o["dma"]:
                        sk, v = semkey[i], val[i] - 16
                        if v > waited.get(sk, 0) and v > want.get(sk, 0):
                            want[sk] = v
                    for sk, v in want.items():
                        eng.wait_ge(sems[sk], v)
                        waited[sk] = v
                    ins = o["emit"](eng)
                    if need_inc[i]:
                        ins.then_inc(sems[semkey[i]], 16 if o["dma"] else 1)
                if eng_name == "sync":
                    for sk, v in finals.items():
                        if v:
                            eng.wait_ge(sems[sk], v)
            return body

        block.sync(body_for("sync"))
        block.tensor(body_for("tensor"))
        block.vector(body_for("vector"))
        block.scalar(body_for("scalar"))
        block.gpsimd(body_for("gpsimd"))
        return finals

    @staticmethod
    def make_sems(nc, stack):
        sems = {}
        for e in Prog.ENGINES:
            sems[e] = stack.enter_context(nc.semaphore(f"s_{e}"))
        for q in ("sync", "gpsimd", "scalar"):
            for i in range(Prog.NDMA):
                sems[f"{q}_d{i}"] = stack.enter_context(
                    nc.semaphore(f"s_{q}_d{i}"))
        return sems


def k_pb(i):
    return f"pb{i}"


def emit_row_bcast_hilo(p, E, row_key, row_ap, width, banks, out_sb, out_key,
                        tmp_rows):
    hi, hi_k, lo, lo_k = tmp_rows
    p.op("vector", lambda e: e.tensor_copy(hi[0:1, 0:width], row_ap),
         reads=[row_key], writes=[hi_k])
    p.op("vector", lambda e: e.tensor_tensor(lo[0:1, 0:width], row_ap,
                                             hi[0:1, 0:width], ALU.subtract),
         reads=[row_key, hi_k], writes=[lo_k])
    nsl = (width + 511) // 512
    for i in range(nsl):
        c0, c1 = i * 512, min(width, (i + 1) * 512)
        pb, pbk = banks[i % len(banks)]
        p.op("tensor", lambda e, c0=c0, c1=c1, pb=pb: e.matmul(
            pb[:, 0:c1 - c0], E["ones_row_b"][:, :], hi[0:1, c0:c1],
            start=True, stop=False), reads=[hi_k], writes=[pbk])
        p.op("tensor", lambda e, c0=c0, c1=c1, pb=pb: e.matmul(
            pb[:, 0:c1 - c0], E["ones_row_b"][:, :], lo[0:1, c0:c1],
            start=False, stop=True), reads=[lo_k], writes=[pbk])
        p.op("scalar", lambda e, c0=c0, c1=c1, pb=pb: e.copy(
            out_sb[:, c0:c1], pb[:, 0:c1 - c0]),
            reads=[pbk], writes=[out_key])


EPS_COL = {1e-5: 0, 1e-12: 1, 1.1920929e-07: 2}


def emit_rstd_row(p, out_ap, in_ap, rw_key, eps, scale=1.0, mult4=False, E=None):
    ec = E["epsc"]
    j = EPS_COL[eps]
    p.op("scalar", lambda e: e.activation(out_ap, in_ap, AF.Ln,
                                          bias=ec[0:1, j:j + 1],
                                          scale=float(scale)),
         reads=[rw_key, "c_eps_sb"], writes=[rw_key])
    if mult4:
        p.op("scalar", lambda e: e.activation(out_ap, out_ap, AF.Exp,
                                              bias=ec[0:1, 3:4], scale=-0.5),
             reads=[rw_key, "c_eps_sb"], writes=[rw_key])
    else:
        p.op("scalar", lambda e: e.activation(out_ap, out_ap, AF.Exp,
                                              bias=0.0, scale=-0.5),
             reads=[rw_key], writes=[rw_key])


class StatRows:

    def __init__(self, p, E, name, width, banks, step=1024):
        self.p, self.E, self.width = p, E, width
        self.step = step
        self.banks = banks
        self.sum_sb = E[f"{name}_sum"]
        self.sum_k = f"{name}_sum"
        self.ssq_sb = E[f"{name}_ssq"]
        self.ssq_k = f"{name}_ssq"
        self.first = True
        self.bi = 0

    def add_chunk(self, src_ap, src_key, sq_tmp, sq_key, col0=0):
        p, E = self.p, self.E
        w = self.widthof(src_ap)
        for b0 in range(0, w, self.step):
            b1 = min(w, b0 + self.step)
            p.op("vector", lambda e, b0=b0, b1=b1: e.tensor_tensor(
                sq_tmp[:, 0:b1 - b0], src_ap[:, b0:b1], src_ap[:, b0:b1],
                ALU.mult), reads=[src_key], writes=[sq_key])
            self._mm_part(src_ap, src_key, sq_tmp, sq_key, col0, b0, b1)
        return

    def _mm_part(self, src_ap, src_key, sq_tmp, sq_key, col0, b0, b1):
        p, E = self.p, self.E
        for i in range((b1 - b0 + 511) // 512):
            a0 = b0 + i * 512
            a1 = min(b1, a0 + 512)
            pb, pbk = self.banks[self.bi % len(self.banks)]
            self.bi += 1
            pb2, pbk2 = self.banks[self.bi % len(self.banks)]
            self.bi += 1
            p.op("tensor", lambda e, a0=a0, a1=a1, pb=pb: e.matmul(
                pb[0:1, 0:a1 - a0], E["ones_col_b"][:, :],
                src_ap[:, a0:a1], start=True, stop=True),
                reads=[src_key], writes=[pbk])
            p.op("tensor", lambda e, a0=a0, a1=a1, pb2=pb2, b0=b0: e.matmul(
                pb2[0:1, 0:a1 - a0], E["ones_col_b"][:, :],
                sq_tmp[:, a0 - b0:a1 - b0], start=True, stop=True),
                reads=[sq_key], writes=[pbk2])
            op = ALU.bypass if self.first else ALU.add
            c0, c1 = col0 + a0, col0 + a1
            if self.first:
                p.op("vector", lambda e, c0=c0, c1=c1, pb=pb: e.tensor_copy(
                    self.sum_sb[0:1, c0:c1], pb[0:1, 0:c1 - c0]),
                    reads=[pbk], writes=[self.sum_k])
                p.op("vector", lambda e, c0=c0, c1=c1, pb2=pb2: e.tensor_copy(
                    self.ssq_sb[0:1, c0:c1], pb2[0:1, 0:c1 - c0]),
                    reads=[pbk2], writes=[self.ssq_k])
            else:
                p.op("vector", lambda e, c0=c0, c1=c1, pb=pb: e.tensor_tensor(
                    self.sum_sb[0:1, c0:c1], self.sum_sb[0:1, c0:c1],
                    pb[0:1, 0:c1 - c0], ALU.add),
                    reads=[pbk, self.sum_k], writes=[self.sum_k])
                p.op("vector", lambda e, c0=c0, c1=c1, pb2=pb2: e.tensor_tensor(
                    self.ssq_sb[0:1, c0:c1], self.ssq_sb[0:1, c0:c1],
                    pb2[0:1, 0:c1 - c0], ALU.add),
                    reads=[pbk2, self.ssq_k], writes=[self.ssq_k])

    @staticmethod
    def widthof(ap):
        return ap.ap[-1][1]

    def chunk_done(self):
        self.first = False

    def finalize(self, denom, eps):
        p, E, w = self.p, self.E, self.width
        m_row, m_key = self.sum_sb, self.sum_k
        r_row, r_key = self.ssq_sb, self.ssq_k
        p.op("vector", lambda e: e.tensor_scalar(
            m_row[0:1, 0:w], m_row[0:1, 0:w], 1.0 / denom, None, ALU.mult),
            reads=[m_key], writes=[m_key])
        p.op("vector", lambda e: e.tensor_scalar(
            r_row[0:1, 0:w], r_row[0:1, 0:w], 1.0 / denom, None,
            ALU.mult), reads=[r_key], writes=[r_key])
        p.op("vector", lambda e: e.tensor_tensor(
            E["row_a"][0:1, 0:w], m_row[0:1, 0:w],
            m_row[0:1, 0:w], ALU.mult), reads=[m_key], writes=["row_a"])
        p.op("vector", lambda e: e.tensor_tensor(
            r_row[0:1, 0:w], r_row[0:1, 0:w],
            E["row_a"][0:1, 0:w], ALU.subtract),
            reads=[r_key, "row_a"], writes=[r_key])
        emit_rstd_row(p, r_row[0:1, 0:w], r_row[0:1, 0:w], r_key, eps, E=E)


def make_env(nc, stack, need):
    ent = stack.enter_context
    E = {}

    def sb(name, shape, dt):
        E[name] = ent(nc.sbuf_tensor(_uname(name), shape, dt))
        return E[name]

    E["pb"] = []
    for i in range(8):
        t = ent(nc.psum_tensor(f"pb{i}", [128, 512], F32))
        E["pb"].append(t)

    sb("ones_col_b", [128, 1], BF16)
    sb("ones_row_b", [1, 128], BF16)
    sb("ones_row_f", [1, 128], F32)
    sb("one_f", [1, 1], F32)
    if need.get("sel"):
        sb("sel", [128, 2], BF16)
        sb("selT", [2, 128], BF16)
    if need.get("mask"):
        sb("mask", [128, 128], F32)

    if need.get("X", True):
        sb("X", [128, NC_ * T], BF16)
    for r in ("row_a", "st_sum", "st_ssq"):
        sb(r, [1, T], F32)

    if need.get("row_b", True):
        sb("row_b", [1, 512], F32)
    rw = need.get("rowb_w", T)
    sb("rowb_hi", [1, rw], BF16)
    sb("rowb_lo", [1, rw], BF16)
    sb("colA", [128, 16], F32)
    sb("colB", [128, 16], F32)
    sb("epsc", [128, 8], F32)
    return E


def load_consts(p, E, dr, need):
    keys = []

    def ld(dst, name):
        p.op("sync", lambda e: e.dma_start(out=dst[:], in_=dr[name].ap()),
             writes=[name + "_sb"], dma=True)
        keys.append(name + "_sb")

    ld(E["ones_col_b"], "c_ones_col")
    ld(E["ones_row_b"], "c_ones_row_b")
    ld(E["ones_row_f"], "c_ones_row_f")
    ld(E["one_f"], "c_one")
    ld(E["epsc"], "c_eps")
    if need.get("sel"):
        ld(E["sel"], "c_sel")
        ld(E["selT"], "c_selT")
    if need.get("mask"):
        ld(E["mask"], "c_mask")
    for eng in ("tensor", "vector", "scalar", "gpsimd"):
        p.op(eng, lambda e: e.nop(), reads=list(keys))


def declare_const_drams(nc, need):
    dr = {}
    dr["c_ones_col"] = nc.dram_tensor("c_ones_col", [128, 1], BF16, kind="ExternalInput")
    dr["c_ones_row_b"] = nc.dram_tensor("c_ones_row_b", [1, 128], BF16, kind="ExternalInput")
    dr["c_ones_row_f"] = nc.dram_tensor("c_ones_row_f", [1, 128], F32, kind="ExternalInput")
    dr["c_one"] = nc.dram_tensor("c_one", [1, 1], F32, kind="ExternalInput")
    dr["c_eps"] = nc.dram_tensor("c_eps", [128, 8], F32, kind="ExternalInput")
    if need.get("sel"):
        dr["c_sel"] = nc.dram_tensor("c_sel", [128, 2], BF16, kind="ExternalInput")
        dr["c_selT"] = nc.dram_tensor("c_selT", [2, 128], BF16, kind="ExternalInput")
    if need.get("mask"):
        dr["c_mask"] = nc.dram_tensor("c_mask", [128, 128], F32, kind="ExternalInput")
    return dr


def const_arrays(need):
    c = {
        "c_ones_col": np.ones((128, 1), BF),
        "c_ones_row_b": np.ones((1, 128), BF),
        "c_ones_row_f": np.ones((1, 128), np.float32),
        "c_one": np.ones((1, 1), np.float32),
        "c_eps": np.tile(np.array([[1e-5, 1e-12, 1.1920929e-07, 1.3862943611198906,
                                    0.0, -2.4260151319598084, 0.0, 0.0]], np.float32), (128, 1)),
    }
    if need.get("sel"):
        sel = np.zeros((128, 2), BF)
        sel[0:64, 0] = 1
        sel[64:128, 1] = 1
        c["c_sel"] = sel
        c["c_selT"] = np.ascontiguousarray(sel.T)
    if need.get("mask"):
        q = np.arange(128)
        c["c_mask"] = np.where(q[None, :] >= q[:, None], 0.0, -1e9).astype(np.float32)
    return c


def emit_load_stream(p, E, x_dram):
    for dc in range(NC_):
        p.op("gpsimd", lambda e, dc=dc: e.dma_start(
            out=E["X"][:, dc * T:(dc + 1) * T],
            in_=x_dram.ap()[dc * 128:(dc + 1) * 128, :]),
            writes=[f"X{dc}"], dma=True)


def emit_store_stream_fp32(p, E, y_dram, src_name, src, scr, scr_k):
    pass



def emit_sumsq_row(p, E, src_sb, src_key, row_sb, row_key, pb, pbk, step=1024,
                   sq=None):
    sq_ap, sq_key = sq if sq is not None else (E["sq2"], "sq2")
    for b0 in range(0, T, step):
        p.op("vector", lambda e, b0=b0, step=step: e.tensor_tensor(
            sq_ap[:, 0:step], src_sb[:, b0:b0 + step],
            src_sb[:, b0:b0 + step], ALU.mult),
            reads=[src_key], writes=[sq_key])
        for i in range(step // 512):
            a0 = b0 + i * 512
            p.op("tensor", lambda e, a0=a0, b0=b0: e.matmul(
                pb[0:1, :], E["ones_col_b"][:, :],
                sq_ap[:, a0 - b0:a0 - b0 + 512], start=True, stop=True),
                reads=[sq_key], writes=[pbk])
            p.op("vector", lambda e, a0=a0: e.tensor_copy(
                row_sb[0:1, a0:a0 + 512], pb[0:1, :]),
                reads=[pbk], writes=[row_key])

def build_block_a(nc, p, E, dr, out_mode, qe=None):
    pb = E["pb"]
    PB = [(pb[i], k_pb(i)) for i in range(8)]

    Xk = [f"X{dc}" for dc in range(NC_)]

    st = StatRows(p, E, "st", T, [PB[0], PB[1], PB[2], PB[3]])
    for dc in range(NC_):
        st.add_chunk(E["X"][:, dc * T:(dc + 1) * T], Xk[dc], E["sq2"][:, 0:1024], "sq2")
        st.chunk_done()
    st.finalize(D, 1e-5)
    p.op("vector", lambda e: e.tensor_copy(E["row_mb"][0:1, :], E["st_sum"][0:1, :]),
         reads=["st_sum"], writes=["row_mb"])
    for j in range(NC_):
        p.op("tensor", lambda e, j=j: e.matmul(
            pb[4][:, j:j + 1], E["st_ssq"][0:1, j * 128:(j + 1) * 128],
            E["one_f"][0:1, 0:1], start=True, stop=True),
            reads=["st_ssq"], writes=[k_pb(4)])
    p.op("scalar", lambda e: e.copy(E["colA"][:, 0:16], pb[4][:, 0:16]),
         reads=[k_pb(4)], writes=["colA"])
    if qe:
        qe.pump(16)

    p.op("gpsimd", lambda e: e.dma_start(out=E["wkv"][:], in_=dr["wkv_p"].ap()),
         writes=["wkv"], dma=True)
    for kv in range(2):
        p.op("vector", lambda e, kv=kv: e.tensor_tensor(
            E["wkv"][:, kv * 2048:(kv + 1) * 2048].rearrange("p (c n) -> p c n", n=128),
            E["wkv"][:, kv * 2048:(kv + 1) * 2048].rearrange("p (c n) -> p c n", n=128),
            E["gnT"][:, :, None].broadcast_to([128, 16, 128]), ALU.mult),
            reads=["wkv", "gnT_sb"], writes=["wkv"])
    for kv in range(2):
        for dc in range(NC_):
            p.op("tensor", lambda e, kv=kv, dc=dc: e.matmul(
                pb[5][0:1, 0:128], E["ones_col_b"][:, :],
                E["wkv"][:, kv * 2048 + dc * 128: kv * 2048 + (dc + 1) * 128],
                start=(dc == 0), stop=(dc == NC_ - 1)),
                reads=["wkv"], writes=[k_pb(5)])
        p.op("vector", lambda e, kv=kv: e.tensor_scalar(
            E["nbias"][0:1, kv * 128:(kv + 1) * 128], pb[5][0:1, 0:128],
            -1.0, None, ALU.mult), reads=[k_pb(5)], writes=["nbias"])

    for tc in range(4):
        for dc in range(NC_):
            p.op("tensor", lambda e, tc=tc, dc=dc: e.matmul(
                pb[tc][:, :], E["wkv"][:, dc * 128:(dc + 1) * 128],
                E["X"][:, dc * T + tc * 512: dc * T + tc * 512 + 512],
                start=(dc == 0), stop=False),
                reads=["wkv", Xk[dc]], writes=[k_pb(tc)])
        p.op("tensor", lambda e, tc=tc: e.matmul(
            pb[tc][:, :], E["nbias"][0:1, 0:128],
            E["row_mb"][0:1, tc * 512:tc * 512 + 512],
            start=False, stop=True),
            reads=["nbias", "row_mb"], writes=[k_pb(tc)])
        p.op("scalar", lambda e, tc=tc: e.copy(
            E["k_sb"][:, tc * 512:tc * 512 + 512], pb[tc][:, :]),
            reads=[k_pb(tc)], writes=["k_sb"])
        if qe:
            qe.pump(6)
    emit_sumsq_row(p, E, E["k_sb"], "k_sb", E["row_a"], "row_a", pb[4], k_pb(4))
    emit_rstd_row(p, E["row_a"][0:1, :], E["row_a"][0:1, :], "row_a", 1e-12,
                  mult4=True, E=E)
    for j in range(NC_):
        p.op("tensor", lambda e, j=j: e.matmul(
            pb[4][:, j:j + 1], E["row_a"][0:1, j * 128:(j + 1) * 128],
            E["one_f"][0:1, 0:1], start=True, stop=True),
            reads=["row_a"], writes=[k_pb(4)])
    p.op("scalar", lambda e: e.copy(E["colB"][:, 0:16], pb[4][:, 0:16]),
         reads=[k_pb(4)], writes=["colB"])

    for tt in range(NC_):
        if qe:
            qe.pump(2)
        for dc in range(NC_):
            p.op("tensor", lambda e, tt=tt, dc=dc: e.matmul(
                pb[4][:, 0:128], E["X"][:, dc * T + tt * 128: dc * T + (tt + 1) * 128],
                E["wkv"][:, 2048 + dc * 128: 2048 + (dc + 1) * 128],
                start=(dc == 0), stop=False),
                reads=["wkv", Xk[dc]], writes=[k_pb(4)])
        p.op("tensor", lambda e, tt=tt: e.matmul(
            pb[4][:, 0:128], E["row_mb"][0:1, tt * 128:(tt + 1) * 128],
            E["nbias"][0:1, 128:256], start=False, stop=True),
            reads=["row_mb", "nbias"], writes=[k_pb(4)])
        p.op("vector", lambda e, tt=tt: e.tensor_scalar(
            E["v_sb"][:, tt * 128:(tt + 1) * 128], pb[4][:, 0:128],
            E["colA"][:, tt:tt + 1], None, ALU.mult),
            reads=[k_pb(4), "colA"], writes=["v_sb"])

    wqbuf = [E["wkv"][:, 0:2048], E["wkv"][:, 2048:4096]]
    for h in range(NH):
        if qe and h == 12:
            qe.finish()
            p.barrier(reads=[t[2] for t in qe.todo] +
                      [f"qw{s}" for s in range(NSQ)],
                      writes=[f"ctx{hh}" for hh in range(12, 16)])
        wbuf = wqbuf[h % 2]
        wk = f"wq{h % 2}"
        wr = [wk, "wkv"] if h < 2 else [wk]
        p.op("gpsimd", lambda e, h=h, wbuf=wbuf: e.dma_start(
            out=wbuf, in_=dr["wq_p"].ap()[h]), writes=wr, dma=True)
        p.op("vector", lambda e, wbuf=wbuf: e.tensor_tensor(
            wbuf.rearrange("p (c n) -> p c n", n=128),
            wbuf.rearrange("p (c n) -> p c n", n=128),
            E["gnT"][:, :, None].broadcast_to([128, 16, 128]), ALU.mult),
            reads=[wk, "gnT_sb"], writes=[wk])
        for dc in range(NC_):
            p.op("tensor", lambda e, dc=dc, wbuf=wbuf: e.matmul(
                pb[5][0:1, 0:128], E["ones_col_b"][:, :],
                wbuf[:, dc * 128:(dc + 1) * 128],
                start=(dc == 0), stop=(dc == NC_ - 1)),
                reads=[wk], writes=[k_pb(5)])
        p.op("vector", lambda e: e.tensor_scalar(
            E["nbias"][0:1, 256:384], pb[5][0:1, 0:128], -1.0, None, ALU.mult),
            reads=[k_pb(5)], writes=["nbias_q"])
        for tc in range(4):
            for dc in range(NC_):
                p.op("tensor", lambda e, tc=tc, dc=dc, wbuf=wbuf: e.matmul(
                    pb[tc][:, :], wbuf[:, dc * 128:(dc + 1) * 128],
                    E["X"][:, dc * T + tc * 512: dc * T + tc * 512 + 512],
                    start=(dc == 0), stop=False),
                    reads=[wk, Xk[dc]], writes=[k_pb(tc)])
            p.op("tensor", lambda e, tc=tc: e.matmul(
                pb[tc][:, :], E["nbias"][0:1, 256:384],
                E["row_mb"][0:1, tc * 512:tc * 512 + 512],
                start=False, stop=True),
                reads=["nbias_q", "row_mb"], writes=[k_pb(tc)])
            p.op("scalar", lambda e, tc=tc: e.copy(
                E["q_sb"][:, tc * 512:tc * 512 + 512], pb[tc][:, :]),
                reads=[k_pb(tc)], writes=["q_sb"])
            if qe:
                qe.pump(2)
        emit_sumsq_row(p, E, E["q_sb"], "q_sb", E["row_a"], "row_a", pb[4], k_pb(4))
        emit_rstd_row(p, E["row_a"][0:1, :], E["row_a"][0:1, :], "row_a", 1e-12,
                      mult4=True, E=E)
        for b0 in range(0, T, 1024):
            emit_row_bcast_hilo(p, E, "row_a", E["row_a"][0:1, b0:b0 + 1024],
                                1024, [PB[0], PB[1]], E["f32a"], "f32a",
                                (E["rowb_hi"], "rowb_hi", E["rowb_lo"], "rowb_lo"))
            p.op("vector", lambda e, b0=b0: e.tensor_tensor(
                E["q_sb"][:, b0:b0 + 1024], E["q_sb"][:, b0:b0 + 1024],
                E["f32a"][:, 0:1024], ALU.mult),
                reads=["q_sb", "f32a"], writes=["q_sb"])

        for qs in range(4):
            q0 = qs * 512
            ctx_pb, ctx_k = PB[4]
            den_pb, den_k = PB[5]

            def emit_S(kt, q0=q0):
                s_pb, s_k = PB[2 + (kt % 2)]
                p.op("tensor", lambda e, kt=kt, q0=q0, s_pb=s_pb: e.matmul(
                    s_pb[:, :], E["k_sb"][:, kt * 128:(kt + 1) * 128],
                    E["q_sb"][:, q0:q0 + 512], start=True, stop=True),
                    reads=["k_sb", "q_sb"], writes=[s_k])

            emit_S(0)
            for kt in range(NC_):
                s_pb, s_k = PB[2 + (kt % 2)]
                eb = E[f"e_sb{kt % 2}"]
                ek = f"e_sb{kt % 2}"
                p.op("scalar", lambda e, kt=kt, s_pb=s_pb, eb=eb: e.activation(
                    eb[:, 0:512], s_pb[:, :], AF.Exp,
                    bias=0.0, scale=E["colB"][:, kt:kt + 1]),
                    reads=[s_k, "colB"], writes=[ek])
                if kt + 1 < NC_:
                    emit_S(kt + 1)
                if qe:
                    qe.pump(1)
                p.op("tensor", lambda e, kt=kt, ctx_pb=ctx_pb, eb=eb: e.matmul(
                    ctx_pb[:, :], E["v_sb"][:, kt * 128:(kt + 1) * 128],
                    eb[:, 0:512], start=(kt == 0), stop=(kt == NC_ - 1)),
                    reads=["v_sb", ek], writes=[ctx_k])
                p.op("tensor", lambda e, kt=kt, den_pb=den_pb, eb=eb: e.matmul(
                    den_pb[0:1, :], E["ones_col_b"][:, :],
                    eb[:, 0:512], start=(kt == 0), stop=(kt == NC_ - 1)),
                    reads=[ek], writes=[den_k])
            p.op("vector", lambda e: e.reciprocal(E["row_b"][0:1, 0:512],
                                                  den_pb[0:1, :]),
                 reads=[den_k], writes=["row_b"])
            emit_row_bcast_hilo(p, E, "row_b", E["row_b"][0:1, 0:512], 512,
                                [PB[2]], E["f32b"][:, 0:512], "f32b",
                                (E["rowb_hi"], "rowb_hi", E["rowb_lo"], "rowb_lo"))
            p.op("vector", lambda e, h=h, q0=q0, ctx_pb=ctx_pb: e.tensor_tensor(
                E["ctx"][:, h * T + q0: h * T + q0 + 512], ctx_pb[:, :],
                E["f32b"][:, 0:512], ALU.mult),
                reads=[ctx_k, "f32b"], writes=[f"ctx{h}"])

    st1 = StatRows(p, E, "st", T, [PB[4], PB[5]])
    for do in range(NC_):
        if qe:
            qe.pump(2)
        wbuf = wqbuf[do % 2]
        wk = f"wq{do % 2}"
        p.op("gpsimd", lambda e, do=do, wbuf=wbuf: e.dma_start(
            out=wbuf, in_=dr["wout_p"].ap()[do]), writes=[wk], dma=True)
        for tc in range(4):
            for hc in range(NC_):
                p.op("tensor", lambda e, tc=tc, hc=hc, wbuf=wbuf: e.matmul(
                    pb[tc % 4][:, :], wbuf[:, hc * 128:(hc + 1) * 128],
                    E["ctx"][:, hc * T + tc * 512: hc * T + tc * 512 + 512],
                    start=(hc == 0), stop=(hc == NC_ - 1)),
                    reads=[wk, f"ctx{hc}"], writes=[k_pb(tc % 4)])
            p.op("scalar", lambda e, tc=tc: e.copy(
                E["q_sb"][:, tc * 512:tc * 512 + 512], pb[tc % 4][:, :]),
                reads=[k_pb(tc % 4)] + (["q_sb"] if do == 0 and tc == 0 else []),
                writes=["o_buf"])
        p.op("sync", lambda e, do=do: e.dma_start(
            out=dr["hb_out"].ap()[do], in_=E["q_sb"][:, :]),
            reads=["o_buf"], writes=[f"hb_out{do}"], dma=True)
        st1.add_chunk(E["q_sb"][:, 0:T], "o_buf", E["sq2"][:, 0:1024], "sq2")
        st1.chunk_done()
    st1.finalize(D, 1e-5)

    st2 = StatRows(p, E, "st", T, [PB[4], PB[5]])
    for b0 in (0, 1024):
        st2.first = True
        emit_row_bcast_hilo(p, E, "st_sum", E["st_sum"][0:1, b0:b0 + 1024], 1024,
                            [PB[0], PB[1]], E["f32a"], "f32a",
                            (E["rowb_hi"], "rowb_hi", E["rowb_lo"], "rowb_lo"))
        emit_row_bcast_hilo(p, E, "st_ssq", E["st_ssq"][0:1, b0:b0 + 1024], 1024,
                            [PB[2], PB[3]], E["f32b"], "f32b",
                            (E["rowb_hi"], "rowb_hi", E["rowb_lo"], "rowb_lo"))
        for do in range(NC_):
            ob = wqbuf[do % 2]
            obk = f"wq{do % 2}"
            p.op("sync", lambda e, do=do, ob=ob, b0=b0: e.dma_start(
                out=ob[:, 0:1024], in_=dr["hb_out"].ap()[do][:, b0:b0 + 1024]),
                reads=[f"hb_out{do}"], writes=[obk], dma=True)
            p.op("vector", lambda e, ob=ob: e.tensor_tensor(
                E["scr"][:, 0:1024], ob[:, 0:1024], E["f32a"][:, 0:1024],
                ALU.subtract), reads=[obk, "f32a"], writes=["scr"])
            p.op("vector", lambda e, do=do: e.tensor_scalar(
                E["scr"][:, 0:1024], E["scr"][:, 0:1024],
                E["goT"][:, do:do + 1], None, ALU.mult),
                reads=["scr", "goT_sb"], writes=["scr"])
            p.op("vector", lambda e: e.tensor_tensor(
                E["scr"][:, 0:1024], E["scr"][:, 0:1024], E["f32b"][:, 0:1024],
                ALU.mult), reads=["scr", "f32b"], writes=["scr"])
            p.op("vector", lambda e, do=do, b0=b0: e.tensor_tensor(
                E["ctx"][:, do * T + b0: do * T + b0 + 1024], E["scr"][:, 0:1024],
                E["X"][:, do * T + b0: do * T + b0 + 1024], ALU.add),
                reads=["scr", f"X{do}"], writes=[f"ctx{do}"])
            st2.add_chunk(E["ctx"][:, do * T + b0: do * T + b0 + 1024],
                          f"ctx{do}", E["sq2"][:, 0:1024], "sq2", col0=b0)
            if do == 0:
                st2.first = False
    st2.finalize(D, 1e-5)

    for b0 in (0, 1024):
        emit_row_bcast_hilo(p, E, "st_sum", E["st_sum"][0:1, b0:b0 + 1024], 1024,
                            [PB[0], PB[1]], E["f32a"], "f32a",
                            (E["rowb_hi"], "rowb_hi", E["rowb_lo"], "rowb_lo"))
        emit_row_bcast_hilo(p, E, "st_ssq", E["st_ssq"][0:1, b0:b0 + 1024], 1024,
                            [PB[2], PB[3]], E["f32b"], "f32b",
                            (E["rowb_hi"], "rowb_hi", E["rowb_lo"], "rowb_lo"))
        for dc in range(NC_):
            p.op("vector", lambda e, dc=dc, b0=b0: e.tensor_tensor(
                E["scr"][:, 0:1024], E["ctx"][:, dc * T + b0: dc * T + b0 + 1024],
                E["f32a"][:, 0:1024], ALU.subtract),
                reads=[f"ctx{dc}", "f32a"], writes=["scr"])
            if out_mode == "hbm_fp32":
                p.op("vector", lambda e: e.tensor_tensor(
                    E["scr"][:, 0:1024], E["scr"][:, 0:1024],
                    E["f32b"][:, 0:1024], ALU.mult),
                    reads=["scr", "f32b"], writes=["scr"])
                p.op("sync", lambda e, dc=dc, b0=b0: e.dma_start(
                    out=dr["y"].ap()[dc * 128:(dc + 1) * 128, b0:b0 + 1024],
                    in_=E["scr"][:, 0:1024]),
                    reads=["scr"], writes=["y"], dma=True)
            else:
                p.op("vector", lambda e, dc=dc, b0=b0: e.tensor_tensor(
                    E["X"][:, dc * T + b0: dc * T + b0 + 1024], E["scr"][:, 0:1024],
                    E["f32b"][:, 0:1024], ALU.mult),
                    reads=["scr", "f32b"], writes=[f"X{dc}"])
    if qe:
        qe.finish()


def env_block_a(nc, stack, E, quant=False):
    ent = stack.enter_context

    def sb(name, shape, dt):
        E[name] = ent(nc.sbuf_tensor(_uname(name), shape, dt))

    sb("ctx", [128, NH * T], BF16)
    if quant:
        names = [f"q{t}{s}" for s in range(NSQ) for t in ("w", "h", "b")]
        base = 12 * T
        for i, nm in enumerate(names[:16]):
            E[nm] = E["ctx"][:, base + i * 512: base + (i + 1) * 512]
        for nm in names[16:]:
            sb(nm, [128, 512], BF16)
        for s in range(4):
            sb(f"qsc{s}", [2, 512], BF16)
    sb("k_sb", [128, T], BF16)
    sb("v_sb", [128, T], BF16)
    sb("q_sb", [128, T], BF16)
    sb("sq2", [128, 1024], BF16)
    sb("row_mb", [1, T], BF16)
    sb("wkv", [128, 2 * 2048], BF16)
    sb("e_sb0", [128, 512], BF16)
    sb("e_sb1", [128, 512], BF16)
    sb("f32a", [128, 1024], F32)
    sb("f32b", [128, 1024], F32)
    sb("scr", [128, 1024], F32)
    sb("nbias", [1, 512], BF16)
    sb("gnT_s", [128, 16], F32)
    sb("goT_s", [128, 16], F32)
    E["gnT"] = E["gnT_s"]
    E["goT"] = E["goT_s"]


def build_launch_a():
    from contextlib import ExitStack
    nc = bass.Bass("TRN2", target_bir_lowering=False)
    need = {}
    dr = declare_const_drams(nc, need)
    dr["x"] = nc.dram_tensor("x", [D, T], F32, kind="ExternalInput")
    dr["wq_p"] = nc.dram_tensor("wq_p", [NH, 128, 2048], F32, kind="ExternalInput")
    dr["wkv_p"] = nc.dram_tensor("wkv_p", [128, 2 * 16 * 128], F32, kind="ExternalInput")
    dr["wout_p"] = nc.dram_tensor("wout_p", [NC_, 128, 2048], F32, kind="ExternalInput")
    dr["gnT"] = nc.dram_tensor("gnT", [128, 16], F32, kind="ExternalInput")
    dr["goT"] = nc.dram_tensor("goT", [128, 16], F32, kind="ExternalInput")
    dr["hb_out"] = nc.dram_tensor("hb_out", [NC_, 128, T], BF16, kind="Internal")
    dr["y"] = nc.dram_tensor("y", [D, T], F32, kind="ExternalOutput")

    with ExitStack() as stack:
        E = make_env(nc, stack, need)
        env_block_a(nc, stack, E)
        sems = Prog.make_sems(nc, stack)
        block = stack.enter_context(nc.Block())
        p = Prog(nc)
        load_consts(p, E, dr, need)
        p.op("sync", lambda e: e.dma_start(out=E["gnT"][:], in_=dr["gnT"].ap()),
             writes=["gnT_sb"], dma=True)
        p.op("sync", lambda e: e.dma_start(out=E["goT"][:], in_=dr["goT"].ap()),
             writes=["goT_sb"], dma=True)
        emit_load_stream(p, E, dr["x"])
        build_block_a(nc, p, E, dr, out_mode="hbm_fp32")
        p.finalize(block, sems)
    return nc


def pack_a(inputs):
    Wq = np.asarray(inputs["L0_attn_Wq"])
    Wkv = np.asarray(inputs["L0_attn_Wkv"])
    Wout = np.asarray(inputs["L0_attn_Wout"])
    g = np.asarray(inputs["L0_attn_g_norm"])
    go = np.asarray(inputs["L0_attn_g_out"])
    wq_p = np.ascontiguousarray(
        Wq.reshape(16, 128, 16, 128).transpose(2, 1, 0, 3)).reshape(NH, 128, 2048)
    wkv_p = np.ascontiguousarray(
        Wkv.reshape(16, 128, 2, 128).transpose(1, 2, 0, 3)).reshape(128, 2 * 16 * 128)
    wout_p = np.ascontiguousarray(
        Wout.reshape(16, 128, 16, 128).transpose(2, 1, 0, 3)).reshape(NC_, 128, 2048)
    gnT = np.ascontiguousarray(g.reshape(16, 128).T).astype(np.float32)
    goT = np.ascontiguousarray(go.reshape(16, 128).T).astype(np.float32)
    return dict(wq_p=wq_p, wkv_p=wkv_p, wout_p=wout_p, gnT=gnT, goT=goT)





INV_M = float(1.0 / np.float64(M_THR))


NSQ = 6


def env_block_b(nc, stack, E, fused=False):
    ent = stack.enter_context

    def sb(name, shape, dt):
        E[name] = ent(nc.sbuf_tensor(_uname(name), shape, dt))

    sb("h_sb", [128, 64 * 512], BF16)
    if not fused:
        sb("xq0", [128, 16 * 512], BF16)
        sb("xq1", [128, 16 * 512], BF16)
    sb("whg0", [128, 2048], BF16)
    sb("whg1", [128, 2048], BF16)
    sb("whp0", [128, 2048], BF16)
    sb("whp1", [128, 2048], BF16)
    sb("s_f32", [128, 512], F32)
    sb("f32a", [128, 512], F32)
    sb("f32b", [128, 512], F32)
    sb("f32c", [128, 512], F32)
    sb("scr", [128, 512], F32)
    sb("sqd", [128, 512], BF16)
    if not fused:
        h = E["h_sb"]
        for s in range(NSQ):
            E[f"qw{s}"] = h[:, (s * 3 + 0) * 512:(s * 3 + 1) * 512]
            E[f"qh{s}"] = h[:, (s * 3 + 1) * 512:(s * 3 + 2) * 512]
            E[f"qb{s}"] = h[:, (s * 3 + 2) * 512:(s * 3 + 3) * 512]
        for s in range(4):
            E[f"qsc{s}"] = h[0:2, (NSQ * 3 + s) * 512:(NSQ * 3 + s + 1) * 512]


class QuantEmitter:

    def __init__(self, p, E, dr, banks2=False):
        self.p, self.E = p, E
        self.banks2 = banks2
        groups = []
        gu = dr["wgu_p"].ap()
        g2 = dr["sc_gu2"].ap()
        order = []
        for j in range(64):
            order += [j, 64 + j]
        for j2 in order:
            grp = [(gu[j2][:, h * 512:(h + 1) * 512],
                    g2[j2][:, h * 512:(h + 1) * 512]) for h in range(4)]
            groups.append((grp, (f"scg{j2}",)))
        wp = dr["wp_p"].ap()
        scp = dr["sc_p"].ap()
        for do in range(16):
            fw = wp[do].rearrange("p c o -> p (c o)")
            fs = scp[do].rearrange("p c o -> p (c o)")
            for g in range(4):
                c0 = g * 2048
                grp = [(fw[:, c0 + h * 512:c0 + (h + 1) * 512],
                        fs[:, c0 + h * 512:c0 + (h + 1) * 512])
                       for h in range(4)]
                groups.append((grp, (f"scp{do}_{g}",)))
        self.todo = []
        uid = 0
        for grp, gkeys in groups:
            ukeys = []
            for i, (src, dst) in enumerate(grp):
                ukey = f"uq{uid}"
                uid += 1
                ukeys.append(ukey)
                bar = (gkeys, list(ukeys)) if i == len(grp) - 1 else None
                self.todo.append((src, dst, ukey, bar))
        self.w = 0

    def _bufs(self, u):
        s = u % NSQ
        s2 = u % 4
        E = self.E
        return (E[f"qw{s}"], E[f"qh{s}"], E[f"qb{s}"], E[f"qsc{s2}"],
                f"qw{s}", f"qh{s}", f"qb{s}", f"qsc{s2}")

    def _wave(self, w):
        p, E = self.p, self.E
        pb = E["pb"]
        n = len(self.todo)
        if 0 <= w - 5 < n:
            u = w - 5
            src, dst, ukey, bar = self.todo[u]
            qw, qh, qb, qsc, kw, kh, kb, ksc = self._bufs(u)
            p.op("sync", lambda e, dst=dst, qh=qh: e.dma_start(
                out=dst, in_=qh[:, :]), reads=[kh], writes=[ukey],
                dma=True)
            if bar is not None:
                gkeys, ukeys = bar
                p.barrier(reads=ukeys, writes=list(gkeys))
        if 0 <= w - 4 < n:
            u = w - 4
            qw, qh, qb, qsc, kw, kh, kb, ksc = self._bufs(u)
            p.op("vector", lambda e, qb=qb, qh=qh:
                 e.scalar_tensor_tensor(
                     qh[:, :], qh[:, :], INV_M, qb[:, :],
                     ALU.mult, ALU.is_gt),
                 reads=[kh, kb], writes=[kh])
            p.op("gpsimd", lambda e, qw=qw, qb=qb: e.tensor_tensor(
                qw[:, :], qw[:, :], qb[:, :], ALU.mult),
                reads=[kw, kb], writes=[kw])
            p.op("vector", lambda e, qh=qh, qw=qw: e.tensor_tensor(
                qh[:, :], qh[:, :], qw[:, :], ALU.mult),
                reads=[kh, kw], writes=[kh])
        if 0 <= w - 3 < n:
            u = w - 3
            qw, qh, qb, qsc, kw, kh, kb, ksc = self._bufs(u)
            b = self.bank(u)
            if u % 2 == 0:
                p.op("scalar", lambda e, b=b, qb=qb: e.copy(
                    qb[:, :], pb[b][:, :]), reads=[k_pb(b)], writes=[kb])
            else:
                p.op("vector", lambda e, b=b, qb=qb: e.tensor_copy(
                    qb[:, :], pb[b][:, :]), reads=[k_pb(b)], writes=[kb])
        if 0 <= w - 2 < n:
            u = w - 2
            qw, qh, qb, qsc, kw, kh, kb, ksc = self._bufs(u)
            b = self.bank(u)
            bk = k_pb(b)
            p.op("tensor", lambda e, b=b, qh=qh: e.matmul(
                pb[b][0:2, :], E["sel"][:, :], qh[:, :],
                start=True, stop=True), reads=[kh], writes=[bk])
            p.op("vector", lambda e, b=b, qsc=qsc: e.tensor_scalar(
                qsc[0:2, :], pb[b][0:2, :], 1.0 / 64.0, 1e-8,
                ALU.mult, ALU.max), reads=[bk], writes=[ksc])
            p.op("tensor", lambda e, b=b, qsc=qsc: e.matmul(
                pb[b][:, :], E["selT"][:, :], qsc[0:2, :],
                start=True, stop=True), reads=[ksc], writes=[bk])
        if 0 <= w - 1 < n:
            u = w - 1
            qw, qh, qb, qsc, kw, kh, kb, ksc = self._bufs(u)
            p.op("scalar", lambda e, qw=qw, qh=qh: e.activation(
                qh[:, :], qw[:, :], AF.Abs), reads=[kw], writes=[kh])
            p.op("scalar", lambda e, qw=qw: e.activation(
                qw[:, :], qw[:, :], AF.Sign), reads=[kw], writes=[kw])
        if w < len(self.todo):
            u = w
            src = self.todo[u][0]
            qw, qh, qb, qsc, kw, kh, kb, ksc = self._bufs(u)
            p.op("gpsimd", lambda e, src=src, qw=qw: e.dma_start(
                out=qw[:, :], in_=src), writes=[kw], dma=True)

    def bank(self, u):
        return 6 + u % 2 if self.banks2 else u % 8

    def pump(self, waves):
        n = len(self.todo) + 5
        for _ in range(waves):
            if self.w >= n:
                return
            self._wave(self.w)
            self.w += 1

    def finish(self):
        self.pump(len(self.todo) + 5)

    def emit_all(self):
        self.finish()


def build_block_b(nc, p, E, dr, out_mode, in_mode, quant=True):
    pb = E["pb"]
    PB = [(pb[i], k_pb(i)) for i in range(8)]

    if quant:
        qe = QuantEmitter(p, E, dr)
        qe.emit_all()
        p.barrier(reads=[t[2] for t in qe.todo] +
                  [f"qw{s}" for s in range(NSQ)],
                  writes=[f"h{j}" for j in range(64)])

    if in_mode == "hbm":
        xin = dr["x"].ap().rearrange("(c p) t -> p c t", p=128)
    st = StatRows(p, E, "st", 512, [PB[4], PB[5]], step=512)
    for q in range(4):
        if in_mode == "hbm":
            xq = E[f"xq{q % 2}"]
            xqk = f"xq{q % 2}"
            p.op("gpsimd", lambda e, q=q, xq=xq: e.dma_start(
                out=xq[:, :].rearrange("p (c t) -> p c t", t=512),
                in_=xin[:, :, q * 512:(q + 1) * 512]),
                writes=[xqk], dma=True)

            def xs(dc, q=q, xq=xq, xqk=xqk):
                return xq[:, dc * 512:(dc + 1) * 512], xqk
        else:
            def xs(dc, q=q):
                return (E["X"][:, dc * T + q * 512: dc * T + (q + 1) * 512],
                        f"X{dc}")
        for j in range(64):
            gb, gbk = pb[(j % 2) * 2], k_pb((j % 2) * 2)
            ub, ubk = pb[1 + (j % 2) * 2], k_pb(1 + (j % 2) * 2)
            assert f"scg{j}" in p.last_writer
            assert f"scg{64 + j}" in p.last_writer
            p.op("scalar", lambda e, j=j: e.dma_start(
                out=E["whg0"][:, :], in_=dr["sc_gu2"].ap()[j]),
                writes=["whg0"], dma=True, reads=[f"scg{j}"])
            p.op("scalar", lambda e, j=j: e.dma_start(
                out=E["whg1"][:, :], in_=dr["sc_gu2"].ap()[64 + j]),
                writes=["whg1"], dma=True, reads=[f"scg{64 + j}"])
            for dc in range(16):
                xa, xk = xs(dc)
                p.op("tensor", lambda e, dc=dc, gb=gb, xa=xa: e.matmul(
                    gb[:, :], E["whg0"][:, dc * 128:(dc + 1) * 128], xa,
                    start=(dc == 0), stop=(dc == 15)),
                    reads=["whg0", xk], writes=[gbk])
            for dc in range(16):
                xa, xk = xs(dc)
                p.op("tensor", lambda e, dc=dc, ub=ub, xa=xa: e.matmul(
                    ub[:, :], E["whg1"][:, dc * 128:(dc + 1) * 128], xa,
                    start=(dc == 0), stop=(dc == 15)),
                    reads=["whg1", xk], writes=[ubk])
            p.op("scalar", lambda e, gb=gb: e.activation(
                E["s_f32"][:, :], gb[:, :], AF.Silu), reads=[gbk], writes=["s_f32"])
            p.op("vector", lambda e, j=j, ub=ub: e.tensor_tensor(
                E["h_sb"][:, j * 512:(j + 1) * 512], E["s_f32"][:, :], ub[:, :],
                ALU.mult), reads=["s_f32", ubk], writes=[f"h{j}"])
            p.op("vector", lambda e, j=j: e.tensor_tensor(
                E["sqd"][:, 0:512], E["h_sb"][:, j * 512:(j + 1) * 512],
                E["h_sb"][:, j * 512:(j + 1) * 512], ALU.mult),
                reads=[f"h{j}"], writes=["sqd"])
            p.op("tensor", lambda e, j=j: e.matmul(
                pb[4][0:1, :], E["ones_col_b"][:, :], E["sqd"][:, 0:512],
                start=(j == 0), stop=(j == 63)), reads=["sqd"], writes=[k_pb(4)])
        p.op("vector", lambda e: e.tensor_copy(E["row_b"][0:1, :], pb[4][0:1, :]),
             reads=[k_pb(4)], writes=["row_b"])
        emit_rstd_row(p, E["row_b"][0:1, :], E["row_b"][0:1, :], "row_b",
                      1.1920929e-07, scale=1.0 / 8192.0, E=E)
        emit_row_bcast_hilo(p, E, "row_b", E["row_b"][0:1, :], 512,
                            [PB[5]], E["f32c"][:, :], "f32c",
                            (E["rowb_hi"], "rowb_hi", E["rowb_lo"], "rowb_lo"))
        st.first = True
        for do in range(16):
            db, dbk = pb[(do % 2) * 2], k_pb((do % 2) * 2)
            for grp in range(4):
                whp = E[f"whp{grp % 2}"]
                whpk = f"whp{grp % 2}"
                if q == 0:
                    assert f"scp{do}_{grp}" in p.last_writer
                p.op("gpsimd", lambda e, do=do, grp=grp, whp=whp: e.dma_start(
                    out=whp[:, :],
                    in_=dr["sc_p"].ap()[do].rearrange(
                        "p c o -> p (c o)")[:, grp * 2048:(grp + 1) * 2048]),
                    writes=[whpk], dma=True, reads=[f"scp{do}_{grp}"])
                for hc2 in range(16):
                    hc = grp * 16 + hc2
                    p.op("tensor", lambda e, hc=hc, hc2=hc2, db=db, whp=whp: e.matmul(
                        db[:, :], whp[:, hc2 * 128:(hc2 + 1) * 128],
                        E["h_sb"][:, hc * 512:(hc + 1) * 512],
                        start=(hc == 0), stop=(hc == 63)),
                        reads=[whpk, f"h{hc}"], writes=[dbk])
            xa, xk = xs(do)
            p.op("vector", lambda e, db=db: e.tensor_tensor(
                E["scr"][:, :], db[:, :], E["f32c"][:, :], ALU.mult),
                reads=[dbk, "f32c"], writes=["scr"])
            p.op("vector", lambda e, xa=xa: e.tensor_tensor(
                xa, E["scr"][:, :], xa, ALU.add),
                reads=["scr", xk], writes=[xk])
            st.add_chunk(xa, xk, E["sqd"][:, 0:512], "sqd")
            if do == 0:
                st.first = False
        st.finalize(D, 1e-5)
        emit_row_bcast_hilo(p, E, "st_sum", E["st_sum"][0:1, 0:512], 512,
                            [PB[4]], E["f32a"][:, :], "f32a",
                            (E["rowb_hi"], "rowb_hi", E["rowb_lo"], "rowb_lo"))
        emit_row_bcast_hilo(p, E, "st_ssq", E["st_ssq"][0:1, 0:512], 512,
                            [PB[5]], E["f32b"][:, :], "f32b",
                            (E["rowb_hi"], "rowb_hi", E["rowb_lo"], "rowb_lo"))
        for do in range(16):
            xa, xk = xs(do)
            p.op("vector", lambda e, xa=xa: e.tensor_tensor(
                E["scr"][:, :], xa, E["f32a"][:, :],
                ALU.subtract), reads=[xk, "f32a"], writes=["scr"])
            if out_mode == "hbm_fp32":
                p.op("vector", lambda e: e.tensor_tensor(
                    E["scr"][:, :], E["scr"][:, :], E["f32b"][:, :], ALU.mult),
                    reads=["scr", "f32b"], writes=["scr"])
                p.op("sync", lambda e, do=do, q=q: e.dma_start(
                    out=dr["y"].ap()[do * 128:(do + 1) * 128,
                                     q * 512:(q + 1) * 512],
                    in_=E["scr"][:, :]), reads=["scr"], writes=[f"y{do}_{q}"],
                    dma=True)
            else:
                p.op("vector", lambda e, do=do, q=q: e.tensor_tensor(
                    E["X"][:, do * T + q * 512: do * T + (q + 1) * 512],
                    E["scr"][:, :], E["f32b"][:, :], ALU.mult),
                    reads=["scr", "f32b"], writes=[f"X{do}"])


def build_launch_b():
    from contextlib import ExitStack
    nc = bass.Bass("TRN2", target_bir_lowering=False)
    need = {"sel": True, "X": False, "aug": False}
    dr = declare_const_drams(nc, need)
    dr["x"] = nc.dram_tensor("x", [D, T], F32, kind="ExternalInput")
    dr["wgu_p"] = nc.dram_tensor("wgu_p", [128, 128, 2048], F32, kind="ExternalInput")
    dr["wp_p"] = nc.dram_tensor("wp_p", [16, 128, 64, 128], F32, kind="ExternalInput")
    dr["sc_gu2"] = nc.dram_tensor("sc_gu2", [128, 128, 2048], BF16, kind="Internal")
    dr["sc_p"] = nc.dram_tensor("sc_p", [16, 128, 64, 128], BF16, kind="Internal")
    dr["y"] = nc.dram_tensor("y", [D, T], F32, kind="ExternalOutput")
    with ExitStack() as stack:
        E = make_env(nc, stack, need)
        env_block_b(nc, stack, E)
        sems = Prog.make_sems(nc, stack)
        block = stack.enter_context(nc.Block())
        p = Prog(nc)
        load_consts(p, E, dr, need)
        build_block_b(nc, p, E, dr, out_mode="hbm_fp32", in_mode="hbm")
        p.finalize(block, sems)
    return nc


def pack_b(inputs):
    wgu = np.asarray(inputs["L0_ffn_w_gate_up"])
    wpr = np.asarray(inputs["L0_ffn_w_proj"])
    wgu_p = np.ascontiguousarray(
        wgu.reshape(2, 64, 128, 16, 128).transpose(0, 1, 4, 3, 2)
    ).reshape(128, 128, 2048)
    wp_p = np.ascontiguousarray(
        wpr.reshape(16, 128, 64, 128).transpose(0, 3, 2, 1))
    return dict(wgu_p=wgu_p, wp_p=wp_p)


SC128 = 0.08838834764831845


def env_block_c(nc, stack, E):
    ent = stack.enter_context

    def sb(name, shape, dt):
        E[name] = ent(nc.sbuf_tensor(_uname(name), shape, dt))

    sb("ctx", [128, NH * T], BF16)
    sb("qk0", [128, 4096], BF16)
    sb("qk1", [128, 4096], BF16)
    sb("v40", [128, 4096], BF16)
    sb("q_sb0", [128, T], BF16)
    sb("q_sb1", [128, T], BF16)
    sb("k_sb0", [128, T], BF16)
    E["k_sb1"] = E["k_sb0"]
    sb("e_sb0", [128, 512], BF16)
    sb("e_sb1", [128, 512], BF16)
    sb("bT_s", [128, 48], F32)
    E["bT"] = E["bT_s"]
    sb("bvr", [1, 512], BF16)
    sb("scs", [1, 16], F32)
    sb("f32a", [128, 512], F32)
    sb("f32b", [128, 512], F32)
    sb("f32c", [128, 512], F32)
    E["negb"] = E["colA"]
    E["scr"] = E["f32c"]


def build_block_c(nc, p, E, dr, out_mode, in_mode):
    pb = E["pb"]
    PB = [(pb[i], k_pb(i)) for i in range(8)]
    Xk = [f"X{dc}" for dc in range(NC_)]
    if in_mode == "hbm":
        emit_load_stream(p, E, dr["x"])
    p.op("sync", lambda e: e.dma_start(out=E["bT"][:], in_=dr["bT"].ap()),
         writes=["bT"], dma=True)

    for hg in range(8):
        wv = E[f"qk{hg % 2}"][:, :]
        wvk = f"qk{hg % 2}"
        v4 = E["v40"]
        v4k = "v40"
        bvk = f"bvr{hg % 2}"
        p.op("gpsimd", lambda e, hg=hg, wv=wv: e.dma_start(
            out=wv, in_=dr["wv_p"].ap()[hg]),
            writes=[wvk], dma=True)
        p.op("gpsimd", lambda e, hg=hg: e.dma_start(
            out=E["bvr"][0:1, (hg % 2) * 256:(hg % 2) * 256 + 256],
            in_=dr["bv_p"].ap()[hg:hg + 1, :]),
            writes=[bvk], dma=True)
        for tt in range(NC_):
            vb, vbk = PB[tt % 2]
            for dc in range(NC_):
                p.op("tensor", lambda e, tt=tt, dc=dc, vb=vb, wv=wv: e.matmul(
                    vb[:, 0:256], E["X"][:, dc * T + tt * 128: dc * T + (tt + 1) * 128],
                    wv[:, dc * 256:(dc + 1) * 256], start=(dc == 0), stop=False),
                    reads=[wvk, Xk[dc]], writes=[vbk])
            p.op("tensor", lambda e, vb=vb, hg=hg: e.matmul(
                vb[:, 0:256], E["ones_row_b"][:, :],
                E["bvr"][0:1, (hg % 2) * 256:(hg % 2) * 256 + 256],
                start=False, stop=True), reads=[bvk], writes=[vbk])
            p.op("scalar", lambda e, tt=tt, vb=vb, v4=v4: e.copy(
                v4[:, tt * 256:(tt + 1) * 256], vb[:, 0:256]),
                reads=[vbk], writes=[v4k])

        for hh in range(2):
            h = hg * 2 + hh
            par = h % 2
            qsb, qsbk = E[f"q_sb{par}"], f"q_sb{par}"
            ksb, ksbk = E["k_sb0"], "k_sb0"
            s0, s1 = (par) * 8, (par) * 8 + 8
            qk = E[f"qk{h % 2}"]
            qkk = f"qk{h % 2}"
            p.op("gpsimd", lambda e, h=h, qk=qk: e.dma_start(
                out=qk[:, :], in_=dr["wqk_p"].ap()[h].transpose([1, 0, 2])),
                writes=[qkk], dma=True)
            for kv in range(2):
                dst = qsb if kv == 0 else ksb
                dstk = qsbk if kv == 0 else ksbk
                for tc in range(4):
                    for dc in range(NC_):
                        p.op("tensor", lambda e, kv=kv, tc=tc, dc=dc, qk=qk: e.matmul(
                            pb[tc][:, :],
                            qk[:, kv * 2048 + dc * 128: kv * 2048 + (dc + 1) * 128],
                            E["X"][:, dc * T + tc * 512: dc * T + tc * 512 + 512],
                            start=(dc == 0), stop=(dc == NC_ - 1)),
                            reads=[qkk, Xk[dc]], writes=[k_pb(tc)])
                    p.op("vector", lambda e, kv=kv, tc=tc, h=h, dst=dst: e.tensor_scalar(
                        dst[:, tc * 512:tc * 512 + 512], pb[tc][:, :],
                        E["bT"][:, kv * 16 + h: kv * 16 + h + 1], None, ALU.add),
                        reads=[k_pb(tc), "bT"], writes=[dstk])
            nk = f"negb{par}"
            sck = f"scs{par}"
            emit_sumsq_row(p, E, ksb, ksbk, E["row_a"], "row_a",
                           pb[4], k_pb(4), step=512,
                           sq=(E["e_sb0"], "e_sb0"))
            p.op("vector", lambda e, s0=s0: e.reduce_max(
                E["scs"][0:1, s0:s0 + 1], E["row_a"][0:1, 0:T], axis=AX.X),
                reads=["row_a"], writes=[sck])
            emit_sumsq_row(p, E, qsb, qsbk, E["row_a"], "row_a",
                           pb[4], k_pb(4), step=512,
                           sq=(E["e_sb0"], "e_sb0"))
            p.op("vector", lambda e, s0=s0: e.reduce_max(
                E["scs"][0:1, s0 + 1:s0 + 2], E["row_a"][0:1, 0:T], axis=AX.X),
                reads=["row_a"], writes=[sck])
            p.op("vector", lambda e, s0=s0: e.tensor_tensor(
                E["scs"][0:1, s0 + 2:s0 + 3], E["scs"][0:1, s0:s0 + 1],
                E["scs"][0:1, s0 + 1:s0 + 2],
                ALU.mult), reads=[sck], writes=[sck])
            p.op("scalar", lambda e, s0=s0: e.activation(
                E["scs"][0:1, s0 + 3:s0 + 4], E["scs"][0:1, s0 + 2:s0 + 3],
                AF.Ln, bias=0.0),
                reads=[sck], writes=[sck])
            p.op("scalar", lambda e, s0=s0: e.activation(
                E["scs"][0:1, s0 + 4:s0 + 5], E["scs"][0:1, s0 + 3:s0 + 4],
                AF.Exp, bias=E["epsc"][0:1, 5:6], scale=0.5),
                reads=[sck, "c_eps_sb"], writes=[sck])
            p.op("vector", lambda e, s0=s0: e.tensor_scalar(
                E["scs"][0:1, s0 + 5:s0 + 6], E["scs"][0:1, s0 + 4:s0 + 5],
                -1.0, None, ALU.mult),
                reads=[sck], writes=[sck])
            p.op("tensor", lambda e, s0=s0: e.matmul(
                pb[4][:, 0:1], E["ones_row_f"][:, :],
                E["scs"][0:1, s0 + 5:s0 + 6],
                start=True, stop=True), reads=[sck], writes=[k_pb(4)])
            p.op("scalar", lambda e, par=par: e.copy(
                E["negb"][:, par:par + 1], pb[4][:, 0:1]),
                reads=[k_pb(4)], writes=[nk])

            for qs in range(4):
                nkt = 4 * qs + 4
                ctxb, ctxk = PB[6]
                denb, denk = PB[7]

                def emit_S(kt, qs=qs, qsb=qsb, ksb=ksb, qsbk=qsbk, ksbk=ksbk):
                    s_pb, s_k = PB[4 + (kt % 2)]
                    d = kt - 4 * qs
                    c0 = max(0, d * 128)
                    p.op("tensor", lambda e, kt=kt, qs=qs, c0=c0, s_pb=s_pb: e.matmul(
                        s_pb[:, c0:512], ksb[:, kt * 128:(kt + 1) * 128],
                        qsb[:, qs * 512 + c0: qs * 512 + 512],
                        start=True, stop=True),
                        reads=[ksbk, qsbk], writes=[s_k])
                    if d >= 0:
                        p.op("vector", lambda e, c0=c0, s_pb=s_pb: e.tensor_tensor(
                            s_pb[:, c0:c0 + 128], s_pb[:, c0:c0 + 128],
                            E["mask"][:, :], ALU.add),
                            reads=[s_k, "c_mask_sb"], writes=[s_k])

                kts = list(range(nkt))
                emit_S(kts[0])
                for idx, kt in enumerate(kts):
                    s_pb, s_k = PB[4 + (kt % 2)]
                    d = kt - 4 * qs
                    c0 = max(0, d * 128)
                    eb = E[f"e_sb{kt % 2}"]
                    ek = f"e_sb{kt % 2}"
                    first = idx == 0
                    last = idx == nkt - 1
                    p.op("scalar", lambda e, c0=c0, s_pb=s_pb, eb=eb, par=par:
                         e.activation(
                        eb[:, c0:512], s_pb[:, c0:512], AF.Exp,
                        bias=E["negb"][:, par:par + 1], scale=SC128),
                        reads=[s_k, f"negb{par}"], writes=[ek])
                    if c0 > 0:
                        p.op("vector", lambda e, c0=c0, eb=eb: e.memset(
                            eb[:, 0:c0], 0.0), reads=[ek], writes=[ek])
                    if idx + 1 < nkt:
                        emit_S(kts[idx + 1])
                    p.op("tensor", lambda e, kt=kt, hh=hh, ctxb=ctxb, v4=v4,
                         eb=eb, first=first, last=last: e.matmul(
                        ctxb[:, 0:512],
                        v4[:, kt * 256 + hh * 128: kt * 256 + (hh + 1) * 128],
                        eb[:, 0:512], start=first, stop=last),
                        reads=[v4k, ek], writes=[ctxk])
                    p.op("tensor", lambda e, kt=kt, denb=denb, eb=eb,
                         first=first, last=last: e.matmul(
                        denb[0:1, 0:512], E["ones_col_b"][:, :],
                        eb[:, 0:512], start=first, stop=last),
                        reads=[ek], writes=[denk])
                p.op("vector", lambda e, denb=denb: e.reciprocal(
                    E["st_sum"][0:1, 0:512], denb[0:1, :]),
                    reads=[denk], writes=["st_sum"])
                emit_row_bcast_hilo(p, E, "st_sum", E["st_sum"][0:1, 0:512], 512,
                                    [PB[4]], E["f32c"][:, :], "f32c",
                                    (E["rowb_hi"], "rowb_hi", E["rowb_lo"], "rowb_lo"))
                p.op("vector", lambda e, h=h, qs=qs, ctxb=ctxb: e.tensor_tensor(
                    E["ctx"][:, h * T + qs * 512: h * T + (qs + 1) * 512],
                    ctxb[:, :], E["f32c"][:, :], ALU.mult),
                    reads=[ctxk, "f32c"], writes=[f"ctx{h}"])

    p.barrier(reads=["f32c"], writes=["scr"])
    st = StatRows(p, E, "st", T, [PB[6], PB[7]], step=512)
    for do in range(NC_):
        wo = E[f"qk{do % 2}"][:, 0:2048]
        wok = f"qk{do % 2}"
        p.op("gpsimd", lambda e, do=do, wo=wo: e.dma_start(
            out=wo, in_=dr["wo_p"].ap()[do]), writes=[wok], dma=True)
        for tc in range(4):
            for hc in range(NC_):
                p.op("tensor", lambda e, tc=tc, hc=hc, wo=wo: e.matmul(
                    pb[tc][:, :], wo[:, hc * 128:(hc + 1) * 128],
                    E["ctx"][:, hc * T + tc * 512: hc * T + tc * 512 + 512],
                    start=(hc == 0), stop=(hc == NC_ - 1)),
                    reads=[wok, f"ctx{hc}"], writes=[k_pb(tc)])
            p.op("vector", lambda e, do=do, tc=tc: e.tensor_tensor(
                E["X"][:, do * T + tc * 512: do * T + tc * 512 + 512],
                pb[tc][:, :],
                E["X"][:, do * T + tc * 512: do * T + tc * 512 + 512],
                ALU.add), reads=[k_pb(tc), Xk[do]], writes=[Xk[do]])
        st.add_chunk(E["X"][:, do * T:(do + 1) * T], Xk[do],
                     E["e_sb0"][:, 0:512], "e_sb0")
        st.chunk_done()
    st.finalize(D, 1e-5)
    for qtr in range(4):
        emit_row_bcast_hilo(p, E, "st_sum", E["st_sum"][0:1, qtr * 512:(qtr + 1) * 512],
                            512, [PB[6]], E["f32a"][:, :], "f32a",
                            (E["rowb_hi"], "rowb_hi", E["rowb_lo"], "rowb_lo"))
        emit_row_bcast_hilo(p, E, "st_ssq", E["st_ssq"][0:1, qtr * 512:(qtr + 1) * 512],
                            512, [PB[7]], E["f32b"][:, :], "f32b",
                            (E["rowb_hi"], "rowb_hi", E["rowb_lo"], "rowb_lo"))
        for dc in range(NC_):
            p.op("vector", lambda e, dc=dc, qtr=qtr: e.tensor_tensor(
                E["scr"][:, :], E["X"][:, dc * T + qtr * 512: dc * T + (qtr + 1) * 512],
                E["f32a"][:, :], ALU.subtract),
                reads=[Xk[dc], "f32a"], writes=["scr"])
            if out_mode == "hbm_fp32":
                p.op("vector", lambda e: e.tensor_tensor(
                    E["scr"][:, :], E["scr"][:, :], E["f32b"][:, :], ALU.mult),
                    reads=["scr", "f32b"], writes=["scr"])
                p.op("sync", lambda e, dc=dc, qtr=qtr: e.dma_start(
                    out=dr["y"].ap()[dc * 128:(dc + 1) * 128,
                                     qtr * 512:(qtr + 1) * 512],
                    in_=E["scr"][:, :]), reads=["scr"], writes=["y"], dma=True)
            else:
                p.op("vector", lambda e, dc=dc, qtr=qtr: e.tensor_tensor(
                    E["X"][:, dc * T + qtr * 512: dc * T + (qtr + 1) * 512],
                    E["scr"][:, :], E["f32b"][:, :], ALU.mult),
                    reads=["scr", "f32b"], writes=[Xk[dc]])


def build_launch_c():
    from contextlib import ExitStack
    nc = bass.Bass("TRN2", target_bir_lowering=False)
    need = {"mask": True, "aug": False, "rowb_w": 1024, "row_b": False}
    dr = declare_const_drams(nc, need)
    dr["x"] = nc.dram_tensor("x", [D, T], F32, kind="ExternalInput")
    dr["wqk_p"] = nc.dram_tensor("wqk_p", [NH, 2, 128, 2048], F32, kind="ExternalInput")
    dr["wv_p"] = nc.dram_tensor("wv_p", [8, 128, 4096], F32, kind="ExternalInput")
    dr["wo_p"] = nc.dram_tensor("wo_p", [NC_, 128, 2048], F32, kind="ExternalInput")
    dr["bT"] = nc.dram_tensor("bT", [128, 48], F32, kind="ExternalInput")
    dr["bv_p"] = nc.dram_tensor("bv_p", [8, 256], F32, kind="ExternalInput")
    dr["y"] = nc.dram_tensor("y", [D, T], F32, kind="ExternalOutput")
    with ExitStack() as stack:
        E = make_env(nc, stack, need)
        env_block_c(nc, stack, E)
        sems = Prog.make_sems(nc, stack)
        block = stack.enter_context(nc.Block())
        p = Prog(nc)
        load_consts(p, E, dr, need)
        build_block_c(nc, p, E, dr, out_mode="hbm_fp32", in_mode="hbm")
        p.finalize(block, sems)
    return nc


def pack_c(inputs):
    wqkv = np.asarray(inputs["L1_attn_w_qkv"])
    b = np.asarray(inputs["L1_attn_b_qkv"])
    wo = np.asarray(inputs["L1_attn_w_o"])
    wq = wqkv[:, 0:2048].reshape(16, 128, 16, 128)
    wk = wqkv[:, 2048:4096].reshape(16, 128, 16, 128)
    wqk_p = np.ascontiguousarray(np.stack(
        [wq.transpose(2, 1, 0, 3).reshape(16, 128, 2048),
         wk.transpose(2, 1, 0, 3).reshape(16, 128, 2048)], axis=1))
    wv_p = np.ascontiguousarray(
        wqkv[:, 4096:6144].reshape(16, 128, 8, 256).transpose(2, 1, 0, 3)
    ).reshape(8, 128, 4096)
    wo_p = np.ascontiguousarray(
        wo.reshape(16, 128, 16, 128).transpose(2, 1, 0, 3)).reshape(NC_, 128, 2048)
    bT = np.ascontiguousarray(
        b.reshape(3, 16, 128).transpose(2, 0, 1).reshape(128, 48)).astype(np.float32)
    bv_p = np.ascontiguousarray(b[4096:].reshape(8, 256)).astype(np.float32)
    return dict(wqk_p=wqk_p, wv_p=wv_p, wo_p=wo_p, bT=bT, bv_p=bv_p)


NI = 44


def env_block_d(nc, stack, E):
    ent = stack.enter_context

    def sb(name, shape, dt):
        E[name] = ent(nc.sbuf_tensor(_uname(name), shape, dt))

    sb("hd", [128, NI * 512], BF16)
    sb("wgu0", [128, 4096], BF16)
    sb("wgu1", [128, 4096], BF16)
    sb("wd0", [128, NI * 128], BF16)
    sb("wd1", [128, NI * 128], BF16)
    sb("s_f32", [128, 512], F32)
    sb("gsc_s", [128, NI], F32)
    sb("usc_s", [128, NI], F32)
    sb("dsc_s", [128, 16], F32)
    E["gsc"] = E["gsc_s"]
    E["usc"] = E["usc_s"]
    E["dsc"] = E["dsc_s"]
    sb("f32a", [128, 512], F32)
    sb("f32b", [128, 512], F32)
    sb("scr", [128, 512], F32)
    sb("sq2", [128, 512], BF16)


def build_block_d(nc, p, E, dr, out_mode, in_mode):
    pb = E["pb"]
    PB = [(pb[i], k_pb(i)) for i in range(8)]
    for nm in ("gsc", "usc", "dsc"):
        p.op("sync", lambda e, nm=nm: e.dma_start(
            out=E[nm][:, :], in_=dr[nm].ap()), writes=[nm], dma=True)

    if in_mode == "hbm":
        emit_load_stream(p, E, dr["x"])
    Xk = [f"X{dc}" for dc in range(NC_)]
    for q in range(4):
        b0 = q * 512

        def xs(dc, b0=b0):
            return (E["X"][:, dc * T + b0: dc * T + b0 + 512], Xk[dc])

        for ic in range(NI):
            wgu = E[f"wgu{ic % 2}"]
            wguk = f"wgu{ic % 2}"
            p.op("gpsimd", lambda e, ic=ic, wgu=wgu: e.dma_start(
                out=wgu[:, :], in_=dr["wgu_p"].ap()[ic]), writes=[wguk], dma=True)
            gb0 = (ic % 2) * 4
            g_b, g_k = pb[gb0], k_pb(gb0)
            u_b, u_k = pb[gb0 + 1], k_pb(gb0 + 1)
            for dc in range(NC_):
                xa, xk = xs(dc)
                p.op("tensor", lambda e, dc=dc, g_b=g_b, wgu=wgu, xa=xa: e.matmul(
                    g_b[:, :], wgu[:, dc * 128:(dc + 1) * 128], xa,
                    start=(dc == 0), stop=(dc == NC_ - 1)),
                    reads=[wguk, xk], writes=[g_k])
            for dc in range(NC_):
                xa, xk = xs(dc)
                p.op("tensor", lambda e, dc=dc, u_b=u_b, wgu=wgu, xa=xa: e.matmul(
                    u_b[:, :], wgu[:, 2048 + dc * 128: 2048 + (dc + 1) * 128], xa,
                    start=(dc == 0), stop=(dc == NC_ - 1)),
                    reads=[wguk, xk], writes=[u_k])
            p.op("scalar", lambda e, ic=ic, g_b=g_b: e.activation(
                E["s_f32"][:, :], g_b[:, :], AF.Silu,
                scale=E["gsc"][:, ic:ic + 1]),
                reads=[g_k, "gsc"], writes=["s_f32"])
            p.op("vector", lambda e, ic=ic, u_b=u_b: e.scalar_tensor_tensor(
                E["hd"][:, ic * 512:(ic + 1) * 512], u_b[:, :],
                E["usc"][:, ic:ic + 1], E["s_f32"][:, :],
                ALU.mult, ALU.mult),
                reads=[u_k, "usc", "s_f32"], writes=[f"hd{ic}"])
        st = StatRows(p, E, "st", 512, [PB[6], PB[7]], step=512)
        st.first = True
        for do in range(NC_):
            wd = E[f"wd{do % 2}"]
            wdk = f"wd{do % 2}"
            p.op("gpsimd", lambda e, do=do, wd=wd: e.dma_start(
                out=wd[:, :], in_=dr["wd_p"].ap()[do]), writes=[wdk], dma=True)
            db, dk = PB[(do % 2) * 3]
            for ic in range(NI):
                p.op("tensor", lambda e, ic=ic, db=db, wd=wd: e.matmul(
                    db[:, :], wd[:, ic * 128:(ic + 1) * 128],
                    E["hd"][:, ic * 512:(ic + 1) * 512],
                    start=(ic == 0), stop=(ic == NI - 1)),
                    reads=[wdk, f"hd{ic}"], writes=[dk])
            xa, xk = xs(do)
            p.op("vector", lambda e, do=do, db=db, xa=xa: e.scalar_tensor_tensor(
                xa, db[:, :], E["dsc"][:, do:do + 1], xa,
                ALU.mult, ALU.add),
                reads=[dk, "dsc", xk], writes=[xk])
            st.add_chunk(xa, xk, E["sq2"][:, 0:512], "sq2")
            if do == 0:
                st.first = False
        st.finalize(D, 1e-5)
        emit_row_bcast_hilo(p, E, "st_sum", E["st_sum"][0:1, 0:512], 512,
                            [PB[6]], E["f32a"][:, :], "f32a",
                            (E["rowb_hi"], "rowb_hi", E["rowb_lo"], "rowb_lo"))
        emit_row_bcast_hilo(p, E, "st_ssq", E["st_ssq"][0:1, 0:512], 512,
                            [PB[7]], E["f32b"][:, :], "f32b",
                            (E["rowb_hi"], "rowb_hi", E["rowb_lo"], "rowb_lo"))
        for dc in range(NC_):
            xa, xk = xs(dc)
            ob, obk = ((E["scr"], "scr") if dc % 2 == 0 else
                       (E["s_f32"], "s_f32"))
            p.op("vector", lambda e, xa=xa, ob=ob: e.tensor_tensor(
                ob[:, :], xa, E["f32a"][:, :], ALU.subtract),
                reads=[xk, "f32a"], writes=[obk])
            p.op("vector", lambda e, ob=ob: e.tensor_tensor(
                ob[:, :], ob[:, :], E["f32b"][:, :], ALU.mult),
                reads=[obk, "f32b"], writes=[obk])
            p.op("sync", lambda e, dc=dc, b0=b0, ob=ob: e.dma_start(
                out=dr["y"].ap()[dc * 128:(dc + 1) * 128, b0:b0 + 512],
                in_=ob[:, :]), reads=[obk], writes=[f"y{dc}_{b0}"],
                dma=True)


def build_launch_d():
    from contextlib import ExitStack
    nc = bass.Bass("TRN2", target_bir_lowering=False)
    need = {"X": True, "aug": False, "rowb_w": 1024}
    dr = declare_const_drams(nc, need)
    dr["x"] = nc.dram_tensor("x", [D, T], F32, kind="ExternalInput")
    dr["wgu_p"] = nc.dram_tensor("wgu_p", [NI, 128, 4096], F32, kind="ExternalInput")
    dr["wd_p"] = nc.dram_tensor("wd_p", [NC_, 128, NI * 128], F32, kind="ExternalInput")
    dr["gsc"] = nc.dram_tensor("gsc", [128, NI], F32, kind="ExternalInput")
    dr["usc"] = nc.dram_tensor("usc", [128, NI], F32, kind="ExternalInput")
    dr["dsc"] = nc.dram_tensor("dsc", [128, 16], F32, kind="ExternalInput")
    dr["y"] = nc.dram_tensor("y", [D, T], F32, kind="ExternalOutput")
    with ExitStack() as stack:
        E = make_env(nc, stack, need)
        env_block_d(nc, stack, E)
        sems = Prog.make_sems(nc, stack)
        block = stack.enter_context(nc.Block())
        p = Prog(nc)
        load_consts(p, E, dr, need)
        build_block_d(nc, p, E, dr, out_mode="hbm_fp32", in_mode="hbm")
        p.finalize(block, sems)
    return nc


def pack_d(inputs):
    gw = np.asarray(inputs["L1_ffn_gate_w"])
    uw = np.asarray(inputs["L1_ffn_up_w"])
    dw = np.asarray(inputs["L1_ffn_down_w"])
    def t_iT(w):
        return w.T.reshape(16, 128, 44, 128).transpose(2, 1, 0, 3)
    wg = t_iT(gw).reshape(NI, 128, 2048)
    wu = t_iT(uw).reshape(NI, 128, 2048)
    wgu_p = np.ascontiguousarray(
        np.stack([wg, wu], axis=1).transpose(0, 2, 1, 3)).reshape(NI, 128, 4096)
    wd_p = np.ascontiguousarray(
        dw.T.reshape(44, 128, 16, 128).transpose(2, 1, 0, 3)).reshape(NC_, 128, NI * 128)
    gsc = np.ascontiguousarray(
        np.asarray(inputs["L1_ffn_gate_s"]).reshape(44, 128).T).astype(np.float32)
    usc = np.ascontiguousarray(
        np.asarray(inputs["L1_ffn_up_s"]).reshape(44, 128).T).astype(np.float32)
    dsc = np.ascontiguousarray(
        np.asarray(inputs["L1_ffn_down_s"]).reshape(16, 128).T).astype(np.float32)
    return dict(wgu_p=wgu_p, wd_p=wd_p, gsc=gsc, usc=usc, dsc=dsc)



_NC_CACHE = {}


def _get_nc(name):
    if name not in _NC_CACHE:
        _NC_CACHE[name] = {"a": build_launch_a, "b": build_launch_b,
                           "c": build_launch_c, "d": build_launch_d}[name]()
    return _NC_CACHE[name]


def kernel_unfused(**inputs):
    x = np.asarray(inputs["L0_attn_x"])
    nb = x.shape[0]
    cores = list(range(nb))

    pa, pb_, pc, pd = (pack_a(inputs), pack_b(inputs), pack_c(inputs),
                       pack_d(inputs))
    ca = const_arrays({})
    cb = const_arrays({"sel": True})
    cc = const_arrays({"mask": True})
    cd = const_arrays({})

    cur = [np.ascontiguousarray(x[b].T) for b in range(nb)]

    res = run_bass_kernel_spmd(
        _get_nc("a"), [dict(ca, **pa, x=cur[b]) for b in range(nb)],
        core_ids=cores)
    cur = [res.results[b]["y"] for b in range(nb)]

    res = run_bass_kernel_spmd(
        _get_nc("b"), [dict(cb, **pb_, x=cur[b]) for b in range(nb)],
        core_ids=cores)
    cur = [res.results[b]["y"] for b in range(nb)]

    res = run_bass_kernel_spmd(
        _get_nc("c"), [dict(cc, **pc, x=cur[b]) for b in range(nb)],
        core_ids=cores)
    cur = [res.results[b]["y"] for b in range(nb)]

    res = run_bass_kernel_spmd(
        _get_nc("d"), [dict(cd, **pd, x=cur[b]) for b in range(nb)],
        core_ids=cores)
    out = np.stack([np.ascontiguousarray(res.results[b]["y"].T)
                    for b in range(nb)])
    return out.astype(np.float32)


def build_fused():
    from contextlib import ExitStack
    nc = bass.Bass("TRN2", target_bir_lowering=False)
    need = {"sel": True, "mask": True, "X": True, "aug": True,
            "rowb_w": 1024, "row_b": True}
    dr = declare_const_drams(nc, need)
    dr["x"] = nc.dram_tensor("x", [D, T], F32, kind="ExternalInput")
    dr["y"] = nc.dram_tensor("y", [D, T], F32, kind="ExternalOutput")
    dr["wq_p"] = nc.dram_tensor("wq_p", [NH, 128, 2048], F32, kind="ExternalInput")
    dr["wkv_p"] = nc.dram_tensor("wkv_p", [128, 2 * 16 * 128], F32, kind="ExternalInput")
    dr["wout_p"] = nc.dram_tensor("wout_p", [NC_, 128, 2048], F32, kind="ExternalInput")
    dr["gnT"] = nc.dram_tensor("gnT", [128, 16], F32, kind="ExternalInput")
    dr["goT"] = nc.dram_tensor("goT", [128, 16], F32, kind="ExternalInput")
    dr["hb_out"] = nc.dram_tensor("hb_out", [NC_, 128, T], BF16, kind="Internal")
    dr["wgu_p"] = nc.dram_tensor("wgu_p", [128, 128, 2048], F32, kind="ExternalInput")
    dr["wp_p"] = nc.dram_tensor("wp_p", [16, 128, 64, 128], F32, kind="ExternalInput")
    dr["sc_gu2"] = nc.dram_tensor("sc_gu2", [128, 128, 2048], BF16, kind="Internal")
    dr["sc_p"] = nc.dram_tensor("sc_p", [16, 128, 64, 128], BF16, kind="Internal")
    dr["wqk_p"] = nc.dram_tensor("wqk_p", [NH, 2, 128, 2048], F32, kind="ExternalInput")
    dr["wv_p"] = nc.dram_tensor("wv_p", [8, 128, 4096], F32, kind="ExternalInput")
    dr["wo_p"] = nc.dram_tensor("wo_p", [NC_, 128, 2048], F32, kind="ExternalInput")
    dr["bT"] = nc.dram_tensor("bT", [128, 48], F32, kind="ExternalInput")
    dr["bv_p"] = nc.dram_tensor("bv_p", [8, 256], F32, kind="ExternalInput")
    dr["wgu_d"] = nc.dram_tensor("wgu_d", [NI, 128, 4096], F32, kind="ExternalInput")
    dr["wd_p"] = nc.dram_tensor("wd_p", [NC_, 128, NI * 128], F32, kind="ExternalInput")
    dr["gsc"] = nc.dram_tensor("gsc", [128, NI], F32, kind="ExternalInput")
    dr["usc"] = nc.dram_tensor("usc", [128, NI], F32, kind="ExternalInput")
    dr["dsc"] = nc.dram_tensor("dsc", [128, 16], F32, kind="ExternalInput")

    with ExitStack() as stack:
        E = make_env(nc, stack, need)
        sems = Prog.make_sems(nc, stack)
        block = stack.enter_context(nc.Block())
        p = Prog(nc)
        load_consts(p, E, dr, need)
        emit_load_stream(p, E, dr["x"])
        with ExitStack() as sa:
            env_block_a(nc, sa, E, quant=True)
            Ea = dict(E)
            qe = QuantEmitter(p, Ea, dr, banks2=True)
            p.op("sync", lambda e: e.dma_start(out=Ea["gnT"][:], in_=dr["gnT"].ap()),
                 writes=["gnT_sb"], dma=True)
            p.op("sync", lambda e: e.dma_start(out=Ea["goT"][:], in_=dr["goT"].ap()),
                 writes=["goT_sb"], dma=True)
            build_block_a(nc, p, Ea, dr, out_mode="sbuf", qe=qe)
        p.full_barrier()
        with ExitStack() as s2:
            env_block_b(nc, s2, E, fused=True)
            build_block_b(nc, p, dict(E), dr, out_mode="sbuf", in_mode="sbuf",
                          quant=False)
        p.full_barrier()
        with ExitStack() as s3:
            env_block_c(nc, s3, E)
            build_block_c(nc, p, dict(E), dr, out_mode="sbuf", in_mode="sbuf")
        p.full_barrier()
        with ExitStack() as s4:
            env_block_d(nc, s4, E)
            dr2 = dict(dr)
            dr2["wgu_p"] = dr["wgu_d"]
            build_block_d(nc, p, dict(E), dr2, out_mode="hbm_fp32", in_mode="sbuf")
        p.finalize(block, sems)
    return nc


def kernel(**inputs):
    import os
    if os.environ.get("MK_UNFUSED"):
        return kernel_unfused(**inputs)
    x = np.asarray(inputs["L0_attn_x"])
    nb = x.shape[0]
    cores = list(range(nb))
    packs = {}
    packs.update(pack_a(inputs))
    packs.update(pack_b(inputs))
    packs.update(pack_c(inputs))
    pd = pack_d(inputs)
    packs["wgu_d"] = pd.pop("wgu_p")
    packs.update(pd)
    consts = const_arrays({"sel": True, "mask": True})
    if "fused" not in _NC_CACHE:
        _NC_CACHE["fused"] = build_fused()
    nc = _NC_CACHE["fused"]
    in_maps = [dict(consts, **packs, x=np.ascontiguousarray(x[b].T))
               for b in range(nb)]
    res = run_bass_kernel_spmd(nc, in_maps, core_ids=cores)
    out = np.stack([np.ascontiguousarray(res.results[b]["y"].T)
                    for b in range(nb)])
    return out.astype(np.float32)
